# version 33
# baseline (speedup 1.0000x reference)
import contextlib
import numpy as np
import ml_dtypes
import concourse.bass as bass
import concourse.mybir as mybir
from concourse.bass_core_rust import IndirectOffsetOnAxis
from concourse.bass_utils import run_bass_kernel_spmd

F32 = mybir.dt.float32
BF16 = mybir.dt.bfloat16
U32 = mybir.dt.uint32
AF = mybir.ActivationFunctionType
ALU = mybir.AluOpType
AX = mybir.AxisListType.X

NCORE = 8
B, L, D, N = 2, 8192, 1024, 16
T = B * L
TPC = T // NCORE
E, CAP, FF = 16, 2560, 1408
DSL = D // NCORE
TCH = 512
NIT = T // TCH
NTIL = TPC // 128
FCH = FF // 128
KD = D // 128
TBL = E * CAP + 128
TRASH = E * CAP
RPE = CAP // 128
SCH = 512
NSC = CAP // SCH
EPS = 1e-5
BIGF = 1.0e9
DEBUG_TAPS = ()



def _consts(core):
    c = {}
    c["ident_bf"] = np.eye(128, dtype=ml_dtypes.bfloat16)
    c["ident_f"] = np.eye(128, dtype=np.float32)
    c["iota16_bc"] = np.broadcast_to(np.arange(16, dtype=np.float32), (128, 16)).copy()
    c["parb"] = np.broadcast_to((np.arange(16) % 2).astype(np.float32), (128, 16)).copy()
    m = np.zeros((16, 128), np.float32)
    for p_ in range(128):
        m[p_ % 16, p_] = 1.0
    c["bcmask16"] = m.astype(ml_dtypes.bfloat16)
    rm = np.zeros((128, 16 * 128), np.float32)
    for gg in range(16):
        for p_ in range(128):
            rm[gg * 8 + p_ // 16, gg * 128 + p_] = 1.0
    c["repmask"] = rm.astype(ml_dtypes.bfloat16)
    ns = np.zeros((128, 16 * 128), np.float32)
    for gg in range(16):
        for p_ in range(128):
            ns[p_, gg * 128 + 8 * gg + p_ // 16] = 1.0
    c["nsmask"] = ns.astype(ml_dtypes.bfloat16)
    tok1 = np.empty((128, NTIL), np.float32)
    for i in range(NTIL):
        tok1[:, i] = core * TPC + i * 128 + np.arange(128) + 1
    c["tok1"] = tok1
    c["trash_col"] = (TRASH + np.arange(128, dtype=np.float32))[:, None].copy()
    c["mask_lt8"] = (np.arange(8) < core).astype(np.float32)[:, None].copy()
    yg = np.empty((128, 8), np.float32)
    for j in range(8):
        yg[:, j] = j * 1024 + np.arange(128) * 8 + core
    c["ygidx"] = yg
    er = np.zeros((128, 1), np.float32)
    er[0:2 * RPE, 0] = core * 2 * RPE + np.arange(2 * RPE)
    c["erow"] = er
    c["ones_row"] = np.ones((1, 128), np.float32)
    c["ones_col16"] = np.ones((16, 1), np.float32)
    return c


def kernel(**inputs):
    ii = {k: np.ascontiguousarray(np.asarray(v, dtype=np.float32))
          for k, v in inputs.items()}
    x2d = ii["L0_attn_x"].reshape(T, D)
    in_maps = []
    for c in range(NCORE):
        m = {
            "x_own": np.ascontiguousarray(x2d[c * TPC:(c + 1) * TPC]),
            "x_full": x2d,
            "Wz": ii["L0_attn_Wz"], "bz": ii["L0_attn_bz"],
            "Wx_sl": np.ascontiguousarray(ii["L0_attn_Wx"][:, c * DSL:(c + 1) * DSL]),
            "bx_sl": np.ascontiguousarray(ii["L0_attn_bx"][c * DSL:(c + 1) * DSL]),
            "Wdt_sl": np.ascontiguousarray(ii["L0_attn_Wdt"][:, c * DSL:(c + 1) * DSL]),
            "bdt_sl": np.ascontiguousarray(ii["L0_attn_bdt"][c * DSL:(c + 1) * DSL]),
            "WB": ii["L0_attn_WB"], "bB": ii["L0_attn_bB"],
            "WC": ii["L0_attn_WC"], "bC": ii["L0_attn_bC"],
            "A_log_sl": np.ascontiguousarray(ii["L0_attn_A_log"][c * DSL:(c + 1) * DSL]),
            "Wout": ii["L0_attn_Wout"], "bout": ii["L0_attn_bout"],
            "router_w": ii["L0_ffn_router_w"], "router_b": ii["L0_ffn_router_b"],
            "wg_sl": np.ascontiguousarray(ii["L0_ffn_w_gate"][2 * c:2 * c + 2]),
            "wu_sl": np.ascontiguousarray(ii["L0_ffn_w_up"][2 * c:2 * c + 2]),
            "wd_sl": np.ascontiguousarray(ii["L0_ffn_w_down"][2 * c:2 * c + 2]),
            "sw_gate": ii["L0_ffn_sw_gate"], "sw_up": ii["L0_ffn_sw_up"],
            "sw_down": ii["L0_ffn_sw_down"],
        }
        m.update(_consts(c))
        in_maps.append(m)
    nc = build_nc()
    res = run_bass_kernel_spmd(nc, in_maps, core_ids=list(range(NCORE)))
    out = np.concatenate([np.asarray(r["o_out"]) for r in res.results], axis=0)
    return np.ascontiguousarray(out.reshape(B, L, D).astype(np.float32))



class Prog:
    ENGS = ("sync", "tensor", "vector", "scalar", "gpsimd")

    def __init__(self, nc):
        self.nc = nc
        self.q = {e: [] for e in self.ENGS}
        self.semh = {}
        self.cnt = {}
        self.tok = {}
        self.last = {}
        self.auto = False

    def sem(self, name):
        if name not in self.semh:
            self.semh[name] = self.nc.alloc_semaphore(name)
            self.cnt[name] = 0
        return self.semh[name]

    def op(self, eng, builder, waits=(), res=(), inc=None, amt=1, rel=()):
        wl = list(waits)
        for r in res:
            wl.extend(self.tok.get(r, ()))
        if self.auto and eng in ("vector", "scalar") and eng in self.last:
            wl.append(self.last[eng])
        post = None
        incs = []
        if inc is not None:
            self.sem(inc)
            self.cnt[inc] += amt
            post = self.cnt[inc]
            incs = [(inc, amt)]
        for s, _v in wl:
            self.sem(s)
        self.q[eng].append((builder, tuple(wl), tuple(incs)))
        if inc is not None and eng in ("vector", "scalar", "gpsimd", "tensor"):
            self.last[eng] = (inc, post)
        if rel:
            assert inc is not None
            for r in rel:
                self.tok[r] = [(inc, post)]
        return post

    def chained(self):
        import contextlib as _ctx

        @_ctx.contextmanager
        def cm():
            prev = self.auto
            self.auto = True
            try:
                yield
            finally:
                self.auto = prev
        return cm()

    def breg(self, e, val):
        if not hasattr(self, "_bregs"):
            self._bregs = {}
        if val not in self._bregs:
            self._bregs[val] = e.to_reg(val)
        return self._bregs[val]

    def also_rel(self, r, sem, val):
        self.tok.setdefault(r, []).append((sem, val))

    def set_tok(self, r, sem, val):
        self.tok[r] = [(sem, val)]

    def dma(self, eng, out, in_, waits=(), res=(), inc=None, rel=(), **kw):
        return self.op(eng, lambda e: e.dma_start(out=out, in_=in_, **kw),
                       waits=waits, res=res, inc=inc, amt=16, rel=rel)

    def flush(self, block):
        for name in self.ENGS:
            ops = self.q[name]
            if not ops:
                continue

            def mk(ops):
                def f(eng):
                    for builder, waits, incs in ops:
                        seen = {}
                        for s, v in waits:
                            seen[s] = max(seen.get(s, 0), v)
                        for s, v in seen.items():
                            eng.wait_ge(self.semh[s], v)
                        ins = builder(eng)
                        for s, a in incs:
                            ins.then_inc(self.semh[s], a)
                return f

            getattr(block, name)(mk(ops))


def build_nc():
    nc = bass.Bass("TRN2", target_bir_lowering=False, debug=False)
    g = {}

    def dram(name, shape, dt, kind="Internal"):
        g[name] = nc.dram_tensor(name, shape, dt, kind=kind).ap()

    for nm, sh, dt in [
        ("x_own", [TPC, D], F32), ("x_full", [T, D], F32),
        ("Wz", [D, D], F32), ("bz", [D], F32),
        ("Wx_sl", [D, DSL], F32), ("bx_sl", [DSL], F32),
        ("Wdt_sl", [D, DSL], F32), ("bdt_sl", [DSL], F32),
        ("WB", [D, N], F32), ("bB", [N], F32),
        ("WC", [D, N], F32), ("bC", [N], F32),
        ("A_log_sl", [DSL, N], F32),
        ("Wout", [D, D], F32), ("bout", [D], F32),
        ("router_w", [D, N], F32), ("router_b", [N], F32),
        ("wg_sl", [2, D, FF], F32), ("wu_sl", [2, D, FF], F32),
        ("wd_sl", [2, FF, D], F32),
        ("sw_gate", [D, FF], F32), ("sw_up", [D, FF], F32), ("sw_down", [FF, D], F32),
        ("ident_bf", [128, 128], BF16), ("ident_f", [128, 128], F32),
        ("iota16_bc", [128, 16], F32),
        ("bcmask16", [16, 128], BF16), ("repmask", [128, 16 * 128], BF16),
        ("nsmask", [128, 16 * 128], BF16), ("tok1", [128, NTIL], F32),
        ("trash_col", [128, 1], F32), ("mask_lt8", [8, 1], F32),
        ("ygidx", [128, 8], F32), ("parb", [128, 16], F32),
        ("ones_row", [1, 128], F32), ("ones_col16", [16, 1], F32),
        ("erow", [128, 1], F32),
    ]:
        dram(nm, sh, dt, kind="ExternalInput")

    dram("o_out", [TPC, D], F32, kind="ExternalOutput")

    dram("ag1a_in", [D // 2, TPC], BF16)
    dram("ag1b_in", [D // 2, TPC], BF16)
    dram("ag1a_out", [NCORE * D // 2, TPC], BF16)
    dram("ag1b_out", [NCORE * D // 2, TPC], BF16)
    dram("ag2_in", [34, TPC], F32)
    dram("ag2_out", [NCORE * 34, TPC], F32)
    dram("ar3_in", [TBL, 1], F32)
    dram("ar3_out", [TBL, 1], F32)
    dram("ag4_in", [DSL * 8, TPC], BF16)
    dram("ag4_out", [NCORE * DSL * 8, TPC], BF16)
    dram("ag5a_in", [CAP, D], BF16)
    dram("ag5b_in", [CAP, D], BF16)
    dram("ag5a_out", [NCORE * CAP, D], BF16)
    dram("ag5b_out", [NCORE * CAP, D], BF16)
    dram("z_dram", [D, TPC], BF16)
    dram("xn_dram", [TPC, D], F32)
    dram("shared_dram", [TPC, D], F32)

    for nm in DEBUG_TAPS:
        src = g[nm]
        dram("tap_" + nm, list(src.shape), src.dtype, kind="ExternalOutput")

    build_program(nc, g)
    return nc


def build_program(nc, g):
    p = Prog(nc)
    stck = contextlib.ExitStack()

    def sb(name, shape, dt):
        g[name] = stck.enter_context(nc.sbuf_tensor(name, shape, dt))

    def ps(name, shape, dt):
        g[name] = stck.enter_context(nc.psum_tensor(name, shape, dt))

    with stck:
        sb("xt", [128, 2 * D], F32)
        sb("UP", [128, 4417], F32)
        sb("xnf", [128, D], F32)
        sb("xnb", [128, D], BF16)
        sb("xnT_tile", [128, KD * 128], BF16)
        sb("wbig", [128, KD * FF], BF16)
        sb("wbig2", [128, KD * FF], BF16)
        sb("wdown", [128, FCH * D], BF16)
        sb("wstage", [128, FF], F32)
        sb("xnTc", [128, 2 * KD * TCH], BF16)
        sb("wsl", [128, KD * 2 * DSL], BF16)
        sb("wsmall", [128, KD * 3 * N], BF16)
        sb("identb", [128, 128], BF16)
        sb("identf", [128, 128], F32)
        sb("bc16", [16, 128], BF16)
        sb("repm", [128, 16 * 128], BF16)
        sb("nsm", [128, 16 * 128], BF16)
        sb("iotab", [128, 16], F32)
        sb("tok1_sb", [128, NTIL], F32)
        sb("trash_sb", [128, 1], F32)
        sb("mlt8", [8, 1], F32)
        sb("onesr", [1, 128], F32)
        sb("ones16c", [16, 1], F32)
        sb("acols", [128, 16], F32)
        sb("bias_sb", [128, 16], F32)
        sb("bb_sb", [16, 2], F32)
        sb("rb_bc", [128, 16], F32)
        sb("bout_bc", [128, D], F32)
        sb("off_bc", [128, 16], F32)
        sb("col", [128, 48], F32)
        sb("packA", [128, 32], F32)
        sb("packB", [128, 32], F32)
        sb("ohtok", [128, 64], F32)
        sb("oh_c", [16, TCH], F32)
        sb("ones16t", [16, TCH], F32)
        sb("cum_c", [16, TCH], F32)
        sb("carry16", [16, 1], F32)
        sb("lcols", [128, 2 * NTIL], F32)
        sb("pcols", [128, 2 * NTIL], F32)
        sb("rcols", [128, 2 * NTIL], F32)
        sb("rcols2", [128, 2 * NTIL], F32)
        sb("ucols", [128, 4 * NTIL], U32)
        sb("off16", [16, 1], F32)
        sb("tot8", [8, 16], F32)
        sb("dtf", [128, TCH], F32)
        sb("dtb", [128, TCH], BF16)
        sb("vf", [128, TCH], F32)
        sb("vb", [128, TCH], BF16)
        sb("bmc_st", [16, TCH], F32)
        sb("bmcb", [16, TCH], BF16)
        sb("bm_bc", [128, TCH], BF16)
        sb("c_bc", [128, TCH], BF16)
        sb("a_sb", [128, 2 * TCH], F32)
        sb("xdb_sb", [128, 2 * TCH], F32)
        sb("h_sb", [128, 2 * TCH], F32)
        sb("yc_sb", [128, TCH], BF16)
        sb("carryg", [128, 16], F32)
        sb("yev", [128, TCH], BF16)
        sb("m3", [128, 3 * FCH * SCH], BF16)
        sb("eout", [128, D], BF16)
        sb("tidu", [128, 128], U32)
        sb("erowu", [128, 1], U32)
        sb("ygu", [128, 8], U32)

        g["xeT"] = g["xnTc"][:, KD * TCH:2 * KD * TCH]
        g["krow"] = g["UP"][0:33, 1024:1280]
        g["tid"] = g["UP"][:, 2048:2176]
        g["ygf"] = g["UP"][:, 2176:2184]
        g["fconst"] = g["UP"][:, 2184:2186]

        ps("P_acc0", [128, 512], F32)
        ps("P_acc1", [128, 512], F32)
        ps("P_acc2", [128, 512], F32)
        ps("P_acc3", [128, 512], F32)
        ps("P_bc0", [128, 512], F32)
        ps("P_bc1", [128, 512], F32)
        ps("P_y", [128, 512], F32)
        ps("P_tr", [128, 128], BF16)

        emit_p1(nc, p, g)
        emit_p2(nc, p, g)
        emit_p3(nc, p, g)
        emit_p4(nc, p, g)

        with nc.Block() as block:
            p.flush(block)



def emit_ln(p, g, src_ap, ve, ae, waits0=(), bf_dests=None):
    op = p.op
    sq = g["UP"][:, 0:D]
    p.auto = True
    op("vector", lambda e: e.reduce_sum(g["col"][:, 0:1], src_ap, axis=AX),
       waits=waits0, inc=ve)
    na = op("scalar", lambda e: e.activation(out=sq, in_=src_ap, func=AF.Square),
            waits=waits0, res=("UPln",), inc=ae)
    nv = op("vector", lambda e: e.reduce_sum(g["col"][:, 1:2], sq, axis=AX),
            waits=[(ae, na)], inc=ve)
    p.set_tok("UPln", ve, nv)
    op("vector", lambda e: e.tensor_scalar_mul(g["col"][:, 2:3], g["col"][:, 0:1], 1.0 / D), inc=ve)
    op("vector", lambda e: e.tensor_scalar_mul(g["col"][:, 3:4], g["col"][:, 1:2], 1.0 / D), inc=ve)
    op("vector", lambda e: e.tensor_mul(g["col"][:, 4:5], g["col"][:, 2:3], g["col"][:, 2:3]), inc=ve)
    nv = op("vector", lambda e: e.tensor_sub(g["col"][:, 5:6], g["col"][:, 3:4], g["col"][:, 4:5]), inc=ve)
    na = op("scalar", lambda e: e.activation(out=g["col"][:, 6:7], in_=g["col"][:, 5:6],
                                             func=AF.Sqrt, bias=g["fconst"][:, 0:1]),
            waits=[(ve, nv)], res=("fc",), inc=ae)
    op("vector", lambda e: e.reciprocal(g["col"][:, 7:8], g["col"][:, 6:7]),
       waits=[(ae, na)], inc=ve)
    nv = op("vector", lambda e: e.tensor_scalar_mul(g["col"][:, 9:10], g["col"][:, 2:3], -1.0), inc=ve)
    nv = op("vector", lambda e: e.tensor_scalar(
        g["xnf"][:], src_ap, g["col"][:, 9:10], g["col"][:, 7:8], ALU.add, ALU.mult),
        res=("xnf",), inc=ve)
    if bf_dests is None:
        na = op("scalar", lambda e: e.activation(out=g["xnb"][:], in_=g["xnf"][:], func=AF.Copy),
                waits=[(ve, nv)], res=("xnb",), inc=ae)
    else:
        for dst, lo, hi, r in bf_dests[:-1]:
            op("scalar", lambda e, dst=dst, lo=lo, hi=hi: e.activation(
                out=dst, in_=g["xnf"][:, lo:hi], func=AF.Copy),
                waits=[(ve, nv)], res=r, inc=ae)
        dst, lo, hi, r = bf_dests[-1]
        na = op("scalar", lambda e, dst=dst, lo=lo, hi=hi: e.activation(
            out=dst, in_=g["xnf"][:, lo:hi], func=AF.Copy),
            waits=[(ve, nv)], res=r, inc=ae)
    p.auto = False
    return nv, na


WSLOTS2 = (("wstage", 0, "sync", "wst0", ()), ("wstage", 704, "gpsimd", "wst1", ()))
WSLOTS5 = WSLOTS2 + (("a_sb", 0, "sync", "wstA", ("a0", "a1")),
                     ("xdb_sb", 0, "gpsimd", "wstX", ("x0", "x1")),
                     ("h_sb", 0, "sync", "wstH", ("h0", "h1")))


def cast_w(p, g, src_chunks, dst_fn, tag, extra_res=(), slots=WSLOTS2):
    nh = 0
    for j, (src_ap, w) in enumerate(src_chunks):
        for h in range(2):
            lo, hi = h * (w // 2), (w if h else w // 2)
            si = nh % len(slots)
            tn, off, q, tok, own = slots[si]
            p.dma(q, g[tn][:, off:off + hi - lo], src_ap[:, lo:hi],
                  res=(tok,) + own, inc="wls%d" % si)
            nl = p.cnt["wls%d" % si]
            p.op("vector", lambda e, j=j, w=w, lo=lo, hi=hi, tn=tn, off=off: e.tensor_copy(
                dst_fn(j, w)[:, lo:hi], g[tn][:, off:off + hi - lo]),
                waits=[("wls%d" % si, nl)],
                res=extra_res if nh == 0 else (),
                inc=tag + "c", rel=(tok,))
            nh += 1
    return p.cnt[tag + "c"]



def emit_p1(nc, p, g):
    op, dma = p.op, p.dma
    ve, ae, te = "p1v", "p1a", "p1t"

    for nm_s, nm_d in [("ident_bf", "identb"), ("ident_f", "identf"),
                       ("bcmask16", "bc16"), ("repmask", "repm"), ("nsmask", "nsm"),
                       ("iota16_bc", "iotab"), ("tok1", "tok1_sb"),
                       ("trash_col", "trash_sb"), ("mask_lt8", "mlt8"),
                       ("ones_row", "onesr"), ("ones_col16", "ones16c"),
                       ("ygidx", "ygf")]:
        dma("sync", g[nm_d][:], g[nm_s][:, :], inc="cld")
    dma("sync", g["bias_sb"][:, 0:8], g["bz"].rearrange("(k pp) -> pp k", pp=128),
        inc="cld", allow_slow_non_contiguous=True)
    dma("sync", g["bias_sb"][:, 8:9], g["bx_sl"][:, None], inc="cld")
    dma("sync", g["bias_sb"][:, 9:10], g["bdt_sl"][:, None], inc="cld")
    dma("sync", g["bb_sb"][:, 0:1], g["bB"][:, None], inc="cld")
    dma("sync", g["bb_sb"][:, 1:2], g["bC"][:, None], inc="cld")
    dma("sync", g["ohtok"][:, 48:64], g["parb"][:, :], inc="cld")
    dma("sync", g["acols"][:],
        g["A_log_sl"].rearrange("(gg ds) n -> (ds n) gg", gg=16),
        inc="cld", allow_slow_non_contiguous=True)
    p.NCLD = p.cnt["cld"]
    op("vector", lambda e: e.memset(g["fconst"][:, 0:1], float(EPS)), inc="fcs")
    op("vector", lambda e: e.memset(g["fconst"][:, 1:2], 1.0), inc="fcs")
    p.set_tok("fc", "fcs", 2)
    op("scalar", lambda e: e.activation(out=g["acols"][:], in_=g["acols"][:], func=AF.Exp),
       waits=[("cld", p.NCLD)], inc="acol")
    op("vector", lambda e: e.tensor_scalar_mul(g["acols"][:], g["acols"][:], -1.0),
       waits=[("acol", 1)], inc="acol")
    p.set_tok("acolv", "acol", 2)

    p.NWZ = cast_w(p, g, [(g["Wz"][k * 128:(k + 1) * 128, :], D) for k in range(KD)],
                   lambda j, w: g["wbig2"][:, j * D:j * D + w], "wz")
    for k in range(KD):
        for jj, nm in enumerate(["WB", "WC", "router_w"]):
            dma("sync", g["wstage"][:, jj * N:(jj + 1) * N],
                g[nm][k * 128:(k + 1) * 128, :],
                res=("wst0",) if jj == 0 else (), inc="wsml")
        nl = p.cnt["wsml"]
        op("vector", lambda e, k=k: e.tensor_copy(
            g["wsmall"][:, k * 3 * N:(k + 1) * 3 * N],
            g["wstage"][:, 0:3 * N]),
            waits=[("wsml", nl)], inc="wsmc", rel=("wst0",))
    p.NWSM = p.cnt["wsmc"]
    for k in range(KD):
        dma("sync", g["wstage"][:, 0:DSL],
            g["Wdt_sl"][k * 128:(k + 1) * 128, :], res=("wst0",), inc="wsll")
        dma("sync", g["wstage"][:, DSL:2 * DSL],
            g["Wx_sl"][k * 128:(k + 1) * 128, :], inc="wsll")
        nl = p.cnt["wsll"]
        op("vector", lambda e, k=k: e.tensor_copy(
            g["wsl"][:, k * 2 * DSL:(k + 1) * 2 * DSL],
            g["wstage"][:, 0:2 * DSL]),
            waits=[("wsll", nl)], inc="wslc", rel=("wst0",))
    p.NWSL = p.cnt["wslc"]

    dma("sync", g["col"][0:1, 16:32], g["router_b"][None, :], inc="rbl")
    nt = op("tensor", lambda e: e.matmul(g["P_y"][:, 0:16], g["onesr"][:, :],
                                         g["col"][0:1, 16:32], start=True, stop=True),
            waits=[("rbl", 16), ("cld", p.NCLD)], res=("PY",), inc=te)
    op("vector", lambda e: e.tensor_copy(g["rb_bc"][:], g["P_y"][:, 0:16]),
       waits=[(te, nt)], inc="rbb", rel=("PY",))

    op("vector", lambda e: e.memset(g["ones16t"][:], 1.0), inc="ones16")
    op("vector", lambda e: e.memset(g["UP"][:, 4096:4417], 0.0), inc="zset")
    dma("sync", g["ar3_in"].rearrange("(pp k) one -> pp (k one)", pp=128),
        g["UP"][:, 4096:4417], inc="ar3z", waits=[("zset", 1)])
    op("vector", lambda e: e.memset(g["bmc_st"][0:1, 0:512], 0.0), inc="zset")
    for lo, hi in ((16, 528), (528, 1040), (1040, 1552), (1552, 2048)):
        dma("sync", g["ag2_in"][32:33, lo:hi], g["bmc_st"][0:1, 0:hi - lo],
            inc="ag2w", waits=[("zset", 2)])
    for lo in (0, 512, 1024, 1536):
        dma("sync", g["ag2_in"][33:34, lo:lo + 512], g["bmc_st"][0:1, 0:512],
            inc="ag2w", waits=[("zset", 2)])
    p.set_tok("bmst0", "ag2w", p.cnt["ag2w"])
    p.set_tok("bmst1", "ag2w", p.cnt["ag2w"])

    def p1_stageA(i):
        sl = i % 2
        tsl = i % 2
        xtv = g["xt"][:, sl * D:(sl + 1) * D]
        nx = 16 * (i // 2 + 1)
        if i + 1 < NTIL:
            sl2 = (i + 1) % 2
            dma("sync", g["xt"][:, sl2 * D:(sl2 + 1) * D],
                g["x_own"][(i + 1) * 128:(i + 2) * 128, :],
                res=("xt%d" % sl2,), inc="xld%d" % sl2)
        if i % 2 == 0:
            bfd = [(g["xnb"][:], 0, D, ("xnb",))]
            tokD = tokV = "xnb"
        else:
            bfd = [(g["dtb"][:], 0, 512, ("dtb",)), (g["vb"][:], 512, D, ("vb",))]
            tokD, tokV = "dtb", "vb"
        nv, na = emit_ln(p, g, xtv, ve, ae, waits0=[("xld%d" % sl, nx)], bf_dests=bfd)
        p.set_tok("xt%d" % sl, ve, nv)
        dma("sync", g["xn_dram"][i * 128:(i + 1) * 128, :], g["xnf"][:],
            inc="xnst", waits=[(ve, nv)])
        p.set_tok("xnf", "xnst", p.cnt["xnst"])

        for k in range(KD):
            w = [(ae, na), ("cld", p.NCLD)]
            if i % 2 == 0:
                src_t = g["xnb"][:, k * 128:(k + 1) * 128]
            else:
                src_t = (g["dtb"][:, k * 128:(k + 1) * 128] if k < 4
                         else g["vb"][:, (k - 4) * 128:(k - 3) * 128])
            if k == 0:
                tokw = list(p.tok.get("xnTb", ()))
            nt = op("tensor", lambda e, src_t=src_t: e.transpose(
                g["P_tr"][:, :], src_t, g["identb"][:, :]),
                waits=w, res=("PTR",) + (("xnTb",) if k == 0 else ()), inc=te)
            op("vector", lambda e, k=k: e.tensor_copy(
                g["xnT_tile"][:, k * 128:(k + 1) * 128],
                g["P_tr"][:, :]), waits=[(te, nt)] + (tokw if k == 0 else []),
                inc="trc", rel=("PTR",))
        ntr = p.cnt["trc"]
        p.set_tok(tokD, "trc", ntr)
        p.set_tok(tokV, "trc", ntr)

        for half, nm in ((0, "ag1a_in"), (1, "ag1b_in")):
            src = g["xnT_tile"].rearrange("pp (k c) -> pp k c", k=KD)[
                :, half * 4:(half + 1) * 4, :]
            dst = g[nm].rearrange("(k pp) t -> pp k t", pp=128)[:, :, i * 128:(i + 1) * 128]
            dma("sync", dst, src, inc="ag1w", waits=[("trc", ntr)])
        nag1 = p.cnt["ag1w"]

        lps = "P_y" if i % 2 == 0 else "P_bc1"
        lpr = "PY" if i % 2 == 0 else "PB1"
        for k in range(KD):
            r = (lpr,) if k == 0 else ()
            op("tensor", lambda e, k=k, tsl=tsl, lps=lps: e.matmul(
                g[lps][:, 0:16],
                g["xnT_tile"][:, k * 128:(k + 1) * 128],
                g["wsmall"][:, k * 3 * N + 2 * N:k * 3 * N + 3 * N],
                start=(k == 0), stop=(k == KD - 1)),
                waits=[("trc", ntr), ("wsmc", p.NWSM)], res=r, inc=te)
        nt_lg = p.cnt[te]

        for jj, woff in ((0, 0), (1, N)):
            for k in range(KD):
                r = ("PA0",) if k == 0 else ()
                op("tensor", lambda e, k=k, woff=woff, tsl=tsl: e.matmul(
                    g["P_acc0"][0:16, 0:128],
                    g["wsmall"][:, k * 3 * N + woff:k * 3 * N + woff + N],
                    g["xnT_tile"][:, k * 128:(k + 1) * 128],
                    start=(k == 0), stop=(k == KD - 1)),
                    waits=[("trc", ntr), ("wsmc", p.NWSM)], res=r, inc=te)
            nt = p.cnt[te]
            na = op("vector", lambda e, jj=jj: e.tensor_single_scalar(
                g["bmc_st"][0:16, jj * 128:(jj + 1) * 128], g["P_acc0"][0:16, 0:128],
                g["bb_sb"][:, jj:jj + 1], ALU.add),
                waits=[(te, nt)], res=("bmst%d" % jj,), inc=ve)
            p.set_tok("PA0", ve, na)
            dma("sync", g["ag2_in"][jj * 16:(jj + 1) * 16, i * 128:(i + 1) * 128],
                g["bmc_st"][0:16, jj * 128:(jj + 1) * 128], inc="ag2w", waits=[(ve, na)])
        p.set_tok("bmst0", "ag2w", p.cnt["ag2w"])
        p.set_tok("bmst1", "ag2w", p.cnt["ag2w"])
        p.tok["xnTb"] = [(te, p.cnt[te]), ("ag1w", nag1)]
        return {"i": i, "tsl": tsl, "ntr": ntr, "nag1": nag1, "nt": nt_lg,
                "lps": lps, "lpr": lpr}

    def p1_stageB(st):
        i, tsl, ntr, nag1, nt = st["i"], st["tsl"], st["ntr"], st["nag1"], st["nt"]
        lps, lpr = st["lps"], st["lpr"]
        p.auto = True
        nv = op("vector", lambda e, lps=lps: e.tensor_add(
            g["col"][:, 10:26], g[lps][:, 0:16], g["rb_bc"][:]),
            waits=[(te, nt), ("rbb", 1)], inc=ve)
        p.set_tok(lpr, ve, nv)
        lg = g["col"][:, 10:26]
        sc1 = g["UP"][:, 2186:2202]
        sc2 = g["UP"][:, 2202:2218]
        op("vector", lambda e: e.reduce_max(g["col"][:, 26:27], lg, axis=AX), res=("UPln",), inc=ve)
        op("vector", lambda e: e.tensor_single_scalar(g["ohtok"][:, 0:16], lg, g["col"][:, 26:27], ALU.is_equal), inc=ve)
        op("vector", lambda e: e.scalar_tensor_tensor(sc1, g["ohtok"][:, 0:16], -BIGF, lg, ALU.mult, ALU.add), inc=ve)
        op("vector", lambda e: e.reduce_max(g["col"][:, 27:28], sc1, axis=AX), inc=ve)
        op("vector", lambda e: e.tensor_mul(sc2, g["ohtok"][:, 0:16], g["iotab"][:]), inc=ve)
        op("vector", lambda e, i=i: e.reduce_max(g["packA"][:, 2 * i:2 * i + 1], sc2, axis=AX), inc=ve)
        op("vector", lambda e: e.tensor_single_scalar(g["ohtok"][:, 16:32], sc1, g["col"][:, 27:28], ALU.is_equal), inc=ve)
        op("vector", lambda e: e.tensor_mul(sc2, g["ohtok"][:, 16:32], g["iotab"][:]), inc=ve)
        op("vector", lambda e, i=i: e.reduce_max(g["packA"][:, 2 * i + 1:2 * i + 2], sc2, axis=AX), inc=ve)
        nv = op("vector", lambda e: e.tensor_scalar_mul(g["col"][:, 28:29], g["col"][:, 26:27], -1.0), inc=ve)
        na = op("scalar", lambda e: e.activation(out=sc1, in_=lg, func=AF.Exp, bias=g["col"][:, 28:29]),
                waits=[(ve, nv)], inc=ae)
        op("vector", lambda e: e.reduce_sum(g["col"][:, 29:30], sc1, axis=AX),
           waits=[(ae, na)], inc=ve)
        na = op("scalar", lambda e: e.activation(out=g["col"][:, 30:31], in_=g["col"][:, 27:28],
                                                 func=AF.Exp, bias=g["col"][:, 28:29]), inc=ae)
        op("vector", lambda e: e.reciprocal(g["col"][:, 31:32], g["col"][:, 29:30]),
           waits=[(ae, na)], inc=ve)
        op("vector", lambda e: e.tensor_mul(g["col"][:, 32:33], g["col"][:, 30:31], g["col"][:, 31:32]), inc=ve)
        op("vector", lambda e: e.tensor_add(g["col"][:, 33:34], g["col"][:, 31:32], g["col"][:, 32:33]), inc=ve)
        op("vector", lambda e: e.tensor_scalar_add(g["col"][:, 33:34], g["col"][:, 33:34], 1e-9), inc=ve)
        op("vector", lambda e: e.reciprocal(g["col"][:, 34:35], g["col"][:, 33:34]), inc=ve)
        op("vector", lambda e, i=i: e.tensor_mul(g["packB"][:, 2 * i:2 * i + 1], g["col"][:, 31:32], g["col"][:, 34:35]), inc=ve)
        op("vector", lambda e, i=i: e.tensor_mul(g["packB"][:, 2 * i + 1:2 * i + 2], g["col"][:, 32:33], g["col"][:, 34:35]), inc=ve)

        op("vector", lambda e, i=i: e.tensor_single_scalar(
            g["ohtok"][:, 0:16], g["iotab"][:], g["packA"][:, 2 * i:2 * i + 1], ALU.is_equal), inc=ve)
        nv = op("vector", lambda e, i=i: e.tensor_single_scalar(
            g["ohtok"][:, 32:48], g["iotab"][:], g["packA"][:, 2 * i + 1:2 * i + 2], ALU.is_equal), inc=ve)
        nt = op("tensor", lambda e: e.transpose(
            g["P_bc0"][0:64, 0:128], g["ohtok"][:, 0:64], g["identf"][:, :]),
            waits=[(ve, nv), ("cld", p.NCLD)], res=("PB0",), inc=te)
        ti = i % 2
        ohv = g["oh_c"].rearrange("ee (tt pp kk) -> ee tt pp kk", tt=2, kk=2)
        op("vector", lambda e, ti=ti: e.tensor_copy(ohv[:, ti, :, 0], g["P_bc0"][0:16, 0:128]),
           waits=[(te, nt)], res=("ohc",) if ti == 0 else (), inc=ve)
        nv = op("vector", lambda e, ti=ti: e.tensor_copy(ohv[:, ti, :, 1], g["P_bc0"][32:48, 0:128]),
                waits=[(te, nt)], inc=ve)
        p.set_tok("PB0", ve, nv)
        p.auto = False

        if i % 2 == 1:
            emit_p1_chunk(nc, p, g, i // 2)

    dma("sync", g["xt"][:, 0:D], g["x_own"][0:128, :], res=("xt0",), inc="xld0")
    st = p1_stageA(0)
    for i in range(NTIL):
        st_next = p1_stageA(i + 1) if i + 1 < NTIL else None
        p1_stageB(st)
        st = st_next

    emit_p1_post(nc, p, g)


def emit_p1_chunk(nc, p, g, ch):
    op = p.op
    ve, ae, te = "p1v", "p1a", "p1t"
    p.auto = True
    nv_oh = p.cnt[ve]
    init = 0.0 if ch == 0 else g["carry16"][:, 0:1]
    nv = op("vector", lambda e, init=init: e.tensor_tensor_scan(
        g["cum_c"][:], g["ones16t"][:], g["oh_c"][:], init, ALU.mult, ALU.add),
        waits=[("ones16", 1), (ve, nv_oh)], inc=ve)
    op("vector", lambda e: e.tensor_copy(g["carry16"][:, 0:1], g["cum_c"][:, TCH - 1:TCH]), inc=ve)
    nv = op("vector", lambda e: e.tensor_mul(g["oh_c"][:], g["cum_c"][:], g["oh_c"][:]), inc=ve)
    nt = op("tensor", lambda e: e.matmul(g["P_bc1"][0:1, 0:512], g["ones16c"][:, :],
                                         g["oh_c"][:], start=True, stop=True),
            waits=[(ve, nv), ("cld", p.NCLD)], res=("PB1",), inc=te)
    p.set_tok("ohc", te, nt)
    lr = g["P_bc1"][0:1, 0:512].rearrange("one (tt pp kk) -> one tt pp kk", tt=2, kk=2)
    kr = g["krow"].rearrange("kk (tt pp) -> kk tt pp", tt=2)
    op("vector", lambda e, lr=lr, kr=kr: e.tensor_copy(kr[0:1, :, :], lr[:, :, :, 0]),
       waits=[(te, nt)], inc=ve)
    nv = op("vector", lambda e, lr=lr, kr=kr: e.tensor_copy(kr[32:33, :, :], lr[:, :, :, 1]), inc=ve)
    p.set_tok("PB1", ve, nv)
    for ti in range(2):
        tile = ch * 2 + ti
        nt = op("tensor", lambda e, ti=ti: e.transpose(
            g["P_bc1"][0:128, 100:133], g["krow"][0:33, ti * 128:(ti + 1) * 128],
            g["identf"][0:33, 0:33]), waits=[(ve, nv)], res=("PB1",), inc=te)
        op("vector", lambda e, tile=tile: e.tensor_copy(
            g["lcols"][:, tile * 2:tile * 2 + 1], g["P_bc1"][0:128, 100:101]),
            waits=[(te, nt)], inc=ve)
        nv2 = op("vector", lambda e, tile=tile: e.tensor_copy(
            g["lcols"][:, tile * 2 + 1:tile * 2 + 2], g["P_bc1"][0:128, 132:133]),
            waits=[(te, nt)], inc=ve)
        p.set_tok("PB1", ve, nv2)
    p.auto = False
    if True:
        pass


def emit_p1_post(nc, p, g):
    op, dma = p.op, p.dma
    ve, ae, te = "p1v", "p1a", "p1t"

    nv = p.cnt[ve]
    dma("sync", g["ag2_in"][32:33, 0:16], g["carry16"][:, 0:1], inc="ag2c",
        waits=[(ve, nv)])

    op("gpsimd", lambda e: e.collective_compute(
        "AllGather", ALU.bypass, replica_groups=[list(range(NCORE))],
        ins=[g["ag1a_in"].tensor.ap().opt()], outs=[g["ag1a_out"].tensor.ap().opt()]),
        waits=[("ag1w", p.cnt["ag1w"])], inc="cc")
    op("gpsimd", lambda e: e.collective_compute(
        "AllGather", ALU.bypass, replica_groups=[list(range(NCORE))],
        ins=[g["ag1b_in"].tensor.ap().opt()], outs=[g["ag1b_out"].tensor.ap().opt()]),
        inc="cc")
    op("gpsimd", lambda e: e.collective_compute(
        "AllGather", ALU.bypass, replica_groups=[list(range(NCORE))],
        ins=[g["ag2_in"].tensor.ap().opt()], outs=[g["ag2_out"].tensor.ap().opt()]),
        waits=[("ag2w", p.cnt["ag2w"]), ("ag2c", 16)], inc="cc")

    p.auto = True
    dma("sync", g["tot8"][:],
        g["ag2_out"].rearrange("(blk r) t -> blk (r t)", r=34)[:, 32 * TPC:32 * TPC + 16],
        inc="t8l", waits=[("cc", 3)])
    nt = op("tensor", lambda e: e.matmul(g["P_bc1"][0:16, 0:1], g["tot8"][:, :],
                                         g["mlt8"][:, :], start=True, stop=True),
            waits=[("t8l", 16)], res=("PB1",), inc=te)
    na = op("scalar", lambda e: e.activation(out=g["off16"][:], in_=g["P_bc1"][0:16, 0:1],
                                             func=AF.Copy), waits=[(te, nt)], inc=ae)
    nt = op("tensor", lambda e: e.transpose(g["P_bc1"][0:1, 4:20], g["off16"][:, :],
                                            g["identf"][0:16, 0:16]),
            waits=[(ae, na)], inc=te)
    na = op("scalar", lambda e: e.activation(out=g["col"][0:1, 0:16], in_=g["P_bc1"][0:1, 4:20],
                                             func=AF.Copy), waits=[(te, nt)], inc=ae)
    nt = op("tensor", lambda e: e.matmul(g["P_bc1"][:, 4:20], g["onesr"][:, :],
                                         g["col"][0:1, 0:16], start=True, stop=True),
            waits=[(ae, na)], inc=te)
    nv = op("vector", lambda e: e.tensor_copy(g["off_bc"][:], g["P_bc1"][:, 4:20]),
            waits=[(te, nt)], inc=ve)
    p.set_tok("PB1", ve, nv)

    for sx in range(2 * NTIL):
        op("vector", lambda e, sx=sx: e.tensor_single_scalar(
            g["ohtok"][:, 0:16], g["iotab"][:], g["packA"][:, sx:sx + 1], ALU.is_equal), inc=ve)
        op("vector", lambda e: e.tensor_mul(g["UP"][:, 128:144], g["ohtok"][:, 0:16], g["off_bc"][:]), inc=ve)
        op("vector", lambda e, sx=sx: e.reduce_sum(g["UP"][:, sx:sx + 1], g["UP"][:, 128:144], axis=AX), inc=ve)
        op("vector", lambda e: e.tensor_mul(g["UP"][:, 144:160], g["ohtok"][:, 0:16],
                                            g["ohtok"][:, 48:64]), inc=ve)
        op("vector", lambda e, sx=sx: e.reduce_sum(g["UP"][:, 32 + sx:32 + sx + 1],
                                                   g["UP"][:, 144:160], axis=AX), inc=ve)
    NS2 = 2 * NTIL
    offc = g["UP"][:, 0:NS2]
    parc = g["UP"][:, 32:32 + NS2]
    t1 = g["UP"][:, 64:64 + NS2]
    t2 = g["UP"][:, 160:160 + NS2]
    t3 = g["UP"][:, 192:192 + NS2]
    vld = g["UP"][:, 224:224 + NS2]
    op("vector", lambda e, offc=offc: e.tensor_add(g["pcols"][:], g["lcols"][:], offc), inc=ve)
    op("vector", lambda e, vld=vld: e.tensor_single_scalar(
        vld, g["pcols"][:], float(CAP), ALU.is_le), inc=ve)
    op("vector", lambda e, vld=vld: e.tensor_mul(g["rcols"][:], g["packB"][:], vld), inc=ve)
    op("vector", lambda e, parc=parc: e.tensor_mul(g["rcols2"][:], g["rcols"][:], parc), inc=ve)
    op("vector", lambda e, t1=t1, parc=parc: e.tensor_scalar(
        t1, parc, -1.0, 1.0, ALU.mult, ALU.add), inc=ve)
    op("vector", lambda e, t1=t1: e.tensor_mul(g["rcols"][:], g["rcols"][:], t1), inc=ve)
    op("vector", lambda e, t1=t1, parc=parc: e.tensor_sub(t1, g["packA"][:], parc), inc=ve)
    op("vector", lambda e, t1=t1: e.tensor_scalar_mul(t1, t1, float(CAP // 2)), inc=ve)
    op("vector", lambda e, t1=t1: e.tensor_add(t1, t1, g["pcols"][:]), inc=ve)
    op("vector", lambda e, t1=t1: e.tensor_scalar_add(t1, t1, -1.0), inc=ve)
    op("vector", lambda e, t1=t1: e.tensor_scalar_min(t1, t1, float(NCORE * CAP - 1)), inc=ve)
    op("vector", lambda e, t1=t1: e.tensor_copy(g["ucols"][:, 0:NS2], t1), inc=ve)
    op("vector", lambda e, t2=t2: e.tensor_scalar_mul(t2, g["packA"][:], float(CAP)), inc=ve)
    op("vector", lambda e, t2=t2: e.tensor_add(t2, t2, g["pcols"][:]), inc=ve)
    op("vector", lambda e, t2=t2: e.tensor_scalar_add(t2, t2, -1.0), inc=ve)
    op("vector", lambda e, t2=t2, vld=vld: e.tensor_mul(t2, t2, vld), inc=ve)
    op("vector", lambda e, t3=t3, vld=vld: e.tensor_scalar(
        t3, vld, -1.0, 1.0, ALU.mult, ALU.add), inc=ve)
    op("vector", lambda e, t3=t3: e.tensor_single_scalar(
        t3, t3, g["trash_sb"][:, 0:1], ALU.mult), inc=ve)
    op("vector", lambda e, t2=t2, t3=t3: e.tensor_add(t2, t2, t3), inc=ve)
    op("vector", lambda e, t2=t2: e.tensor_copy(g["ucols"][:, NS2:2 * NS2], t2), inc=ve)

    p.auto = False
    NV = p.cnt[ve]
    for i in range(NTIL):
        for kk in range(2):
            sx = 2 * i + kk
            op("gpsimd", lambda e, i=i, sx=sx: e.indirect_dma_start(
                out=g["ar3_in"][:, :],
                out_offset=IndirectOffsetOnAxis(ap=g["ucols"][:, 32 + sx:32 + sx + 1], axis=0),
                in_=g["tok1_sb"][:, i:i + 1], in_offset=None,
                bounds_check=p.breg(e, TBL - 1), oob_is_err=False),
                waits=[("p1v", NV), ("ar3z", 16), ("sct", 16 * sx)], inc="sct", amt=16)
    op("gpsimd", lambda e: e.collective_compute(
        "AllReduce", ALU.add, replica_groups=[list(range(NCORE))],
        ins=[g["ar3_in"].tensor.ap().opt()], outs=[g["ar3_out"].tensor.ap().opt()]),
        waits=[("sct", p.cnt["sct"])], inc="cc")



def emit_p2(nc, p, g):
    op, dma = p.op, p.dma
    ve, ae, te, pl = "p2v", "p2a", "p2t", "p2p"

    def p2_loads(it):
        blk, tw = it // 4, it % 4
        sl = it % 2
        for k in range(KD):
            src = g["ag1a_out"] if k < 4 else g["ag1b_out"]
            kk = k % 4
            w = [("cc", 2)] if it < 2 else []
            dma("sync", g["xnTc"][:, (sl * KD + k) * TCH:(sl * KD + k + 1) * TCH],
                src[blk * 512 + kk * 128:blk * 512 + (kk + 1) * 128, tw * TCH:(tw + 1) * TCH],
                waits=w, res=("xTc%d" % sl,) if k == 0 else (), inc="xTl%d" % sl)
        dma("sync", g["bmc_st"][0:16, 0:TCH],
            g["ag2_out"][blk * 34:blk * 34 + 16, tw * TCH:(tw + 1) * TCH],
            waits=[("cc", 3)], res=("bmst0", "bmst1"), inc="bmLl")

    def emit_head_pe(it):
        sl = it % 2
        nxl = 128 * (it // 2 + 1)
        for k in range(KD):
            r = ("PA0",) if k == 0 else ()
            op("tensor", lambda e, k=k, sl=sl: e.matmul(
                g["P_acc0"][:, :], g["wsl"][:, k * 2 * DSL:k * 2 * DSL + DSL],
                g["xnTc"][:, (sl * KD + k) * TCH:(sl * KD + k + 1) * TCH],
                start=(k == 0), stop=(k == KD - 1)),
                waits=[("xTl%d" % sl, nxl), ("wslc", p.NWSL)] if k == 0 else (), res=r, inc=te)
        n_dt = p.cnt[te]
        for k in range(KD):
            r = ("PA1",) if k == 0 else ()
            op("tensor", lambda e, k=k, sl=sl: e.matmul(
                g["P_acc1"][:, :], g["wsl"][:, k * 2 * DSL + DSL:(k + 1) * 2 * DSL],
                g["xnTc"][:, (sl * KD + k) * TCH:(sl * KD + k + 1) * TCH],
                start=(k == 0), stop=(k == KD - 1)),
                waits=[("xTl%d" % sl, nxl)] if k == 0 else (), res=r, inc=te)
        n_xs = p.cnt[te]
        p.set_tok("xTc%d" % sl, te, n_xs)
        return n_dt, n_xs

    def emit_head_act(it, n_dt, n_xs, w0=()):
        if it % 2 == 0:
            dtb_ap, vb_ap, tokD, tokV = g["dtb"][:], g["vb"][:], "dtb", "vb"
        else:
            dtb_ap, vb_ap = g["m3"][:, 0:512], g["m3"][:, 512:1024]
            tokD, tokV = "m3a0", "m3b0"
        u1 = g["UP"][:, 1024:1536]
        u2 = g["UP"][:, 1536:2048]
        nab = op("scalar", lambda e: e.activation(out=u1, in_=g["P_acc0"][:, :], func=AF.Abs,
                                                  bias=g["bias_sb"][:, 9:10]),
                 waits=[(te, n_dt)] + list(w0), res=("UPsp",), inc=ae)
        nab = op("scalar", lambda e: e.activation(out=u2, in_=u1, func=AF.Exp, scale=-1.0),
                 waits=[(ae, nab)], inc=ae)
        nab = op("scalar", lambda e: e.activation(out=u1, in_=u2, func=AF.Ln,
                                                  bias=g["fconst"][:, 1:2]),
                 waits=[(ae, nab)], res=("fc",), inc=ae)
        na = op("scalar", lambda e: e.activation(out=g["dtf"][:], in_=g["P_acc0"][:, :],
                                                 func=AF.Relu, bias=g["bias_sb"][:, 9:10]),
                waits=[(ae, nab)], res=("dtf",), inc=ae)
        p.set_tok("PA0", ae, na)
        nv = op("vector", lambda e: e.tensor_add(g["dtf"][:], g["dtf"][:], u1),
                waits=[(ae, na)], inc=ve)
        p.set_tok("UPsp", ve, nv)
        nv = op("vector", lambda e: e.scalar_tensor_tensor(
            g["vf"][:], g["P_acc1"][:, :], g["bias_sb"][:, 8:9], g["dtf"][:], ALU.add, ALU.mult),
            waits=[(te, n_xs), (ve, nv)], res=("vf",), inc=ve)
        p.set_tok("PA1", ve, nv)
        na_dt = op("scalar", lambda e, dtb_ap=dtb_ap: e.activation(
            out=dtb_ap, in_=g["dtf"][:], func=AF.Copy),
            waits=[(ve, nv - 1)], res=(tokD,), inc=ae)
        p.set_tok("dtf", ae, na_dt)
        na_v = op("scalar", lambda e, vb_ap=vb_ap: e.activation(
            out=vb_ap, in_=g["vf"][:], func=AF.Copy),
            waits=[(ve, nv)], res=(tokV,), inc=ae)
        p.set_tok("vf", ae, na_v)
        return {"na_dt": na_dt, "na_v": na_v, "dtb_ap": dtb_ap, "vb_ap": vb_ap,
                "tokD": tokD, "tokV": tokV}

    p2_loads(0)
    w0 = [("p1v", p.cnt["p1v"]), ("p1a", p.cnt["p1a"]), ("p1t", p.cnt["p1t"]),
          ("p1p", p.cnt.get("p1p", 0))]
    hs = emit_head_act(0, *emit_head_pe(0), w0=w0)
    for it in range(NIT):
        blk, tw = it // 4, it % 4
        first_of_b = (it % 16) == 0
        sl = it % 2
        nbm = 16 * (it + 1)
        na_dt, na_v = hs["na_dt"], hs["na_v"]
        dtb_ap, vb_ap = hs["dtb_ap"], hs["vb_ap"]

        na_b = op("scalar", lambda e: e.activation(out=g["bmcb"][0:16, 0:TCH],
                                                   in_=g["bmc_st"][0:16, 0:TCH], func=AF.Copy),
                  waits=[("bmLl", nbm)], res=("bmcb",), inc=ae)
        nt = op("tensor", lambda e: e.matmul(g["P_bc0"][:, :], g["bc16"][:, :],
                                             g["bmcb"][0:16, 0:TCH], start=True, stop=True),
                waits=[(ae, na_b), ("cld", p.NCLD)], res=("PB0",), inc=te)
        nv1 = op("vector", lambda e: e.tensor_copy(g["bm_bc"][:], g["P_bc0"][:, :]),
                 waits=[(te, nt)], res=("bmbc",), inc=ve)
        p.set_tok("PB0", ve, nv1)
        dma("sync", g["bmc_st"][0:16, 0:TCH],
            g["ag2_out"][blk * 34 + 16:blk * 34 + 32, tw * TCH:(tw + 1) * TCH],
            waits=[("p2a", na_b)], inc="bmCl")
        nbm2 = p.cnt["bmCl"]
        na_c = op("scalar", lambda e: e.activation(out=g["bmcb"][0:16, 0:TCH],
                                                   in_=g["bmc_st"][0:16, 0:TCH], func=AF.Copy),
                  waits=[("bmCl", nbm2), (te, nt)], inc=ae)
        p.set_tok("bmst0", ae, na_c)
        p.set_tok("bmst1", ae, na_c)
        p.set_tok("bmcb", ae, na_c)
        nt = op("tensor", lambda e: e.matmul(g["P_bc0"][:, :], g["bc16"][:, :],
                                             g["bmcb"][0:16, 0:TCH], start=True, stop=True),
                waits=[(ae, na_c)], res=("PB0",), inc=te)
        nv2 = op("vector", lambda e: e.tensor_copy(g["c_bc"][:], g["P_bc0"][:, :]),
                 waits=[(te, nt)], res=("cbc",), inc=ve)
        p.set_tok("PB0", ve, nv2)
        p.also_rel("bmcb", "p2t", nt)
        if it + 1 < NIT:
            p2_loads(it + 1)

        order = [s4 * 4 + q for q in range(4) for s4 in range(4)]
        last_ym = None
        pend = []

        def emit_yred(gi, gg, np_):
            r = ("PY",) if gi == 0 else ()
            ycv = g["yc_sb"][:, 0:TCH] if gi % 2 == 0 else g["m3"][:, 16384:16384 + TCH]
            nty = op("tensor", lambda e, gg=gg, gi=gi, ycv=ycv: e.matmul(
                g["P_y"][:, :], g["nsm"][:, gg * 128:(gg + 1) * 128], ycv,
                start=(gi == 0), stop=(gi == 15)),
                waits=[(pl, np_)], res=r, inc=te)
            p.set_tok("yc%d" % (gi % 2), te, nty)

        for gi, gg in enumerate(order):
            gp = gi % 2
            av = g["a_sb"][:, gp * TCH:(gp + 1) * TCH]
            xv = g["xdb_sb"][:, gp * TCH:(gp + 1) * TCH]
            hv = g["h_sb"][:, gp * TCH:(gp + 1) * TCH]

            bA = "P_bc1" if gp == 0 else "P_acc3"
            bAr = "PB1" if gp == 0 else "PA3"
            bV = "P_bc0" if gp == 0 else "P_acc2"
            bVr = "PB0" if gp == 0 else "PA2"
            nt_a = op("tensor", lambda e, gg=gg, bA=bA, dtb_ap=dtb_ap: e.matmul(
                g[bA][:, :], g["repm"][:, gg * 128:(gg + 1) * 128], dtb_ap,
                start=True, stop=True), waits=[(ae, na_dt)], res=(bAr,), inc=te)
            na2 = op("scalar", lambda e, gg=gg, av=av, bA=bA: e.activation(
                out=av, in_=g[bA][:, :], func=AF.Exp, scale=g["acols"][:, gg:gg + 1]),
                waits=[(te, nt_a), ("acol", 2)], res=("a%d" % gp,), inc=ae)
            p.set_tok(bAr, ae, na2)

            nt_v = op("tensor", lambda e, gg=gg, bV=bV, vb_ap=vb_ap: e.matmul(
                g[bV][:, :], g["repm"][:, gg * 128:(gg + 1) * 128], vb_ap,
                start=True, stop=True), waits=[(ae, na_v)], res=(bVr,), inc=te)
            nv3 = op("vector", lambda e, xv=xv, bV=bV: e.tensor_mul(xv, g[bV][:, :], g["bm_bc"][:]),
                     waits=[(te, nt_v), (ve, nv1)], res=("x%d" % gp,), inc=ve)
            p.set_tok(bVr, ve, nv3)
            if gi >= 2:
                emit_yred(gi - 2, order[gi - 2], pend[gi - 2])

            init = 0.0 if first_of_b else g["carryg"][:, gg:gg + 1]
            nv4 = op("vector", lambda e, av=av, xv=xv, hv=hv, init=init: e.tensor_tensor_scan(
                hv, av, xv, init, ALU.mult, ALU.add),
                waits=[(ae, na2), (ve, nv3)], res=("h%d" % gp,), inc=ve)
            p.set_tok("a%d" % gp, ve, nv4)
            p.set_tok("x%d" % gp, ve, nv4)
            nv5 = op("vector", lambda e, gg=gg, hv=hv: e.tensor_copy(
                g["carryg"][:, gg:gg + 1], hv[:, TCH - 1:TCH]),
                waits=[(ve, nv4)], inc=ve)

            ycv = g["yc_sb"][:, 0:TCH] if gp == 0 else g["m3"][:, 16384:16384 + TCH]
            np_ = op("gpsimd", lambda e, hv=hv, ycv=ycv: e.tensor_mul(ycv, hv, g["c_bc"][:]),
                     waits=[(ve, nv5), (ve, nv2)], res=("yc%d" % gp,), inc=pl)
            p.set_tok("h%d" % gp, pl, np_)
            pend.append(np_)
            last_ym = np_
            if gi == 9 and it + 1 < NIT:
                ndx_next = emit_head_pe(it + 1)
        if it + 1 < NIT:
            hs_next = emit_head_act(it + 1, *ndx_next)
        emit_yred(14, order[14], pend[14])
        emit_yred(15, order[15], pend[15])
        p.set_tok("cbc", pl, last_ym)
        p.set_tok("bmbc", ve, p.cnt[ve])
        p.set_tok(hs["tokD"], te, p.cnt[te])
        p.set_tok(hs["tokV"], te, p.cnt[te])
        if it + 1 < NIT:
            hs = hs_next
        nt_y = p.cnt[te]
        na = op("scalar", lambda e: e.activation(out=g["yev"][:], in_=g["P_y"][:, :], func=AF.Copy),
                waits=[(te, nt_y)], res=("yev",), inc=ae)
        p.set_tok("PY", ae, na)
        dma("sync",
            g["ag4_in"].rearrange("(dp blk) t -> dp blk t", blk=8)[:, blk, tw * TCH:(tw + 1) * TCH],
            g["yev"][:], inc="ag4w", waits=[(ae, na)])
        p.set_tok("yev", "ag4w", p.cnt["ag4w"])

    op("gpsimd", lambda e: e.collective_compute(
        "AllGather", ALU.bypass, replica_groups=[list(range(NCORE))],
        ins=[g["ag4_in"].tensor.ap().opt()], outs=[g["ag4_out"].tensor.ap().opt()]),
        waits=[("ag4w", p.cnt["ag4w"])], inc="cc")





def mlp_block(p, g, rhs_fn, waits1, waits2, waits3, out_cb, inject=None):
    op = p.op
    ae, te = "p3a", "p3t"
    m3 = g["m3"]
    npl_list = []
    h1o, h2o, mto = 0, FCH * SCH, 2 * FCH * SCH
    for f in range(FCH):
        fb = f % 2
        b1 = "P_acc0" if fb == 0 else "P_acc2"
        b1r = "PA0" if fb == 0 else "PA2"
        b2 = "P_acc1" if fb == 0 else "P_acc3"
        b2r = "PA1" if fb == 0 else "PA3"
        for k in range(KD):
            r = (b1r,) if k == 0 else ()
            op("tensor", lambda e, f=f, k=k, b1=b1: e.matmul(
                g[b1][:, :], g["wbig"][:, k * FF + f * 128:k * FF + (f + 1) * 128],
                rhs_fn(k), start=(k == 0), stop=(k == KD - 1)),
                waits=waits1 if k == 0 else (), res=r, inc=te)
        nt1 = p.cnt[te]
        na1 = op("scalar", lambda e, f=f, b1=b1: e.activation(
            out=m3[:, h1o + f * SCH:h1o + (f + 1) * SCH], in_=g[b1][:, :],
            func=AF.Silu), waits=[(te, nt1)], res=("m3a%d" % f,), inc=ae)
        p.set_tok(b1r, ae, na1)
        for k in range(KD):
            r = (b2r,) if k == 0 else ()
            op("tensor", lambda e, f=f, k=k, b2=b2: e.matmul(
                g[b2][:, :], g["wbig2"][:, k * FF + f * 128:k * FF + (f + 1) * 128],
                rhs_fn(k), start=(k == 0), stop=(k == KD - 1)),
                waits=waits2 if k == 0 else (), res=r, inc=te)
        nt2 = p.cnt[te]
        if f == FCH - 1:
            n_mm1 = nt1
            n_mm2 = nt2
        na2 = op("scalar", lambda e, f=f, b2=b2: e.activation(
            out=m3[:, h2o + f * SCH:h2o + (f + 1) * SCH], in_=g[b2][:, :],
            func=AF.Copy), waits=[(te, nt2)], res=("m3b%d" % f,), inc=ae)
        p.set_tok(b2r, ae, na2)
        npl = op("gpsimd", lambda e, f=f: e.tensor_mul(
            m3[:, mto + f * SCH:mto + (f + 1) * SCH],
            m3[:, h1o + f * SCH:h1o + (f + 1) * SCH],
            m3[:, h2o + f * SCH:h2o + (f + 1) * SCH]),
            waits=[(ae, na1), (ae, na2)], res=("m3m%d" % f,), inc="p3p")
        p.set_tok("m3a%d" % f, "p3p", npl)
        p.set_tok("m3b%d" % f, "p3p", npl)
        npl_list.append(npl)
        if inject is not None:
            inject(f)
    n_mm2 = p.cnt[te]
    for ss in range(4):
        for dh in range(2):
            ob = (ss * 2 + dh) % 2
            b3 = "P_bc0" if ob == 0 else "P_bc1"
            b3r = "PB0" if ob == 0 else "PB1"
            for f in range(FCH):
                w = (list(waits3) if f == 0 else []) + [("p3p", npl_list[f])]
                r = (b3r,) if f == 0 else ()
                op("tensor", lambda e, ss=ss, dh=dh, f=f, b3=b3: e.matmul(
                    g[b3][:, :],
                    m3[:, mto + f * SCH + ss * 128:mto + f * SCH + (ss + 1) * 128],
                    g["wdown"][:, f * D + dh * 512:f * D + (dh + 1) * 512],
                    start=(f == 0), stop=(f == FCH - 1)), waits=w, res=r, inc=te)
            out_cb(ss, dh, p.cnt[te], b3, b3r)
            if inject is not None:
                inject(FCH + ss * 2 + dh)
    for f in range(FCH):
        p.set_tok("m3m%d" % f, te, p.cnt[te])
    return n_mm1, n_mm2, p.cnt[te]


def emit_p3(nc, p, g):
    op, dma = p.op, p.dma
    ve, ae, te, pl = "p3v", "p3a", "p3t", "p3p"
    NAG1 = p.cnt["ag1w"]

    for tc in range(4):
        sl = tc % 2
        for k in range(KD):
            src = g["ag1a_in"] if k < 4 else g["ag1b_in"]
            kk = k % 4
            dma("sync", g["xnTc"][:, (sl * KD + k) * TCH:(sl * KD + k + 1) * TCH],
                src[kk * 128:(kk + 1) * 128, tc * TCH:(tc + 1) * TCH],
                waits=[("ag1w", NAG1)], res=("xTc%d" % sl,) if k == 0 else (),
                inc="zxl%d" % sl)
        nxl = p.cnt["zxl%d" % sl]
        for dc in range(KD):
            dp = dc % 2
            zps = "P_acc0" if dp == 0 else "P_acc1"
            zpr = "PA0" if dp == 0 else "PA1"
            zev = g["yev"][:] if dp == 0 else g["dtb"][:]
            zet = "yev" if dp == 0 else "dtb"
            zct = "zst%d" % dp
            for k in range(KD):
                r = (zpr,) if k == 0 else ()
                op("tensor", lambda e, dc=dc, k=k, sl=sl, zps=zps: e.matmul(
                    g[zps][:, :], g["wbig2"][:, k * D + dc * 128:k * D + (dc + 1) * 128],
                    g["xnTc"][:, (sl * KD + k) * TCH:(sl * KD + k + 1) * TCH],
                    start=(k == 0), stop=(k == KD - 1)),
                    waits=[("zxl%d" % sl, nxl), ("wzc", p.NWZ)], res=r, inc=te)
            nt = p.cnt[te]
            na = op("scalar", lambda e, dc=dc, zev=zev, zps=zps: e.activation(
                out=zev, in_=g[zps][:, :], func=AF.Sigmoid,
                bias=g["bias_sb"][:, dc:dc + 1]),
                waits=[(te, nt)], res=(zet,), inc=ae)
            p.set_tok(zpr, ae, na)
            dma("sync", g["z_dram"][dc * 128:(dc + 1) * 128, tc * TCH:(tc + 1) * TCH],
                zev, inc=zct, waits=[(ae, na)])
            p.set_tok(zet, zct, p.cnt[zct])
        p.set_tok("xTc%d" % sl, te, p.cnt[te])
    p.set_tok("wbig2b", te, p.cnt[te])

    dma("sync", g["col"][:, 40:41], g["erow"][:, :], inc="erl")
    p.auto = True
    op("vector", lambda e: e.tensor_copy(g["erowu"][:], g["col"][:, 40:41]),
       waits=[("erl", 16)], inc=ve)
    op("gpsimd", lambda e: e.indirect_dma_start(
        out=g["tid"][:, :], out_offset=None,
        in_=g["ar3_out"].rearrange("(r pp) one -> r (pp one)", pp=128),
        in_offset=IndirectOffsetOnAxis(ap=g["erowu"][:, 0:1], axis=0),
        bounds_check=p.breg(e, TBL // 128 - 1), oob_is_err=False),
        waits=[("cc", 4), ("p3v", p.cnt[ve])], inc="tgl", amt=16)
    nt = op("tensor", lambda e: e.transpose(g["P_bc1"][:, 0:128], g["tid"][:, :],
                                            g["identf"][:, :]),
            waits=[("tgl", 16)], res=("PB1",), inc=te)
    op("vector", lambda e: e.tensor_copy(g["tid"][:, :], g["P_bc1"][:, 0:128]),
       waits=[(te, nt)], inc=ve)
    p.set_tok("PB1", ve, p.cnt[ve])
    op("vector", lambda e: e.tensor_scalar_add(g["tid"][:, :], g["tid"][:, :], -1.0), inc=ve)
    op("vector", lambda e: e.tensor_scalar_max(g["tid"][:, :], g["tid"][:, :], 0.0), inc=ve)
    nv_tid = op("vector", lambda e: e.tensor_copy(g["tidu"][:], g["tid"][:, :]), inc=ve)
    p.auto = False

    chunks = [(e01, sc) for e01 in range(2) for sc in range(NSC)]
    ln_na = {}
    trc_after = {}

    def prep_unit(ci, part):
        e01, sc = chunks[ci]
        pp = ci % 2
        base = pp * KD * TCH
        ss = part // 3
        ph = part % 3
        xsl = ss % 2
        xtv = g["xt"][:, xsl * D:(xsl + 1) * D]
        if ph == 0:
            j = e01 * RPE + sc * 4 + ss
            op("gpsimd", lambda e, j=j, xtv=xtv: e.indirect_dma_start(
                out=xtv, out_offset=None,
                in_=g["x_full"][:, :],
                in_offset=IndirectOffsetOnAxis(ap=g["tidu"][:, j:j + 1], axis=0),
                bounds_check=p.breg(e, T - 1), oob_is_err=False),
                waits=[("p3v", nv_tid)], res=("xt%d" % xsl,), inc="xg%d" % xsl, amt=16)
            nxg = p.cnt["xg%d" % xsl]
            if ss == 0:
                bfd = [(g["xnb"][:], 0, D, ("xnb",))]
            elif ss == 2:
                bfd = [(g["yev"][:], 0, 512, ("yev",)),
                       (g["yc_sb"][:], 512, D, ("yc0",))]
            else:
                bfd = [(g["dtb"][:], 0, 512, ("dtb",)), (g["vb"][:], 512, D, ("vb",))]
            nv, na = emit_ln(p, g, xtv, ve, ae, waits0=[("xg%d" % xsl, nxg)], bf_dests=bfd)
            p.set_tok("xt%d" % xsl, ve, nv)
            ln_na[(ci, ss)] = na
        else:
            na = ln_na[(ci, ss)]
            ks = range(4) if ph == 1 else range(4, 8)
            for k in ks:
                first = (ph == 1 and ss == 0 and k == 0)
                r = ("PTR",) + ((("xeTb", "xTc1") if pp == 1 else ("xTc0",)) if first else ())
                if ss == 0:
                    src_t = g["xnb"][:, k * 128:(k + 1) * 128]
                elif ss == 2:
                    src_t = (g["yev"][:, k * 128:(k + 1) * 128] if k < 4
                             else g["yc_sb"][:, (k - 4) * 128:(k - 3) * 128])
                else:
                    src_t = (g["dtb"][:, k * 128:(k + 1) * 128] if k < 4
                             else g["vb"][:, (k - 4) * 128:(k - 3) * 128])
                nt = op("tensor", lambda e, src_t=src_t: e.transpose(
                    g["P_tr"][:, :], src_t, g["identb"][:, :]),
                    waits=[(ae, na)], res=r, inc=te)
                op("vector", lambda e, k=k, ss=ss, base=base: e.tensor_copy(
                    g["xnTc"][:, base + k * SCH + ss * 128:base + k * SCH + (ss + 1) * 128],
                    g["P_tr"][:, :]), waits=[(te, nt)], inc="p3trc", rel=("PTR",))
            if ph == 2 and ss == 3:
                trc_after[ci] = p.cnt["p3trc"]
            if ph == 2:
                if ss == 0:
                    p.set_tok("xnb", "p3trc", p.cnt["p3trc"])
                elif ss == 2:
                    p.set_tok("yev", "p3trc", p.cnt["p3trc"])
                    p.set_tok("yc0", "p3trc", p.cnt["p3trc"])
                else:
                    p.set_tok("dtb", "p3trc", p.cnt["p3trc"])
                    p.set_tok("vb", "p3trc", p.cnt["p3trc"])

    for part in range(12):
        prep_unit(0, part)
    prep_unit(1, 0)
    prep_unit(1, 3)
    prep_unit(1, 6)

    isched = {}
    for X in range(1, len(chunks)):
        if X == 1:
            locs = {1: (0, 0), 2: (0, 1), 4: (0, 2), 5: (0, 3), 9: (0, 4),
                    7: (0, 5), 8: (0, 6), 10: (0, 7), 11: (0, 8)}
        else:
            locs = {0: (X - 2, 10), 3: (X - 2, 11), 6: (X - 2, 12),
                    1: (X - 1, 0), 2: (X - 1, 1), 4: (X - 1, 2), 5: (X - 1, 3),
                    9: (X - 1, 4), 7: (X - 1, 5), 8: (X - 1, 6),
                    10: (X - 1, 7), 11: (X - 1, 8)}
        for pt, loc in locs.items():
            isched[loc] = (X, pt)

    wN = {}
    for ci, (e01, sc) in enumerate(chunks):
        pp = ci % 2
        base = pp * KD * TCH
        if sc == 0:
            wsl_e = WSLOTS2 if e01 == 0 else WSLOTS5
            wN[e01] = (
                cast_w(p, g, [(g["wg_sl"][e01, k * 128:(k + 1) * 128, :], FF) for k in range(KD)],
                       lambda j, w: g["wbig"][:, j * FF:j * FF + w], "wg%d" % e01,
                       extra_res=("wbigb",), slots=wsl_e),
                cast_w(p, g, [(g["wu_sl"][e01, k * 128:(k + 1) * 128, :], FF) for k in range(KD)],
                       lambda j, w: g["wbig2"][:, j * FF:j * FF + w], "wu%d" % e01,
                       extra_res=("wbig2b",), slots=wsl_e),
                cast_w(p, g, [(g["wd_sl"][e01, f * 128:(f + 1) * 128, :], D) for f in range(FCH)],
                       lambda j, w: g["wdown"][:, j * D:j * D + w], "wd%d" % e01,
                       extra_res=("wdownb",), slots=wsl_e))
        NWG, NWU, NWD = wN[e01]
        waits1 = [("p3trc", trc_after[ci]), ("wg%dc" % e01, NWG)]
        waits2 = [("p3trc", trc_after[ci]), ("wu%dc" % e01, NWU)]
        waits3 = [("wd%dc" % e01, NWD)]

        def inject(slot, ci=ci):
            tgt = isched.get((ci, slot))
            if tgt is not None:
                prep_unit(tgt[0], tgt[1])

        def out_cb(ss, dh, nt, b3, b3r, e01=e01, sc=sc):
            na = p.op("scalar", lambda e, dh=dh, b3=b3: e.activation(
                out=g["eout"][:, dh * 512:(dh + 1) * 512], in_=g[b3][:, :],
                func=AF.Copy), waits=[("p3t", nt)], res=("eoutb",), inc="p3a")
            p.set_tok(b3r, "p3a", na)
            if dh == 1:
                tbl = g["ag5a_in"] if e01 == 0 else g["ag5b_in"]
                tw5 = "ag5wa" if e01 == 0 else "ag5wb"
                p.dma("sync",
                      tbl[sc * 512 + ss * 128:sc * 512 + (ss + 1) * 128, :],
                      g["eout"][:, :], inc=tw5, waits=[("p3a", na)])
                p.set_tok("eoutb", tw5, p.cnt[tw5])

        en1, en2, en3 = mlp_block(
            p, g, lambda k, base=base: g["xnTc"][:, base + k * SCH:base + (k + 1) * SCH],
            waits1, waits2, waits3, out_cb, inject)
        if pp == 1:
            p.set_tok("xeTb", te, en2)
            p.set_tok("xTc1", te, en2)
        else:
            p.set_tok("xTc0", te, en2)
        if sc == NSC - 1:
            p.set_tok("wbigb", te, en1)
            p.set_tok("wbig2b", te, en2)
            p.set_tok("wdownb", te, en3)
            if e01 == 0:
                op("gpsimd", lambda e: e.collective_compute(
                    "AllGather", ALU.bypass, replica_groups=[list(range(NCORE))],
                    ins=[g["ag5a_in"].tensor.ap().opt()],
                    outs=[g["ag5a_out"].tensor.ap().opt()]),
                    waits=[("ag5wa", p.cnt["ag5wa"])], inc="cc")
            else:
                nag5b = p.cnt["ag5wb"]


    NSWG = cast_w(p, g, [(g["sw_gate"][k * 128:(k + 1) * 128, :], FF) for k in range(KD)],
                  lambda j, w: g["wbig"][:, j * FF:j * FF + w], "swg",
                  extra_res=("wbigb",), slots=WSLOTS5)
    NSWD = cast_w(p, g, [(g["sw_down"][f * 128:(f + 1) * 128, :], D) for f in range(FCH)],
                  lambda j, w: g["wdown"][:, j * D:j * D + w], "swd",
                  extra_res=("wdownb",), slots=WSLOTS5)
    NSWU = cast_w(p, g, [(g["sw_up"][k * 128:(k + 1) * 128, :], FF) for k in range(KD)],
                  lambda j, w: g["wbig2"][:, j * FF:j * FF + w], "swu",
                  extra_res=("wbig2b",), slots=WSLOTS5)

    for tc in range(4):
        sl = tc % 2
        for k in range(KD):
            src = g["ag1a_in"] if k < 4 else g["ag1b_in"]
            kk = k % 4
            dma("sync", g["xnTc"][:, (sl * KD + k) * TCH:(sl * KD + k + 1) * TCH],
                src[kk * 128:(kk + 1) * 128, tc * TCH:(tc + 1) * TCH],
                res=("xTc%d" % sl,) if k == 0 else (), inc="sxl%d" % sl)
        nxl = p.cnt["sxl%d" % sl]
        waits1 = [("sxl%d" % sl, nxl), ("swgc", NSWG)]
        waits2 = [("sxl%d" % sl, nxl), ("swuc", NSWU)]
        waits3 = [("swdc", NSWD)]

        def out_cb(ss, dh, nt, b3, b3r, tc=tc):
            ur = 3072 + dh * 512
            na = p.op("scalar", lambda e, ur=ur, b3=b3: e.activation(
                out=g["UP"][:, ur:ur + 512], in_=g[b3][:, :], func=AF.Copy),
                waits=[("p3t", nt)], res=("UPm%d" % dh,), inc="p3a")
            p.set_tok(b3r, "p3a", na)
            p.dma("sync", g["shared_dram"][tc * 512 + ss * 128:tc * 512 + (ss + 1) * 128,
                                           dh * 512:(dh + 1) * 512],
                  g["UP"][:, ur:ur + 512], inc="shst%d" % dh, waits=[("p3a", na)])
            p.set_tok("UPm%d" % dh, "shst%d" % dh, p.cnt["shst%d" % dh])

        sn1, sn2, sn3 = mlp_block(
            p, g,
            lambda k, sl=sl: g["xnTc"][:, (sl * KD + k) * TCH:(sl * KD + k + 1) * TCH],
            waits1, waits2, waits3, out_cb)
        p.set_tok("xTc%d" % sl, te, sn2)
    p.set_tok("wbigb", te, sn1)
    p.set_tok("wbig2b", te, sn2)
    p.set_tok("wdownb", te, sn3)
    for j in range(8):
        p.set_tok("m3y%d" % j, te, sn3)
    op("gpsimd", lambda e: e.collective_compute(
        "AllGather", ALU.bypass, replica_groups=[list(range(NCORE))],
        ins=[g["ag5b_in"].tensor.ap().opt()], outs=[g["ag5b_out"].tensor.ap().opt()]),
        waits=[("ag5wb", nag5b), ("shst0", 16 * 16), ("shst1", 16 * 16)], inc="cc")



def emit_p4(nc, p, g):
    op, dma = p.op, p.dma
    ve, ae, te, pl = "p4v", "p4a", "p4t", "p4p"
    for j in range(8):
        p.set_tok("m3y%d" % j, "p3t", p.cnt["p3t"])

    NWO = cast_w(p, g, [(g["Wout"][k * 128:(k + 1) * 128, :], D) for k in range(KD)],
                 lambda j, w: g["wbig2"][:, j * D:j * D + w], "wo",
                 extra_res=("wbig2b",))
    dma("sync", g["UP"][0:1, 0:D], g["bout"][None, :], inc="bol", res=("UPp4",))
    for dh in range(2):
        nt = op("tensor", lambda e, dh=dh: e.matmul(
            g["P_bc1"][:, :], g["onesr"][:, :], g["UP"][0:1, dh * 512:(dh + 1) * 512],
            start=True, stop=True), waits=[("bol", 16)], res=("PB1",), inc=te)
        nv = op("vector", lambda e, dh=dh: e.tensor_copy(
            g["bout_bc"][:, dh * 512:(dh + 1) * 512], g["P_bc1"][:, :]),
            waits=[(te, nt)], inc=ve)
        p.set_tok("PB1", ve, nv)
    p.set_tok("UPp4", ve, p.cnt[ve])
    p.set_tok("m0P0", "p4t", p.cnt["p4t"])
    p.set_tok("m1P0", "p4t", p.cnt["p4t"])
    p.set_tok("s0P0", "shst0", 16 * 16)
    p.set_tok("s1P0", "shst1", 16 * 16)
    p.also_rel("s0P0", "shst1", 16 * 16)
    p.also_rel("s1P0", "shst0", 16 * 16)
    for _t in ("m0P0", "m1P0", "gP0", "s0P0", "s1P0", "UPp4"):
        p.also_rel(_t, "p3a", p.cnt["p3a"])
        p.also_rel(_t, "p3v", p.cnt["p3v"])
        p.also_rel(_t, "p2a", p.cnt["p2a"])
        p.also_rel(_t, "p2v", p.cnt["p2v"])

    with p.chained():
        op("vector", lambda e: e.tensor_copy(g["ygu"][:], g["ygf"][:]),
           waits=[("cld", p.NCLD)], inc=ve)
    nv_yg = p.cnt[ve]

    for j in range(8):
        op("gpsimd", lambda e, j=j: e.indirect_dma_start(
            out=g["m3"][:, j * TPC:(j + 1) * TPC], out_offset=None,
            in_=g["ag4_out"][:, :],
            in_offset=IndirectOffsetOnAxis(ap=g["ygu"][:, j:j + 1], axis=0),
            bounds_check=p.breg(e, NCORE * DSL * 8 - 1), oob_is_err=False),
            waits=[("cc", 5), ("p4v", nv_yg)], res=("m3y%d" % j,), inc="ygl", amt=16)
    zsbv = g["xnTc"][:, 0:TPC]
    for j in range(8):
        dma("sync", zsbv, g["z_dram"][j * 128:(j + 1) * 128, :],
            waits=[("zst0", 16 * 16), ("zst1", 16 * 16)], res=("zsbb", "xTc0"), inc="zld")
        nz = p.cnt["zld"]
        nv = op("vector", lambda e, j=j, zsbv=zsbv: e.tensor_mul(
            g["m3"][:, j * TPC:(j + 1) * TPC], g["m3"][:, j * TPC:(j + 1) * TPC],
            zsbv),
            waits=[("ygl", 16 * 8), ("zld", nz)], inc=ve)
        p.set_tok("zsbb", ve, nv)
        p.set_tok("xTc0", ve, nv)
    NV_YZ = p.cnt[ve]

    dma("sync", g["xt"][:, 0:D], g["xn_dram"][0:128, :],
        waits=[("xnst", 16 * 16)], res=("xt0",), inc="xnl0")
    for i in range(NTIL):
        P = i % 2
        if P == 0:
            moe0 = g["UP"][:, 0:D]
            moe1 = g["UP"][:, 1024:1024 + D]
            ggv = g["UP"][:, 2048:2048 + D]
            sv = [g["UP"][:, 3072:3584], g["UP"][:, 3584:4096]]
            uv = g["UP"][:, 1024:1024 + D]
        else:
            moe0 = g["a_sb"][:, 0:D]
            moe1 = g["xdb_sb"][:, 0:D]
            ggv = g["h_sb"][:, 0:D]
            sv = [g["dtf"][:, 0:512], g["vf"][:, 0:512]]
            uv = g["xdb_sb"][:, 0:D]
        ex = {"m0": ("wstA",), "m1": ("wstX",), "gg": ("wstH",),
              "s0": ("dtf",), "s1": ("vf",)} if i == 1 else {}

        for dh in range(2):
            pa = "P_acc0" if dh == 0 else "P_acc1"
            for k in range(KD):
                r = ("PA%d" % dh,) if k == 0 else ()
                op("tensor", lambda e, k=k, dh=dh, i=i, pa=pa: e.matmul(
                    g[pa][:, :], g["m3"][:, k * TPC + i * 128:k * TPC + (i + 1) * 128],
                    g["wbig2"][:, k * D + dh * 512:k * D + (dh + 1) * 512],
                    start=(k == 0), stop=(k == KD - 1)),
                    waits=[("p4v", NV_YZ), ("woc", NWO)] if k == 0 else (), res=r, inc=te)
            nt = p.cnt[te]
            nv = op("vector", lambda e, dh=dh, pa=pa, sva=sv[dh]: e.tensor_add(
                sva, g[pa][:, :], g["bout_bc"][:, dh * 512:(dh + 1) * 512]),
                waits=[(te, nt)], res=("s%dP%d" % (dh, P),) + ex.get("s%d" % dh, ()),
                inc=ve)
            p.set_tok("PA%d" % dh, ve, nv)
        nv_s = p.cnt[ve]

        nv_m = []
        for kk in range(2):
            sx = 2 * i + kk
            msl = P * 2 + kk
            if P == 0:
                mgv = g["xnTc"][:, TPC + kk * D:TPC + (kk + 1) * D]
                mgv2 = g["xnTc"][:, kk * D:(kk + 1) * D]
                slot_tok = ("xTc0",)
            else:
                mgv = g["xnTc"][:, 4096 + kk * D:4096 + (kk + 1) * D]
                mgv2 = g["xnTc"][:, 4096 + 2048 + kk * D:4096 + 2048 + (kk + 1) * D]
                slot_tok = ("xTc1", "xeTb") if i == 1 else ()
            mo = moe0 if kk == 0 else moe1
            op("gpsimd", lambda e, sx=sx, mgv=mgv: e.indirect_dma_start(
                out=mgv, out_offset=None,
                in_=g["ag5a_out"][:, :],
                in_offset=IndirectOffsetOnAxis(ap=g["ucols"][:, sx:sx + 1], axis=0),
                bounds_check=p.breg(e, NCORE * CAP - 1), oob_is_err=False),
                waits=[("cc", 6)], res=("mg%d" % msl,) + slot_tok, inc="mglA%d" % msl, amt=16)
            op("gpsimd", lambda e, sx=sx, mgv2=mgv2: e.indirect_dma_start(
                out=mgv2, out_offset=None,
                in_=g["ag5b_out"][:, :],
                in_offset=IndirectOffsetOnAxis(ap=g["ucols"][:, sx:sx + 1], axis=0),
                bounds_check=p.breg(e, NCORE * CAP - 1), oob_is_err=False),
                waits=[("cc", 7)], res=("mgB%d" % msl, "zsbb" if P == 0 else "mgB%d" % msl) + slot_tok,
                inc="mglB%d" % msl, amt=16)
            nmgA = p.cnt["mglA%d" % msl]
            nmgB = p.cnt["mglB%d" % msl]
            na = op("scalar", lambda e, sx=sx, mgv=mgv, mo=mo: e.activation(
                out=mo, in_=mgv, func=AF.Copy,
                scale=g["rcols"][:, sx:sx + 1]),
                waits=[("mglA%d" % msl, nmgA)],
                res=("m%dP%d" % (kk, P),) + ex.get("m%d" % kk, ()), inc=ae)
            p.set_tok("mg%d" % msl, ae, na)
            if P == 0:
                p.also_rel("xTc0", "p4a", na)
            nv = op("vector", lambda e, sx=sx, mgv2=mgv2, mo=mo: e.scalar_tensor_tensor(
                mo, mgv2, g["rcols2"][:, sx:sx + 1], mo, ALU.mult, ALU.add),
                waits=[("mglB%d" % msl, nmgB), ("p4a", na)], inc=ve)
            p.set_tok("mgB%d" % msl, ve, nv)
            if P == 0:
                p.also_rel("zsbb", "p4v", nv)
                p.also_rel("xTc0", "p4v", nv)
            else:
                p.also_rel("xTc1", "p4v", nv)
                p.also_rel("xeTb", "p4v", nv)
            nv_m.append(nv)
        nv_moe = p.cnt[ve]

        xsl = i % 2
        xtv = g["xt"][:, xsl * D:(xsl + 1) * D]
        nxn0 = 16 * (i // 2 + 1)
        nv = op("vector", lambda e, moe0=moe0, moe1=moe1: e.tensor_add(
            moe0, moe0, moe1),
            waits=[("p4v", nv_moe)], inc=ve)
        p.set_tok("m1P%d" % P, ve, nv)
        nv = op("vector", lambda e, xtv=xtv, moe0=moe0: e.tensor_add(moe0, moe0, xtv),
                waits=[("xnl%d" % xsl, nxn0), (ve, nv)], inc=ve)
        dma("sync", xtv, g["shared_dram"][i * 128:(i + 1) * 128, :],
            waits=[("shst0", 16 * 16), ("shst1", 16 * 16), (ve, nv)], inc="xshl")
        nxn = p.cnt["xshl"]
        nv = op("vector", lambda e, xtv=xtv, moe0=moe0: e.tensor_add(moe0, moe0, xtv),
                waits=[("xshl", nxn), (ve, nv)], inc=ve)
        na = op("scalar", lambda e, ggv=ggv, moe0=moe0: e.activation(
            out=ggv, in_=moe0, func=AF.Silu),
            waits=[(ve, nv)], res=("gP%d" % P,) + ex.get("gg", ()), inc=ae)
        p.set_tok("m0P%d" % P, ae, na)
        nvm = op("vector", lambda e, uv=uv, ggv=ggv, sva=sv[0]: e.tensor_mul(
            uv[:, 0:512], sva, ggv[:, 0:512]),
            waits=[("p4a", na), ("p4v", nv_s)], res=("m1P%d" % P,), inc=ve)
        nv = op("vector", lambda e, uv=uv, ggv=ggv, svb=sv[1]: e.tensor_mul(
            uv[:, 512:1024], svb, ggv[:, 512:1024]),
            waits=[("p4a", na), ("p4v", nv_s), (ve, nvm)], inc=ve)
        p.set_tok("s0P%d" % P, ve, nv)
        p.set_tok("s1P%d" % P, ve, nv)
        p.set_tok("gP%d" % P, ve, nv)
        dma("sync", xtv, g["x_own"][i * 128:(i + 1) * 128, :], waits=[(ve, nv)],
            inc="xxl")
        nxn = p.cnt["xxl"]
        if i + 1 < NTIL:
            xsl2 = (i + 1) % 2
            dma("sync", g["xt"][:, xsl2 * D:(xsl2 + 1) * D],
                g["xn_dram"][(i + 1) * 128:(i + 2) * 128, :],
                res=("xt%d" % xsl2,), inc="xnl%d" % xsl2)
        nv = op("vector", lambda e, xtv=xtv, uv=uv: e.tensor_add(g["xnf"][:], uv, xtv),
                waits=[("xxl", nxn), (ve, nv)], res=("xnf",), inc=ve)
        p.set_tok("m1P%d" % P, ve, nv)
        p.set_tok("xt%d" % xsl, ve, nv)
        dma("sync", g["o_out"][i * 128:(i + 1) * 128, :], g["xnf"][:],
            inc="ost", waits=[(ve, nv)])
        p.set_tok("xnf", "ost", p.cnt["ost"])

    allw = [("ost", p.cnt["ost"]), ("cc", p.cnt["cc"])]
    for nm in DEBUG_TAPS:
        dma("sync", g["tap_" + nm][tuple(slice(None) for _ in g[nm].shape)],
            g[nm][tuple(slice(None) for _ in g[nm].shape)], inc="tapw", waits=allw)
    if DEBUG_TAPS:
        op("sync", lambda e: e.nop(), waits=[("tapw", p.cnt["tapw"])])
    op("sync", lambda e: e.nop(), waits=[("ost", p.cnt["ost"])])


# version 34
# speedup vs baseline: 1.1254x; 1.1254x over previous
import contextlib
import numpy as np
import ml_dtypes
import concourse.bass as bass
import concourse.mybir as mybir
from concourse.bass_core_rust import IndirectOffsetOnAxis
from concourse.bass_utils import run_bass_kernel_spmd

F32 = mybir.dt.float32
BF16 = mybir.dt.bfloat16
U32 = mybir.dt.uint32
AF = mybir.ActivationFunctionType
ALU = mybir.AluOpType
AX = mybir.AxisListType.X

NCORE = 8
B, L, D, N = 2, 8192, 1024, 16
T = B * L
TPC = T // NCORE
E, CAP, FF = 16, 2560, 1408
DSL = D // NCORE
TCH = 512
NIT = T // TCH
NTIL = TPC // 128
FCH = FF // 128
KD = D // 128
TBL = E * CAP + 128
TRASH = E * CAP
RPE = CAP // 128
SCH = 512
NSC = CAP // SCH
EPS = 1e-5
BIGF = 1.0e9
DEBUG_TAPS = ()



def _consts(core):
    c = {}
    c["ident_bf"] = np.eye(128, dtype=ml_dtypes.bfloat16)
    c["ident_f"] = np.eye(128, dtype=np.float32)
    c["iota16_bc"] = np.broadcast_to(np.arange(16, dtype=np.float32), (128, 16)).copy()
    c["parb"] = np.broadcast_to((np.arange(16) % 2).astype(np.float32), (128, 16)).copy()
    m = np.zeros((16, 128), np.float32)
    for p_ in range(128):
        m[p_ % 16, p_] = 1.0
    c["bcmask16"] = m.astype(ml_dtypes.bfloat16)
    rm = np.zeros((128, 16 * 128), np.float32)
    for gg in range(16):
        for p_ in range(128):
            rm[gg * 8 + p_ // 16, gg * 128 + p_] = 1.0
    c["repmask"] = rm.astype(ml_dtypes.bfloat16)
    ns = np.zeros((128, 16 * 128), np.float32)
    for gg in range(16):
        for p_ in range(128):
            ns[p_, gg * 128 + 8 * gg + p_ // 16] = 1.0
    c["nsmask"] = ns.astype(ml_dtypes.bfloat16)
    tok1 = np.empty((128, NTIL), np.float32)
    for i in range(NTIL):
        tok1[:, i] = core * TPC + i * 128 + np.arange(128) + 1
    c["tok1"] = tok1
    c["trash_col"] = (TRASH + np.arange(128, dtype=np.float32))[:, None].copy()
    c["mask_lt8"] = (np.arange(8) < core).astype(np.float32)[:, None].copy()
    yg = np.empty((128, 8), np.float32)
    for j in range(8):
        yg[:, j] = j * 1024 + np.arange(128) * 8 + core
    c["ygidx"] = yg
    er = np.zeros((128, 1), np.float32)
    er[0:2 * RPE, 0] = core * 2 * RPE + np.arange(2 * RPE)
    c["erow"] = er
    c["ones_row"] = np.ones((1, 128), np.float32)
    c["ones_col16"] = np.ones((16, 1), np.float32)
    return c


def kernel(**inputs):
    ii = {k: np.ascontiguousarray(np.asarray(v, dtype=np.float32))
          for k, v in inputs.items()}
    x2d = ii["L0_attn_x"].reshape(T, D)
    in_maps = []
    for c in range(NCORE):
        m = {
            "x_own": np.ascontiguousarray(x2d[c * TPC:(c + 1) * TPC]),
            "x_full": x2d,
            "Wz": ii["L0_attn_Wz"], "bz": ii["L0_attn_bz"],
            "Wx_sl": np.ascontiguousarray(ii["L0_attn_Wx"][:, c * DSL:(c + 1) * DSL]),
            "bx_sl": np.ascontiguousarray(ii["L0_attn_bx"][c * DSL:(c + 1) * DSL]),
            "Wdt_sl": np.ascontiguousarray(ii["L0_attn_Wdt"][:, c * DSL:(c + 1) * DSL]),
            "bdt_sl": np.ascontiguousarray(ii["L0_attn_bdt"][c * DSL:(c + 1) * DSL]),
            "WB": ii["L0_attn_WB"], "bB": ii["L0_attn_bB"],
            "WC": ii["L0_attn_WC"], "bC": ii["L0_attn_bC"],
            "A_log_sl": np.ascontiguousarray(ii["L0_attn_A_log"][c * DSL:(c + 1) * DSL]),
            "Wout": ii["L0_attn_Wout"], "bout": ii["L0_attn_bout"],
            "router_w": ii["L0_ffn_router_w"], "router_b": ii["L0_ffn_router_b"],
            "wg_sl": np.ascontiguousarray(ii["L0_ffn_w_gate"][2 * c:2 * c + 2]),
            "wu_sl": np.ascontiguousarray(ii["L0_ffn_w_up"][2 * c:2 * c + 2]),
            "wd_sl": np.ascontiguousarray(ii["L0_ffn_w_down"][2 * c:2 * c + 2]),
            "sw_gate": ii["L0_ffn_sw_gate"], "sw_up": ii["L0_ffn_sw_up"],
            "sw_down": ii["L0_ffn_sw_down"],
        }
        m.update(_consts(c))
        in_maps.append(m)
    nc = build_nc()
    res = run_bass_kernel_spmd(nc, in_maps, core_ids=list(range(NCORE)))
    out = np.concatenate([np.asarray(r["o_out"]) for r in res.results], axis=0)
    return np.ascontiguousarray(out.reshape(B, L, D).astype(np.float32))



class Prog:
    ENGS = ("sync", "tensor", "vector", "scalar", "gpsimd")

    def __init__(self, nc):
        self.nc = nc
        self.q = {e: [] for e in self.ENGS}
        self.semh = {}
        self.cnt = {}
        self.tok = {}
        self.last = {}
        self.auto = False

    def sem(self, name):
        if name not in self.semh:
            self.semh[name] = self.nc.alloc_semaphore(name)
            self.cnt[name] = 0
        return self.semh[name]

    def op(self, eng, builder, waits=(), res=(), inc=None, amt=1, rel=()):
        wl = list(waits)
        for r in res:
            wl.extend(self.tok.get(r, ()))
        if self.auto and eng in ("vector", "scalar") and eng in self.last:
            wl.append(self.last[eng])
        post = None
        incs = []
        if inc is not None:
            self.sem(inc)
            self.cnt[inc] += amt
            post = self.cnt[inc]
            incs = [(inc, amt)]
        for s, _v in wl:
            self.sem(s)
        self.q[eng].append((builder, tuple(wl), tuple(incs)))
        if inc is not None and eng in ("vector", "scalar", "gpsimd", "tensor"):
            self.last[eng] = (inc, post)
        if rel:
            assert inc is not None
            for r in rel:
                self.tok[r] = [(inc, post)]
        return post

    def chained(self):
        import contextlib as _ctx

        @_ctx.contextmanager
        def cm():
            prev = self.auto
            self.auto = True
            try:
                yield
            finally:
                self.auto = prev
        return cm()

    def breg(self, e, val):
        if not hasattr(self, "_bregs"):
            self._bregs = {}
        if val not in self._bregs:
            self._bregs[val] = e.to_reg(val)
        return self._bregs[val]

    def also_rel(self, r, sem, val):
        self.tok.setdefault(r, []).append((sem, val))

    def set_tok(self, r, sem, val):
        self.tok[r] = [(sem, val)]

    def dma(self, eng, out, in_, waits=(), res=(), inc=None, rel=(), **kw):
        return self.op(eng, lambda e: e.dma_start(out=out, in_=in_, **kw),
                       waits=waits, res=res, inc=inc, amt=16, rel=rel)

    def flush(self, block):
        for name in self.ENGS:
            ops = self.q[name]
            if not ops:
                continue

            def mk(ops):
                def f(eng):
                    for builder, waits, incs in ops:
                        seen = {}
                        for s, v in waits:
                            seen[s] = max(seen.get(s, 0), v)
                        for s, v in seen.items():
                            eng.wait_ge(self.semh[s], v)
                        ins = builder(eng)
                        for s, a in incs:
                            ins.then_inc(self.semh[s], a)
                return f

            getattr(block, name)(mk(ops))


def build_nc():
    nc = bass.Bass("TRN2", target_bir_lowering=False, debug=False)
    g = {}

    def dram(name, shape, dt, kind="Internal"):
        g[name] = nc.dram_tensor(name, shape, dt, kind=kind).ap()

    for nm, sh, dt in [
        ("x_own", [TPC, D], F32), ("x_full", [T, D], F32),
        ("Wz", [D, D], F32), ("bz", [D], F32),
        ("Wx_sl", [D, DSL], F32), ("bx_sl", [DSL], F32),
        ("Wdt_sl", [D, DSL], F32), ("bdt_sl", [DSL], F32),
        ("WB", [D, N], F32), ("bB", [N], F32),
        ("WC", [D, N], F32), ("bC", [N], F32),
        ("A_log_sl", [DSL, N], F32),
        ("Wout", [D, D], F32), ("bout", [D], F32),
        ("router_w", [D, N], F32), ("router_b", [N], F32),
        ("wg_sl", [2, D, FF], F32), ("wu_sl", [2, D, FF], F32),
        ("wd_sl", [2, FF, D], F32),
        ("sw_gate", [D, FF], F32), ("sw_up", [D, FF], F32), ("sw_down", [FF, D], F32),
        ("ident_bf", [128, 128], BF16), ("ident_f", [128, 128], F32),
        ("iota16_bc", [128, 16], F32),
        ("bcmask16", [16, 128], BF16), ("repmask", [128, 16 * 128], BF16),
        ("nsmask", [128, 16 * 128], BF16), ("tok1", [128, NTIL], F32),
        ("trash_col", [128, 1], F32), ("mask_lt8", [8, 1], F32),
        ("ygidx", [128, 8], F32), ("parb", [128, 16], F32),
        ("ones_row", [1, 128], F32), ("ones_col16", [16, 1], F32),
        ("erow", [128, 1], F32),
    ]:
        dram(nm, sh, dt, kind="ExternalInput")

    dram("o_out", [TPC, D], F32, kind="ExternalOutput")

    dram("ag1a_in", [D // 2, TPC], BF16)
    dram("ag1b_in", [D // 2, TPC], BF16)
    dram("ag1a_out", [NCORE * D // 2, TPC], BF16)
    dram("ag1b_out", [NCORE * D // 2, TPC], BF16)
    dram("ag2_in", [34, TPC], F32)
    dram("ag2_out", [NCORE * 34, TPC], F32)
    dram("ar3_in", [TBL, 1], F32)
    dram("ar3_out", [TBL, 1], F32)
    dram("ag4_in", [DSL * 8, TPC], BF16)
    dram("ag4_out", [NCORE * DSL * 8, TPC], BF16)
    dram("ag5a_in", [CAP, D], BF16)
    dram("ag5b_in", [CAP, D], BF16)
    dram("ag5a_out", [NCORE * CAP, D], BF16)
    dram("ag5b_out", [NCORE * CAP, D], BF16)
    dram("z_dram", [D, TPC], BF16)
    dram("xn_dram", [TPC, D], F32)
    dram("shared_dram", [TPC, D], F32)

    for nm in DEBUG_TAPS:
        src = g[nm]
        dram("tap_" + nm, list(src.shape), src.dtype, kind="ExternalOutput")

    build_program(nc, g)
    return nc


def build_program(nc, g):
    p = Prog(nc)
    stck = contextlib.ExitStack()

    def sb(name, shape, dt):
        g[name] = stck.enter_context(nc.sbuf_tensor(name, shape, dt))

    def ps(name, shape, dt):
        g[name] = stck.enter_context(nc.psum_tensor(name, shape, dt))

    with stck:
        sb("xt", [128, 2 * D], F32)
        sb("UP", [128, 4417], F32)
        sb("xnf", [128, D], F32)
        sb("xnb", [128, D], BF16)
        sb("xnT_tile", [128, KD * 128], BF16)
        sb("wbig", [128, KD * FF], BF16)
        sb("wbig2", [128, KD * FF], BF16)
        sb("wdown", [128, FCH * D], BF16)
        sb("wstage", [128, FF], F32)
        sb("xnTc", [128, 2 * KD * TCH], BF16)
        sb("wsl", [128, KD * 2 * DSL], BF16)
        sb("wsmall", [128, KD * 3 * N], BF16)
        sb("identb", [128, 128], BF16)
        sb("identf", [128, 128], F32)
        sb("bc16", [16, 128], BF16)
        sb("repm", [128, 16 * 128], BF16)
        sb("nsm", [128, 16 * 128], BF16)
        sb("iotab", [128, 16], F32)
        sb("tok1_sb", [128, NTIL], F32)
        sb("trash_sb", [128, 1], F32)
        sb("mlt8", [8, 1], F32)
        sb("onesr", [1, 128], F32)
        sb("ones16c", [16, 1], F32)
        sb("acols", [128, 16], F32)
        sb("bias_sb", [128, 16], F32)
        sb("bb_sb", [16, 2], F32)
        sb("rb_bc", [128, 16], F32)
        sb("bout_bc", [128, D], F32)
        sb("off_bc", [128, 16], F32)
        sb("col", [128, 48], F32)
        sb("packA", [128, 32], F32)
        sb("packB", [128, 32], F32)
        sb("ohtok", [128, 64], F32)
        sb("oh_c", [16, TCH], F32)
        sb("ones16t", [16, TCH], F32)
        sb("cum_c", [16, TCH], F32)
        sb("carry16", [16, 1], F32)
        sb("lcols", [128, 2 * NTIL], F32)
        sb("pcols", [128, 2 * NTIL], F32)
        sb("rcols", [128, 2 * NTIL], F32)
        sb("rcols2", [128, 2 * NTIL], F32)
        sb("ucols", [128, 4 * NTIL], U32)
        sb("off16", [16, 1], F32)
        sb("tot8", [8, 16], F32)
        sb("dtf", [128, TCH], F32)
        sb("dtb", [128, TCH], BF16)
        sb("vf", [128, TCH], F32)
        sb("vb", [128, TCH], BF16)
        sb("bmc_st", [16, TCH], F32)
        sb("bmcb", [16, TCH], BF16)
        sb("bm_bc", [128, TCH], BF16)
        sb("c_bc", [128, TCH], BF16)
        sb("a_sb", [128, 2 * TCH], F32)
        sb("xdb_sb", [128, 2 * TCH], F32)
        sb("h_sb", [128, 2 * TCH], F32)
        sb("yc_sb", [128, TCH], BF16)
        sb("carryg", [128, 16], F32)
        sb("yev", [128, TCH], BF16)
        sb("m3", [128, 3 * FCH * SCH], BF16)
        sb("eout", [128, D], BF16)
        sb("tidu", [128, 128], U32)
        sb("erowu", [128, 1], U32)
        sb("ygu", [128, 8], U32)

        g["xeT"] = g["xnTc"][:, KD * TCH:2 * KD * TCH]
        g["krow"] = g["UP"][0:33, 1024:1280]
        g["tid"] = g["UP"][:, 2048:2176]
        g["ygf"] = g["UP"][:, 2176:2184]
        g["fconst"] = g["UP"][:, 2184:2186]

        ps("P_acc0", [128, 512], F32)
        ps("P_acc1", [128, 512], F32)
        ps("P_acc2", [128, 512], F32)
        ps("P_acc3", [128, 512], F32)
        ps("P_bc0", [128, 512], F32)
        ps("P_bc1", [128, 512], F32)
        ps("P_y", [128, 512], F32)
        ps("P_tr", [128, 128], BF16)

        emit_p1(nc, p, g)
        emit_p2(nc, p, g)
        emit_p3(nc, p, g)
        emit_p4(nc, p, g)

        with nc.Block() as block:
            p.flush(block)



def emit_ln(p, g, src_ap, ve, ae, waits0=(), bf_dests=None):
    op = p.op
    sq = g["UP"][:, 0:D]
    p.auto = True
    op("vector", lambda e: e.reduce_sum(g["col"][:, 0:1], src_ap, axis=AX),
       waits=waits0, inc=ve)
    na = op("scalar", lambda e: e.activation(out=sq, in_=src_ap, func=AF.Square),
            waits=waits0, res=("UPln",), inc=ae)
    nv = op("vector", lambda e: e.reduce_sum(g["col"][:, 1:2], sq, axis=AX),
            waits=[(ae, na)], inc=ve)
    p.set_tok("UPln", ve, nv)
    op("vector", lambda e: e.tensor_scalar_mul(g["col"][:, 2:3], g["col"][:, 0:1], 1.0 / D), inc=ve)
    op("vector", lambda e: e.tensor_scalar_mul(g["col"][:, 3:4], g["col"][:, 1:2], 1.0 / D), inc=ve)
    op("vector", lambda e: e.tensor_mul(g["col"][:, 4:5], g["col"][:, 2:3], g["col"][:, 2:3]), inc=ve)
    nv = op("vector", lambda e: e.tensor_sub(g["col"][:, 5:6], g["col"][:, 3:4], g["col"][:, 4:5]), inc=ve)
    na = op("scalar", lambda e: e.activation(out=g["col"][:, 6:7], in_=g["col"][:, 5:6],
                                             func=AF.Sqrt, bias=g["fconst"][:, 0:1]),
            waits=[(ve, nv)], res=("fc",), inc=ae)
    op("vector", lambda e: e.reciprocal(g["col"][:, 7:8], g["col"][:, 6:7]),
       waits=[(ae, na)], inc=ve)
    nv = op("vector", lambda e: e.tensor_scalar_mul(g["col"][:, 9:10], g["col"][:, 2:3], -1.0), inc=ve)
    nv = op("vector", lambda e: e.tensor_scalar(
        g["xnf"][:], src_ap, g["col"][:, 9:10], g["col"][:, 7:8], ALU.add, ALU.mult),
        res=("xnf",), inc=ve)
    if bf_dests is None:
        na = op("scalar", lambda e: e.activation(out=g["xnb"][:], in_=g["xnf"][:], func=AF.Copy),
                waits=[(ve, nv)], res=("xnb",), inc=ae)
    else:
        for dst, lo, hi, r in bf_dests[:-1]:
            op("scalar", lambda e, dst=dst, lo=lo, hi=hi: e.activation(
                out=dst, in_=g["xnf"][:, lo:hi], func=AF.Copy),
                waits=[(ve, nv)], res=r, inc=ae)
        dst, lo, hi, r = bf_dests[-1]
        na = op("scalar", lambda e, dst=dst, lo=lo, hi=hi: e.activation(
            out=dst, in_=g["xnf"][:, lo:hi], func=AF.Copy),
            waits=[(ve, nv)], res=r, inc=ae)
    p.auto = False
    return nv, na


WSLOTS2 = (("wstage", 0, "sync", "wst0", ()), ("wstage", 704, "gpsimd", "wst1", ()))
WSLOTS5 = WSLOTS2 + (("a_sb", 0, "sync", "wstA", ("a0", "a1")),
                     ("xdb_sb", 0, "gpsimd", "wstX", ("x0", "x1")),
                     ("h_sb", 0, "sync", "wstH", ("h0", "h1")))


def cast_w(p, g, src_chunks, dst_fn, tag, extra_res=(), slots=WSLOTS2):
    nh = 0
    for j, (src_ap, w) in enumerate(src_chunks):
        for h in range(2):
            lo, hi = h * (w // 2), (w if h else w // 2)
            si = nh % len(slots)
            tn, off, q, tok, own = slots[si]
            p.dma(q, g[tn][:, off:off + hi - lo], src_ap[:, lo:hi],
                  res=(tok,) + own, inc="wls%d" % si)
            nl = p.cnt["wls%d" % si]
            p.op("vector", lambda e, j=j, w=w, lo=lo, hi=hi, tn=tn, off=off: e.tensor_copy(
                dst_fn(j, w)[:, lo:hi], g[tn][:, off:off + hi - lo]),
                waits=[("wls%d" % si, nl)],
                res=extra_res if nh == 0 else (),
                inc=tag + "c", rel=(tok,))
            nh += 1
    return p.cnt[tag + "c"]



def emit_p1(nc, p, g):
    op, dma = p.op, p.dma
    ve, ae, te = "p1v", "p1a", "p1t"

    for nm_s, nm_d in [("ident_bf", "identb"), ("ident_f", "identf"),
                       ("bcmask16", "bc16"), ("repmask", "repm"), ("nsmask", "nsm"),
                       ("iota16_bc", "iotab"), ("tok1", "tok1_sb"),
                       ("trash_col", "trash_sb"), ("mask_lt8", "mlt8"),
                       ("ones_row", "onesr"), ("ones_col16", "ones16c"),
                       ("ygidx", "ygf")]:
        dma("sync", g[nm_d][:], g[nm_s][:, :], inc="cld")
    dma("sync", g["bias_sb"][:, 0:8], g["bz"].rearrange("(k pp) -> pp k", pp=128),
        inc="cld", allow_slow_non_contiguous=True)
    dma("sync", g["bias_sb"][:, 8:9], g["bx_sl"][:, None], inc="cld")
    dma("sync", g["bias_sb"][:, 9:10], g["bdt_sl"][:, None], inc="cld")
    dma("sync", g["bb_sb"][:, 0:1], g["bB"][:, None], inc="cld")
    dma("sync", g["bb_sb"][:, 1:2], g["bC"][:, None], inc="cld")
    dma("sync", g["ohtok"][:, 48:64], g["parb"][:, :], inc="cld")
    dma("sync", g["acols"][:],
        g["A_log_sl"].rearrange("(gg ds) n -> (ds n) gg", gg=16),
        inc="cld", allow_slow_non_contiguous=True)
    p.NCLD = p.cnt["cld"]
    op("vector", lambda e: e.memset(g["fconst"][:, 0:1], float(EPS)), inc="fcs")
    op("vector", lambda e: e.memset(g["fconst"][:, 1:2], 1.0), inc="fcs")
    p.set_tok("fc", "fcs", 2)
    op("scalar", lambda e: e.activation(out=g["acols"][:], in_=g["acols"][:], func=AF.Exp),
       waits=[("cld", p.NCLD)], inc="acol")
    op("vector", lambda e: e.tensor_scalar_mul(g["acols"][:], g["acols"][:], -1.0),
       waits=[("acol", 1)], inc="acol")
    p.set_tok("acolv", "acol", 2)

    p.NWZ = cast_w(p, g, [(g["Wz"][k * 128:(k + 1) * 128, :], D) for k in range(KD)],
                   lambda j, w: g["wbig2"][:, j * D:j * D + w], "wz")
    for k in range(KD):
        for jj, nm in enumerate(["WB", "WC", "router_w"]):
            dma("sync", g["wstage"][:, jj * N:(jj + 1) * N],
                g[nm][k * 128:(k + 1) * 128, :],
                res=("wst0",) if jj == 0 else (), inc="wsml")
        nl = p.cnt["wsml"]
        op("vector", lambda e, k=k: e.tensor_copy(
            g["wsmall"][:, k * 3 * N:(k + 1) * 3 * N],
            g["wstage"][:, 0:3 * N]),
            waits=[("wsml", nl)], inc="wsmc", rel=("wst0",))
    p.NWSM = p.cnt["wsmc"]
    for k in range(KD):
        dma("sync", g["wstage"][:, 0:DSL],
            g["Wdt_sl"][k * 128:(k + 1) * 128, :], res=("wst0",), inc="wsll")
        dma("sync", g["wstage"][:, DSL:2 * DSL],
            g["Wx_sl"][k * 128:(k + 1) * 128, :], inc="wsll")
        nl = p.cnt["wsll"]
        op("vector", lambda e, k=k: e.tensor_copy(
            g["wsl"][:, k * 2 * DSL:(k + 1) * 2 * DSL],
            g["wstage"][:, 0:2 * DSL]),
            waits=[("wsll", nl)], inc="wslc", rel=("wst0",))
    p.NWSL = p.cnt["wslc"]

    dma("sync", g["col"][0:1, 16:32], g["router_b"][None, :], inc="rbl")
    nt = op("tensor", lambda e: e.matmul(g["P_y"][:, 0:16], g["onesr"][:, :],
                                         g["col"][0:1, 16:32], start=True, stop=True),
            waits=[("rbl", 16), ("cld", p.NCLD)], res=("PY",), inc=te)
    op("vector", lambda e: e.tensor_copy(g["rb_bc"][:], g["P_y"][:, 0:16]),
       waits=[(te, nt)], inc="rbb", rel=("PY",))

    op("vector", lambda e: e.memset(g["ones16t"][:], 1.0), inc="ones16")
    op("vector", lambda e: e.memset(g["UP"][:, 4096:4417], 0.0), inc="zset")
    dma("sync", g["ar3_in"].rearrange("(pp k) one -> pp (k one)", pp=128),
        g["UP"][:, 4096:4417], inc="ar3z", waits=[("zset", 1)])
    op("vector", lambda e: e.memset(g["bmc_st"][0:1, 0:512], 0.0), inc="zset")
    for lo, hi in ((16, 528), (528, 1040), (1040, 1552), (1552, 2048)):
        dma("sync", g["ag2_in"][32:33, lo:hi], g["bmc_st"][0:1, 0:hi - lo],
            inc="ag2w", waits=[("zset", 2)])
    for lo in (0, 512, 1024, 1536):
        dma("sync", g["ag2_in"][33:34, lo:lo + 512], g["bmc_st"][0:1, 0:512],
            inc="ag2w", waits=[("zset", 2)])
    p.set_tok("bmst0", "ag2w", p.cnt["ag2w"])
    p.set_tok("bmst1", "ag2w", p.cnt["ag2w"])

    def p1_stageA(i):
        sl = i % 2
        tsl = i % 2
        xtv = g["xt"][:, sl * D:(sl + 1) * D]
        nx = 16 * (i // 2 + 1)
        if i + 1 < NTIL:
            sl2 = (i + 1) % 2
            dma("sync", g["xt"][:, sl2 * D:(sl2 + 1) * D],
                g["x_own"][(i + 1) * 128:(i + 2) * 128, :],
                res=("xt%d" % sl2,), inc="xld%d" % sl2)
        if i % 2 == 0:
            bfd = [(g["xnb"][:], 0, D, ("xnb",))]
            tokD = tokV = "xnb"
        else:
            bfd = [(g["dtb"][:], 0, 512, ("dtb",)), (g["vb"][:], 512, D, ("vb",))]
            tokD, tokV = "dtb", "vb"
        nv, na = emit_ln(p, g, xtv, ve, ae, waits0=[("xld%d" % sl, nx)], bf_dests=bfd)
        p.set_tok("xt%d" % sl, ve, nv)
        dma("sync", g["xn_dram"][i * 128:(i + 1) * 128, :], g["xnf"][:],
            inc="xnst", waits=[(ve, nv)])
        p.set_tok("xnf", "xnst", p.cnt["xnst"])

        for k in range(KD):
            w = [(ae, na), ("cld", p.NCLD)]
            if i % 2 == 0:
                src_t = g["xnb"][:, k * 128:(k + 1) * 128]
            else:
                src_t = (g["dtb"][:, k * 128:(k + 1) * 128] if k < 4
                         else g["vb"][:, (k - 4) * 128:(k - 3) * 128])
            if k == 0:
                tokw = list(p.tok.get("xnTb", ()))
            nt = op("tensor", lambda e, src_t=src_t: e.transpose(
                g["P_tr"][:, :], src_t, g["identb"][:, :]),
                waits=w, res=("PTR",) + (("xnTb",) if k == 0 else ()), inc=te)
            op("vector", lambda e, k=k: e.tensor_copy(
                g["xnT_tile"][:, k * 128:(k + 1) * 128],
                g["P_tr"][:, :]), waits=[(te, nt)] + (tokw if k == 0 else []),
                inc="trc", rel=("PTR",))
        ntr = p.cnt["trc"]
        p.set_tok(tokD, "trc", ntr)
        p.set_tok(tokV, "trc", ntr)

        for half, nm in ((0, "ag1a_in"), (1, "ag1b_in")):
            src = g["xnT_tile"].rearrange("pp (k c) -> pp k c", k=KD)[
                :, half * 4:(half + 1) * 4, :]
            dst = g[nm].rearrange("(k pp) t -> pp k t", pp=128)[:, :, i * 128:(i + 1) * 128]
            dma("sync", dst, src, inc="ag1w", waits=[("trc", ntr)])
        nag1 = p.cnt["ag1w"]

        lps = "P_y" if i % 2 == 0 else "P_bc1"
        lpr = "PY" if i % 2 == 0 else "PB1"
        for k in range(KD):
            r = (lpr,) if k == 0 else ()
            op("tensor", lambda e, k=k, tsl=tsl, lps=lps: e.matmul(
                g[lps][:, 0:16],
                g["xnT_tile"][:, k * 128:(k + 1) * 128],
                g["wsmall"][:, k * 3 * N + 2 * N:k * 3 * N + 3 * N],
                start=(k == 0), stop=(k == KD - 1)),
                waits=[("trc", ntr), ("wsmc", p.NWSM)], res=r, inc=te)
        nt_lg = p.cnt[te]

        for jj, woff in ((0, 0), (1, N)):
            for k in range(KD):
                r = ("PA0",) if k == 0 else ()
                op("tensor", lambda e, k=k, woff=woff, tsl=tsl: e.matmul(
                    g["P_acc0"][0:16, 0:128],
                    g["wsmall"][:, k * 3 * N + woff:k * 3 * N + woff + N],
                    g["xnT_tile"][:, k * 128:(k + 1) * 128],
                    start=(k == 0), stop=(k == KD - 1)),
                    waits=[("trc", ntr), ("wsmc", p.NWSM)], res=r, inc=te)
            nt = p.cnt[te]
            na = op("vector", lambda e, jj=jj: e.tensor_single_scalar(
                g["bmc_st"][0:16, jj * 128:(jj + 1) * 128], g["P_acc0"][0:16, 0:128],
                g["bb_sb"][:, jj:jj + 1], ALU.add),
                waits=[(te, nt)], res=("bmst%d" % jj,), inc=ve)
            p.set_tok("PA0", ve, na)
            dma("sync", g["ag2_in"][jj * 16:(jj + 1) * 16, i * 128:(i + 1) * 128],
                g["bmc_st"][0:16, jj * 128:(jj + 1) * 128], inc="ag2w", waits=[(ve, na)])
        p.set_tok("bmst0", "ag2w", p.cnt["ag2w"])
        p.set_tok("bmst1", "ag2w", p.cnt["ag2w"])
        p.tok["xnTb"] = [(te, p.cnt[te]), ("ag1w", nag1)]
        return {"i": i, "tsl": tsl, "ntr": ntr, "nag1": nag1, "nt": nt_lg,
                "lps": lps, "lpr": lpr}

    def p1_stageB(st):
        i, tsl, ntr, nag1, nt = st["i"], st["tsl"], st["ntr"], st["nag1"], st["nt"]
        lps, lpr = st["lps"], st["lpr"]
        p.auto = True
        nv = op("vector", lambda e, lps=lps: e.tensor_add(
            g["col"][:, 10:26], g[lps][:, 0:16], g["rb_bc"][:]),
            waits=[(te, nt), ("rbb", 1)], inc=ve)
        p.set_tok(lpr, ve, nv)
        lg = g["col"][:, 10:26]
        sc1 = g["UP"][:, 2186:2202]
        sc2 = g["UP"][:, 2202:2218]
        op("vector", lambda e: e.reduce_max(g["col"][:, 26:27], lg, axis=AX), res=("UPln",), inc=ve)
        op("vector", lambda e: e.tensor_single_scalar(g["ohtok"][:, 0:16], lg, g["col"][:, 26:27], ALU.is_equal), inc=ve)
        op("vector", lambda e: e.scalar_tensor_tensor(sc1, g["ohtok"][:, 0:16], -BIGF, lg, ALU.mult, ALU.add), inc=ve)
        op("vector", lambda e: e.reduce_max(g["col"][:, 27:28], sc1, axis=AX), inc=ve)
        op("vector", lambda e: e.tensor_mul(sc2, g["ohtok"][:, 0:16], g["iotab"][:]), inc=ve)
        op("vector", lambda e, i=i: e.reduce_max(g["packA"][:, 2 * i:2 * i + 1], sc2, axis=AX), inc=ve)
        op("vector", lambda e: e.tensor_single_scalar(g["ohtok"][:, 16:32], sc1, g["col"][:, 27:28], ALU.is_equal), inc=ve)
        op("vector", lambda e: e.tensor_mul(sc2, g["ohtok"][:, 16:32], g["iotab"][:]), inc=ve)
        op("vector", lambda e, i=i: e.reduce_max(g["packA"][:, 2 * i + 1:2 * i + 2], sc2, axis=AX), inc=ve)
        nv = op("vector", lambda e: e.tensor_scalar_mul(g["col"][:, 28:29], g["col"][:, 26:27], -1.0), inc=ve)
        na = op("scalar", lambda e: e.activation(out=sc1, in_=lg, func=AF.Exp, bias=g["col"][:, 28:29]),
                waits=[(ve, nv)], inc=ae)
        op("vector", lambda e: e.reduce_sum(g["col"][:, 29:30], sc1, axis=AX),
           waits=[(ae, na)], inc=ve)
        na = op("scalar", lambda e: e.activation(out=g["col"][:, 30:31], in_=g["col"][:, 27:28],
                                                 func=AF.Exp, bias=g["col"][:, 28:29]), inc=ae)
        op("vector", lambda e: e.reciprocal(g["col"][:, 31:32], g["col"][:, 29:30]),
           waits=[(ae, na)], inc=ve)
        op("vector", lambda e: e.tensor_mul(g["col"][:, 32:33], g["col"][:, 30:31], g["col"][:, 31:32]), inc=ve)
        op("vector", lambda e: e.tensor_add(g["col"][:, 33:34], g["col"][:, 31:32], g["col"][:, 32:33]), inc=ve)
        op("vector", lambda e: e.tensor_scalar_add(g["col"][:, 33:34], g["col"][:, 33:34], 1e-9), inc=ve)
        op("vector", lambda e: e.reciprocal(g["col"][:, 34:35], g["col"][:, 33:34]), inc=ve)
        op("vector", lambda e, i=i: e.tensor_mul(g["packB"][:, 2 * i:2 * i + 1], g["col"][:, 31:32], g["col"][:, 34:35]), inc=ve)
        op("vector", lambda e, i=i: e.tensor_mul(g["packB"][:, 2 * i + 1:2 * i + 2], g["col"][:, 32:33], g["col"][:, 34:35]), inc=ve)

        op("vector", lambda e, i=i: e.tensor_single_scalar(
            g["ohtok"][:, 0:16], g["iotab"][:], g["packA"][:, 2 * i:2 * i + 1], ALU.is_equal), inc=ve)
        nv = op("vector", lambda e, i=i: e.tensor_single_scalar(
            g["ohtok"][:, 32:48], g["iotab"][:], g["packA"][:, 2 * i + 1:2 * i + 2], ALU.is_equal), inc=ve)
        nt = op("tensor", lambda e: e.transpose(
            g["P_bc0"][0:64, 0:128], g["ohtok"][:, 0:64], g["identf"][:, :]),
            waits=[(ve, nv), ("cld", p.NCLD)], res=("PB0",), inc=te)
        ti = i % 2
        ohv = g["oh_c"].rearrange("ee (tt pp kk) -> ee tt pp kk", tt=2, kk=2)
        op("vector", lambda e, ti=ti: e.tensor_copy(ohv[:, ti, :, 0], g["P_bc0"][0:16, 0:128]),
           waits=[(te, nt)], res=("ohc",) if ti == 0 else (), inc=ve)
        nv = op("vector", lambda e, ti=ti: e.tensor_copy(ohv[:, ti, :, 1], g["P_bc0"][32:48, 0:128]),
                waits=[(te, nt)], inc=ve)
        p.set_tok("PB0", ve, nv)
        p.auto = False

        if i % 2 == 1:
            emit_p1_chunk(nc, p, g, i // 2)

    dma("sync", g["xt"][:, 0:D], g["x_own"][0:128, :], res=("xt0",), inc="xld0")
    st = p1_stageA(0)
    for i in range(NTIL):
        st_next = p1_stageA(i + 1) if i + 1 < NTIL else None
        p1_stageB(st)
        st = st_next

    emit_p1_post(nc, p, g)


def emit_p1_chunk(nc, p, g, ch):
    op = p.op
    ve, ae, te = "p1v", "p1a", "p1t"
    p.auto = True
    nv_oh = p.cnt[ve]
    init = 0.0 if ch == 0 else g["carry16"][:, 0:1]
    nv = op("vector", lambda e, init=init: e.tensor_tensor_scan(
        g["cum_c"][:], g["ones16t"][:], g["oh_c"][:], init, ALU.mult, ALU.add),
        waits=[("ones16", 1), (ve, nv_oh)], inc=ve)
    op("vector", lambda e: e.tensor_copy(g["carry16"][:, 0:1], g["cum_c"][:, TCH - 1:TCH]), inc=ve)
    nv = op("vector", lambda e: e.tensor_mul(g["oh_c"][:], g["cum_c"][:], g["oh_c"][:]), inc=ve)
    nt = op("tensor", lambda e: e.matmul(g["P_bc1"][0:1, 0:512], g["ones16c"][:, :],
                                         g["oh_c"][:], start=True, stop=True),
            waits=[(ve, nv), ("cld", p.NCLD)], res=("PB1",), inc=te)
    p.set_tok("ohc", te, nt)
    lr = g["P_bc1"][0:1, 0:512].rearrange("one (tt pp kk) -> one tt pp kk", tt=2, kk=2)
    kr = g["krow"].rearrange("kk (tt pp) -> kk tt pp", tt=2)
    op("vector", lambda e, lr=lr, kr=kr: e.tensor_copy(kr[0:1, :, :], lr[:, :, :, 0]),
       waits=[(te, nt)], inc=ve)
    nv = op("vector", lambda e, lr=lr, kr=kr: e.tensor_copy(kr[32:33, :, :], lr[:, :, :, 1]), inc=ve)
    p.set_tok("PB1", ve, nv)
    for ti in range(2):
        tile = ch * 2 + ti
        nt = op("tensor", lambda e, ti=ti: e.transpose(
            g["P_bc1"][0:128, 100:133], g["krow"][0:33, ti * 128:(ti + 1) * 128],
            g["identf"][0:33, 0:33]), waits=[(ve, nv)], res=("PB1",), inc=te)
        op("vector", lambda e, tile=tile: e.tensor_copy(
            g["lcols"][:, tile * 2:tile * 2 + 1], g["P_bc1"][0:128, 100:101]),
            waits=[(te, nt)], inc=ve)
        nv2 = op("vector", lambda e, tile=tile: e.tensor_copy(
            g["lcols"][:, tile * 2 + 1:tile * 2 + 2], g["P_bc1"][0:128, 132:133]),
            waits=[(te, nt)], inc=ve)
        p.set_tok("PB1", ve, nv2)
    p.auto = False
    if True:
        pass


def emit_p1_post(nc, p, g):
    op, dma = p.op, p.dma
    ve, ae, te = "p1v", "p1a", "p1t"

    nv = p.cnt[ve]
    dma("sync", g["ag2_in"][32:33, 0:16], g["carry16"][:, 0:1], inc="ag2c",
        waits=[(ve, nv)])

    op("gpsimd", lambda e: e.collective_compute(
        "AllGather", ALU.bypass, replica_groups=[list(range(NCORE))],
        ins=[g["ag1a_in"].tensor.ap().opt()], outs=[g["ag1a_out"].tensor.ap().opt()]),
        waits=[("ag1w", p.cnt["ag1w"])], inc="cc")
    op("gpsimd", lambda e: e.collective_compute(
        "AllGather", ALU.bypass, replica_groups=[list(range(NCORE))],
        ins=[g["ag1b_in"].tensor.ap().opt()], outs=[g["ag1b_out"].tensor.ap().opt()]),
        inc="cc")
    op("gpsimd", lambda e: e.collective_compute(
        "AllGather", ALU.bypass, replica_groups=[list(range(NCORE))],
        ins=[g["ag2_in"].tensor.ap().opt()], outs=[g["ag2_out"].tensor.ap().opt()]),
        waits=[("ag2w", p.cnt["ag2w"]), ("ag2c", 16)], inc="cc")

    p.auto = True
    dma("sync", g["tot8"][:],
        g["ag2_out"].rearrange("(blk r) t -> blk (r t)", r=34)[:, 32 * TPC:32 * TPC + 16],
        inc="t8l", waits=[("cc", 3)])
    nt = op("tensor", lambda e: e.matmul(g["P_bc1"][0:16, 0:1], g["tot8"][:, :],
                                         g["mlt8"][:, :], start=True, stop=True),
            waits=[("t8l", 16)], res=("PB1",), inc=te)
    na = op("scalar", lambda e: e.activation(out=g["off16"][:], in_=g["P_bc1"][0:16, 0:1],
                                             func=AF.Copy), waits=[(te, nt)], inc=ae)
    nt = op("tensor", lambda e: e.transpose(g["P_bc1"][0:1, 4:20], g["off16"][:, :],
                                            g["identf"][0:16, 0:16]),
            waits=[(ae, na)], inc=te)
    na = op("scalar", lambda e: e.activation(out=g["col"][0:1, 0:16], in_=g["P_bc1"][0:1, 4:20],
                                             func=AF.Copy), waits=[(te, nt)], inc=ae)
    nt = op("tensor", lambda e: e.matmul(g["P_bc1"][:, 4:20], g["onesr"][:, :],
                                         g["col"][0:1, 0:16], start=True, stop=True),
            waits=[(ae, na)], inc=te)
    nv = op("vector", lambda e: e.tensor_copy(g["off_bc"][:], g["P_bc1"][:, 4:20]),
            waits=[(te, nt)], inc=ve)
    p.set_tok("PB1", ve, nv)

    for sx in range(2 * NTIL):
        op("vector", lambda e, sx=sx: e.tensor_single_scalar(
            g["ohtok"][:, 0:16], g["iotab"][:], g["packA"][:, sx:sx + 1], ALU.is_equal), inc=ve)
        op("vector", lambda e: e.tensor_mul(g["UP"][:, 128:144], g["ohtok"][:, 0:16], g["off_bc"][:]), inc=ve)
        op("vector", lambda e, sx=sx: e.reduce_sum(g["UP"][:, sx:sx + 1], g["UP"][:, 128:144], axis=AX), inc=ve)
        op("vector", lambda e: e.tensor_mul(g["UP"][:, 144:160], g["ohtok"][:, 0:16],
                                            g["ohtok"][:, 48:64]), inc=ve)
        op("vector", lambda e, sx=sx: e.reduce_sum(g["UP"][:, 32 + sx:32 + sx + 1],
                                                   g["UP"][:, 144:160], axis=AX), inc=ve)
    NS2 = 2 * NTIL
    offc = g["UP"][:, 0:NS2]
    parc = g["UP"][:, 32:32 + NS2]
    t1 = g["UP"][:, 64:64 + NS2]
    t2 = g["UP"][:, 160:160 + NS2]
    t3 = g["UP"][:, 192:192 + NS2]
    vld = g["UP"][:, 224:224 + NS2]
    op("vector", lambda e, offc=offc: e.tensor_add(g["pcols"][:], g["lcols"][:], offc), inc=ve)
    op("vector", lambda e, vld=vld: e.tensor_single_scalar(
        vld, g["pcols"][:], float(CAP), ALU.is_le), inc=ve)
    op("vector", lambda e, vld=vld: e.tensor_mul(g["rcols"][:], g["packB"][:], vld), inc=ve)
    op("vector", lambda e, parc=parc: e.tensor_mul(g["rcols2"][:], g["rcols"][:], parc), inc=ve)
    op("vector", lambda e, t1=t1, parc=parc: e.tensor_scalar(
        t1, parc, -1.0, 1.0, ALU.mult, ALU.add), inc=ve)
    op("vector", lambda e, t1=t1: e.tensor_mul(g["rcols"][:], g["rcols"][:], t1), inc=ve)
    op("vector", lambda e, t1=t1, parc=parc: e.tensor_sub(t1, g["packA"][:], parc), inc=ve)
    op("vector", lambda e, t1=t1: e.tensor_scalar_mul(t1, t1, float(CAP // 2)), inc=ve)
    op("vector", lambda e, t1=t1: e.tensor_add(t1, t1, g["pcols"][:]), inc=ve)
    op("vector", lambda e, t1=t1: e.tensor_scalar_add(t1, t1, -1.0), inc=ve)
    op("vector", lambda e, t1=t1: e.tensor_scalar_min(t1, t1, float(NCORE * CAP - 1)), inc=ve)
    op("vector", lambda e, t1=t1: e.tensor_copy(g["ucols"][:, 0:NS2], t1), inc=ve)
    op("vector", lambda e, t2=t2: e.tensor_scalar_mul(t2, g["packA"][:], float(CAP)), inc=ve)
    op("vector", lambda e, t2=t2: e.tensor_add(t2, t2, g["pcols"][:]), inc=ve)
    op("vector", lambda e, t2=t2: e.tensor_scalar_add(t2, t2, -1.0), inc=ve)
    op("vector", lambda e, t2=t2, vld=vld: e.tensor_mul(t2, t2, vld), inc=ve)
    op("vector", lambda e, t3=t3, vld=vld: e.tensor_scalar(
        t3, vld, -1.0, 1.0, ALU.mult, ALU.add), inc=ve)
    op("vector", lambda e, t3=t3: e.tensor_single_scalar(
        t3, t3, g["trash_sb"][:, 0:1], ALU.mult), inc=ve)
    op("vector", lambda e, t2=t2, t3=t3: e.tensor_add(t2, t2, t3), inc=ve)
    op("vector", lambda e, t2=t2: e.tensor_copy(g["ucols"][:, NS2:2 * NS2], t2), inc=ve)

    p.auto = False
    NV = p.cnt[ve]
    for i in range(NTIL):
        for kk in range(2):
            sx = 2 * i + kk
            op("gpsimd", lambda e, i=i, sx=sx: e.indirect_dma_start(
                out=g["ar3_in"][:, :],
                out_offset=IndirectOffsetOnAxis(ap=g["ucols"][:, 32 + sx:32 + sx + 1], axis=0),
                in_=g["tok1_sb"][:, i:i + 1], in_offset=None,
                bounds_check=p.breg(e, TBL - 1), oob_is_err=False),
                waits=[("p1v", NV), ("ar3z", 16), ("sct", 16 * sx)], inc="sct", amt=16)
    op("gpsimd", lambda e: e.collective_compute(
        "AllReduce", ALU.add, replica_groups=[list(range(NCORE))],
        ins=[g["ar3_in"].tensor.ap().opt()], outs=[g["ar3_out"].tensor.ap().opt()]),
        waits=[("sct", p.cnt["sct"])], inc="cc")



def emit_p2(nc, p, g):
    op, dma = p.op, p.dma
    ve, ae, te, pl = "p2v", "p2a", "p2t", "p2p"

    def p2_loads(it):
        blk, tw = it // 4, it % 4
        sl = it % 2
        for k in range(KD):
            src = g["ag1a_out"] if k < 4 else g["ag1b_out"]
            kk = k % 4
            w = [("cc", 2)] if it < 2 else []
            dma("sync", g["xnTc"][:, (sl * KD + k) * TCH:(sl * KD + k + 1) * TCH],
                src[blk * 512 + kk * 128:blk * 512 + (kk + 1) * 128, tw * TCH:(tw + 1) * TCH],
                waits=w, res=("xTc%d" % sl,) if k == 0 else (), inc="xTl%d" % sl)
        dma("sync", g["bmc_st"][0:16, 0:TCH],
            g["ag2_out"][blk * 34:blk * 34 + 16, tw * TCH:(tw + 1) * TCH],
            waits=[("cc", 3)], res=("bmst0", "bmst1"), inc="bmLl")

    def emit_head_pe(it):
        sl = it % 2
        nxl = 128 * (it // 2 + 1)
        for k in range(KD):
            r = ("PA0",) if k == 0 else ()
            op("tensor", lambda e, k=k, sl=sl: e.matmul(
                g["P_acc0"][:, :], g["wsl"][:, k * 2 * DSL:k * 2 * DSL + DSL],
                g["xnTc"][:, (sl * KD + k) * TCH:(sl * KD + k + 1) * TCH],
                start=(k == 0), stop=(k == KD - 1)),
                waits=[("xTl%d" % sl, nxl), ("wslc", p.NWSL)] if k == 0 else (), res=r, inc=te)
        n_dt = p.cnt[te]
        for k in range(KD):
            r = ("PA1",) if k == 0 else ()
            op("tensor", lambda e, k=k, sl=sl: e.matmul(
                g["P_acc1"][:, :], g["wsl"][:, k * 2 * DSL + DSL:(k + 1) * 2 * DSL],
                g["xnTc"][:, (sl * KD + k) * TCH:(sl * KD + k + 1) * TCH],
                start=(k == 0), stop=(k == KD - 1)),
                waits=[("xTl%d" % sl, nxl)] if k == 0 else (), res=r, inc=te)
        n_xs = p.cnt[te]
        p.set_tok("xTc%d" % sl, te, n_xs)
        return n_dt, n_xs

    def emit_head_act(it, n_dt, n_xs, w0=()):
        if it % 2 == 0:
            dtb_ap, vb_ap, tokD, tokV = g["dtb"][:], g["vb"][:], "dtb", "vb"
        else:
            dtb_ap, vb_ap = g["m3"][:, 0:512], g["m3"][:, 512:1024]
            tokD, tokV = "m3a0", "m3b0"
        u1 = g["UP"][:, 1024:1536]
        u2 = g["UP"][:, 1536:2048]
        nab = op("scalar", lambda e: e.activation(out=u1, in_=g["P_acc0"][:, :], func=AF.Abs,
                                                  bias=g["bias_sb"][:, 9:10]),
                 waits=[(te, n_dt)] + list(w0), res=("UPsp",), inc=ae)
        nab = op("scalar", lambda e: e.activation(out=u2, in_=u1, func=AF.Exp, scale=-1.0),
                 waits=[(ae, nab)], inc=ae)
        nab = op("scalar", lambda e: e.activation(out=u1, in_=u2, func=AF.Ln,
                                                  bias=g["fconst"][:, 1:2]),
                 waits=[(ae, nab)], res=("fc",), inc=ae)
        na = op("scalar", lambda e: e.activation(out=g["dtf"][:], in_=g["P_acc0"][:, :],
                                                 func=AF.Relu, bias=g["bias_sb"][:, 9:10]),
                waits=[(ae, nab)], res=("dtf",), inc=ae)
        p.set_tok("PA0", ae, na)
        nv = op("vector", lambda e: e.tensor_add(g["dtf"][:], g["dtf"][:], u1),
                waits=[(ae, na)], inc=ve)
        p.set_tok("UPsp", ve, nv)
        nv = op("vector", lambda e: e.scalar_tensor_tensor(
            g["vf"][:], g["P_acc1"][:, :], g["bias_sb"][:, 8:9], g["dtf"][:], ALU.add, ALU.mult),
            waits=[(te, n_xs), (ve, nv)], res=("vf",), inc=ve)
        p.set_tok("PA1", ve, nv)
        na_dt = op("scalar", lambda e, dtb_ap=dtb_ap: e.activation(
            out=dtb_ap, in_=g["dtf"][:], func=AF.Copy),
            waits=[(ve, nv - 1)], res=(tokD,), inc=ae)
        p.set_tok("dtf", ae, na_dt)
        na_v = op("scalar", lambda e, vb_ap=vb_ap: e.activation(
            out=vb_ap, in_=g["vf"][:], func=AF.Copy),
            waits=[(ve, nv)], res=(tokV,), inc=ae)
        p.set_tok("vf", ae, na_v)
        return {"na_dt": na_dt, "na_v": na_v, "dtb_ap": dtb_ap, "vb_ap": vb_ap,
                "tokD": tokD, "tokV": tokV}

    p2_loads(0)
    w0 = [("p1v", p.cnt["p1v"]), ("p1a", p.cnt["p1a"]), ("p1t", p.cnt["p1t"]),
          ("p1p", p.cnt.get("p1p", 0))]
    hs = emit_head_act(0, *emit_head_pe(0), w0=w0)
    for it in range(NIT):
        blk, tw = it // 4, it % 4
        first_of_b = (it % 16) == 0
        sl = it % 2
        nbm = 16 * (it + 1)
        na_dt, na_v = hs["na_dt"], hs["na_v"]
        dtb_ap, vb_ap = hs["dtb_ap"], hs["vb_ap"]

        na_b = op("scalar", lambda e: e.activation(out=g["bmcb"][0:16, 0:TCH],
                                                   in_=g["bmc_st"][0:16, 0:TCH], func=AF.Copy),
                  waits=[("bmLl", nbm)], res=("bmcb",), inc=ae)
        nt = op("tensor", lambda e: e.matmul(g["P_bc0"][:, :], g["bc16"][:, :],
                                             g["bmcb"][0:16, 0:TCH], start=True, stop=True),
                waits=[(ae, na_b), ("cld", p.NCLD)], res=("PB0",), inc=te)
        nv1 = op("vector", lambda e: e.tensor_copy(g["bm_bc"][:], g["P_bc0"][:, :]),
                 waits=[(te, nt)], res=("bmbc",), inc=ve)
        p.set_tok("PB0", ve, nv1)
        dma("sync", g["bmc_st"][0:16, 0:TCH],
            g["ag2_out"][blk * 34 + 16:blk * 34 + 32, tw * TCH:(tw + 1) * TCH],
            waits=[("p2a", na_b)], inc="bmCl")
        nbm2 = p.cnt["bmCl"]
        na_c = op("scalar", lambda e: e.activation(out=g["bmcb"][0:16, 0:TCH],
                                                   in_=g["bmc_st"][0:16, 0:TCH], func=AF.Copy),
                  waits=[("bmCl", nbm2), (te, nt)], inc=ae)
        p.set_tok("bmst0", ae, na_c)
        p.set_tok("bmst1", ae, na_c)
        p.set_tok("bmcb", ae, na_c)
        nt = op("tensor", lambda e: e.matmul(g["P_bc0"][:, :], g["bc16"][:, :],
                                             g["bmcb"][0:16, 0:TCH], start=True, stop=True),
                waits=[(ae, na_c)], res=("PB0",), inc=te)
        nv2 = op("vector", lambda e: e.tensor_copy(g["c_bc"][:], g["P_bc0"][:, :]),
                 waits=[(te, nt)], res=("cbc",), inc=ve)
        p.set_tok("PB0", ve, nv2)
        p.also_rel("bmcb", "p2t", nt)
        if it + 1 < NIT:
            p2_loads(it + 1)

        order = [s4 * 4 + q for q in range(4) for s4 in range(4)]
        last_ym = None
        pend = []

        def emit_yred(gi, gg, np_):
            r = ("PY",) if gi == 0 else ()
            ycv = g["yc_sb"][:, 0:TCH] if gi % 2 == 0 else g["m3"][:, 16384:16384 + TCH]
            nty = op("tensor", lambda e, gg=gg, gi=gi, ycv=ycv: e.matmul(
                g["P_y"][:, :], g["nsm"][:, gg * 128:(gg + 1) * 128], ycv,
                start=(gi == 0), stop=(gi == 15)),
                waits=[(pl, np_)], res=r, inc=te)
            p.set_tok("yc%d" % (gi % 2), te, nty)

        for gi, gg in enumerate(order):
            gp = gi % 2
            av = g["a_sb"][:, gp * TCH:(gp + 1) * TCH]
            xv = g["xdb_sb"][:, gp * TCH:(gp + 1) * TCH]
            hv = g["h_sb"][:, gp * TCH:(gp + 1) * TCH]

            bA = "P_bc1" if gp == 0 else "P_acc3"
            bAr = "PB1" if gp == 0 else "PA3"
            bV = "P_bc0" if gp == 0 else "P_acc2"
            bVr = "PB0" if gp == 0 else "PA2"
            nt_a = op("tensor", lambda e, gg=gg, bA=bA, dtb_ap=dtb_ap: e.matmul(
                g[bA][:, :], g["repm"][:, gg * 128:(gg + 1) * 128], dtb_ap,
                start=True, stop=True), waits=[(ae, na_dt)], res=(bAr,), inc=te)
            na2 = op("scalar", lambda e, gg=gg, av=av, bA=bA: e.activation(
                out=av, in_=g[bA][:, :], func=AF.Exp, scale=g["acols"][:, gg:gg + 1]),
                waits=[(te, nt_a), ("acol", 2)], res=("a%d" % gp,), inc=ae)
            p.set_tok(bAr, ae, na2)

            nt_v = op("tensor", lambda e, gg=gg, bV=bV, vb_ap=vb_ap: e.matmul(
                g[bV][:, :], g["repm"][:, gg * 128:(gg + 1) * 128], vb_ap,
                start=True, stop=True), waits=[(ae, na_v)], res=(bVr,), inc=te)
            nv3 = op("vector", lambda e, xv=xv, bV=bV: e.tensor_mul(xv, g[bV][:, :], g["bm_bc"][:]),
                     waits=[(te, nt_v), (ve, nv1)], res=("x%d" % gp,), inc=ve)
            p.set_tok(bVr, ve, nv3)
            if gi >= 2:
                emit_yred(gi - 2, order[gi - 2], pend[gi - 2])

            init = 0.0 if first_of_b else g["carryg"][:, gg:gg + 1]
            nv4 = op("vector", lambda e, av=av, xv=xv, hv=hv, init=init: e.tensor_tensor_scan(
                hv, av, xv, init, ALU.mult, ALU.add),
                waits=[(ae, na2), (ve, nv3)], res=("h%d" % gp,), inc=ve)
            p.set_tok("a%d" % gp, ve, nv4)
            p.set_tok("x%d" % gp, ve, nv4)
            nv5 = op("vector", lambda e, gg=gg, hv=hv: e.tensor_copy(
                g["carryg"][:, gg:gg + 1], hv[:, TCH - 1:TCH]),
                waits=[(ve, nv4)], inc=ve)

            ycv = g["yc_sb"][:, 0:TCH] if gp == 0 else g["m3"][:, 16384:16384 + TCH]
            np_ = op("gpsimd", lambda e, hv=hv, ycv=ycv: e.tensor_mul(ycv, hv, g["c_bc"][:]),
                     waits=[(ve, nv5), (ve, nv2)], res=("yc%d" % gp,), inc=pl)
            p.set_tok("h%d" % gp, pl, np_)
            pend.append(np_)
            last_ym = np_
            if gi == 9 and it + 1 < NIT:
                ndx_next = emit_head_pe(it + 1)
        if it + 1 < NIT:
            hs_next = emit_head_act(it + 1, *ndx_next)
        emit_yred(14, order[14], pend[14])
        emit_yred(15, order[15], pend[15])
        p.set_tok("cbc", pl, last_ym)
        p.set_tok("bmbc", ve, p.cnt[ve])
        p.set_tok(hs["tokD"], te, p.cnt[te])
        p.set_tok(hs["tokV"], te, p.cnt[te])
        if it + 1 < NIT:
            hs = hs_next
        nt_y = p.cnt[te]
        na = op("scalar", lambda e: e.activation(out=g["yev"][:], in_=g["P_y"][:, :], func=AF.Copy),
                waits=[(te, nt_y)], res=("yev",), inc=ae)
        p.set_tok("PY", ae, na)
        dma("sync",
            g["ag4_in"].rearrange("(dp blk) t -> dp blk t", blk=8)[:, blk, tw * TCH:(tw + 1) * TCH],
            g["yev"][:], inc="ag4w", waits=[(ae, na)])
        p.set_tok("yev", "ag4w", p.cnt["ag4w"])

    op("gpsimd", lambda e: e.collective_compute(
        "AllGather", ALU.bypass, replica_groups=[list(range(NCORE))],
        ins=[g["ag4_in"].tensor.ap().opt()], outs=[g["ag4_out"].tensor.ap().opt()]),
        waits=[("ag4w", p.cnt["ag4w"])], inc="cc")





def mlp_block(p, g, rhs_fn, waits1, waits2, waits3, out_cb, inject=None):
    op = p.op
    ae, te = "p3a", "p3t"
    m3 = g["m3"]
    npl_list = []
    h1o, h2o, mto = 0, FCH * SCH, 2 * FCH * SCH
    for f in range(FCH):
        fb = f % 2
        b1 = "P_acc0" if fb == 0 else "P_acc2"
        b1r = "PA0" if fb == 0 else "PA2"
        b2 = "P_acc1" if fb == 0 else "P_acc3"
        b2r = "PA1" if fb == 0 else "PA3"
        for k in range(KD):
            r = (b1r,) if k == 0 else ()
            op("tensor", lambda e, f=f, k=k, b1=b1: e.matmul(
                g[b1][:, :], g["wbig"][:, k * FF + f * 128:k * FF + (f + 1) * 128],
                rhs_fn(k), start=(k == 0), stop=(k == KD - 1)),
                waits=waits1 if k == 0 else (), res=r, inc=te)
        nt1 = p.cnt[te]
        na1 = op("scalar", lambda e, f=f, b1=b1: e.activation(
            out=m3[:, h1o + f * SCH:h1o + (f + 1) * SCH], in_=g[b1][:, :],
            func=AF.Silu), waits=[(te, nt1)], res=("m3a%d" % f,), inc=ae)
        p.set_tok(b1r, ae, na1)
        for k in range(KD):
            r = (b2r,) if k == 0 else ()
            op("tensor", lambda e, f=f, k=k, b2=b2: e.matmul(
                g[b2][:, :], g["wbig2"][:, k * FF + f * 128:k * FF + (f + 1) * 128],
                rhs_fn(k), start=(k == 0), stop=(k == KD - 1)),
                waits=waits2 if k == 0 else (), res=r, inc=te)
        nt2 = p.cnt[te]
        if f == FCH - 1:
            n_mm1 = nt1
            n_mm2 = nt2
        na2 = op("scalar", lambda e, f=f, b2=b2: e.activation(
            out=m3[:, h2o + f * SCH:h2o + (f + 1) * SCH], in_=g[b2][:, :],
            func=AF.Copy), waits=[(te, nt2)], res=("m3b%d" % f,), inc=ae)
        p.set_tok(b2r, ae, na2)
        npl = op("gpsimd", lambda e, f=f: e.tensor_mul(
            m3[:, mto + f * SCH:mto + (f + 1) * SCH],
            m3[:, h1o + f * SCH:h1o + (f + 1) * SCH],
            m3[:, h2o + f * SCH:h2o + (f + 1) * SCH]),
            waits=[(ae, na1), (ae, na2)], res=("m3m%d" % f,), inc="p3p")
        p.set_tok("m3a%d" % f, "p3p", npl)
        p.set_tok("m3b%d" % f, "p3p", npl)
        npl_list.append(npl)
        if inject is not None:
            inject(f)
    n_mm2 = p.cnt[te]
    for ss in range(4):
        for dh in range(2):
            ob = (ss * 2 + dh) % 2
            b3 = "P_bc0" if ob == 0 else "P_bc1"
            b3r = "PB0" if ob == 0 else "PB1"
            for f in range(FCH):
                w = (list(waits3) if f == 0 else []) + [("p3p", npl_list[f])]
                r = (b3r,) if f == 0 else ()
                op("tensor", lambda e, ss=ss, dh=dh, f=f, b3=b3: e.matmul(
                    g[b3][:, :],
                    m3[:, mto + f * SCH + ss * 128:mto + f * SCH + (ss + 1) * 128],
                    g["wdown"][:, f * D + dh * 512:f * D + (dh + 1) * 512],
                    start=(f == 0), stop=(f == FCH - 1)), waits=w, res=r, inc=te)
            out_cb(ss, dh, p.cnt[te], b3, b3r)
            if inject is not None:
                inject(FCH + ss * 2 + dh)
    for f in range(FCH):
        p.set_tok("m3m%d" % f, te, p.cnt[te])
    return n_mm1, n_mm2, p.cnt[te]


def emit_p3(nc, p, g):
    op, dma = p.op, p.dma
    ve, ae, te, pl = "p3v", "p3a", "p3t", "p3p"
    NAG1 = p.cnt["ag1w"]

    for tc in range(4):
        sl = tc % 2
        for k in range(KD):
            src = g["ag1a_in"] if k < 4 else g["ag1b_in"]
            kk = k % 4
            dma("sync", g["xnTc"][:, (sl * KD + k) * TCH:(sl * KD + k + 1) * TCH],
                src[kk * 128:(kk + 1) * 128, tc * TCH:(tc + 1) * TCH],
                waits=[("ag1w", NAG1)], res=("xTc%d" % sl,) if k == 0 else (),
                inc="zxl%d" % sl)
        nxl = p.cnt["zxl%d" % sl]
        for dc in range(KD):
            dp = dc % 2
            zps = "P_acc0" if dp == 0 else "P_acc1"
            zpr = "PA0" if dp == 0 else "PA1"
            zev = g["yev"][:] if dp == 0 else g["dtb"][:]
            zet = "yev" if dp == 0 else "dtb"
            zct = "zst%d" % dp
            for k in range(KD):
                r = (zpr,) if k == 0 else ()
                op("tensor", lambda e, dc=dc, k=k, sl=sl, zps=zps: e.matmul(
                    g[zps][:, :], g["wbig2"][:, k * D + dc * 128:k * D + (dc + 1) * 128],
                    g["xnTc"][:, (sl * KD + k) * TCH:(sl * KD + k + 1) * TCH],
                    start=(k == 0), stop=(k == KD - 1)),
                    waits=[("zxl%d" % sl, nxl), ("wzc", p.NWZ)], res=r, inc=te)
            nt = p.cnt[te]
            na = op("scalar", lambda e, dc=dc, zev=zev, zps=zps: e.activation(
                out=zev, in_=g[zps][:, :], func=AF.Sigmoid,
                bias=g["bias_sb"][:, dc:dc + 1]),
                waits=[(te, nt)], res=(zet,), inc=ae)
            p.set_tok(zpr, ae, na)
            dma("sync", g["z_dram"][dc * 128:(dc + 1) * 128, tc * TCH:(tc + 1) * TCH],
                zev, inc=zct, waits=[(ae, na)])
            p.set_tok(zet, zct, p.cnt[zct])
        p.set_tok("xTc%d" % sl, te, p.cnt[te])
    p.set_tok("wbig2b", te, p.cnt[te])

    dma("sync", g["col"][:, 40:41], g["erow"][:, :], inc="erl")
    p.auto = True
    op("vector", lambda e: e.tensor_copy(g["erowu"][:], g["col"][:, 40:41]),
       waits=[("erl", 16)], inc=ve)
    op("gpsimd", lambda e: e.indirect_dma_start(
        out=g["tid"][:, :], out_offset=None,
        in_=g["ar3_out"].rearrange("(r pp) one -> r (pp one)", pp=128),
        in_offset=IndirectOffsetOnAxis(ap=g["erowu"][:, 0:1], axis=0),
        bounds_check=p.breg(e, TBL // 128 - 1), oob_is_err=False),
        waits=[("cc", 4), ("p3v", p.cnt[ve])], inc="tgl", amt=16)
    nt = op("tensor", lambda e: e.transpose(g["P_bc1"][:, 0:128], g["tid"][:, :],
                                            g["identf"][:, :]),
            waits=[("tgl", 16)], res=("PB1",), inc=te)
    op("vector", lambda e: e.tensor_copy(g["tid"][:, :], g["P_bc1"][:, 0:128]),
       waits=[(te, nt)], inc=ve)
    p.set_tok("PB1", ve, p.cnt[ve])
    op("vector", lambda e: e.tensor_scalar_add(g["tid"][:, :], g["tid"][:, :], -1.0), inc=ve)
    op("vector", lambda e: e.tensor_scalar_max(g["tid"][:, :], g["tid"][:, :], 0.0), inc=ve)
    nv_tid = op("vector", lambda e: e.tensor_copy(g["tidu"][:], g["tid"][:, :]), inc=ve)
    p.auto = False

    chunks = [(e01, sc) for e01 in range(2) for sc in range(NSC)]
    ln_na = {}
    trc_after = {}

    def prep_unit(ci, part):
        e01, sc = chunks[ci]
        pp = ci % 2
        base = pp * KD * TCH
        ss = part // 3
        ph = part % 3
        xsl = ss % 2
        xtv = g["xt"][:, xsl * D:(xsl + 1) * D]
        if ph == 0:
            j = e01 * RPE + sc * 4 + ss
            op("gpsimd", lambda e, j=j, xtv=xtv: e.indirect_dma_start(
                out=xtv, out_offset=None,
                in_=g["x_full"][:, :],
                in_offset=IndirectOffsetOnAxis(ap=g["tidu"][:, j:j + 1], axis=0),
                bounds_check=p.breg(e, T - 1), oob_is_err=False),
                waits=[("p3v", nv_tid)], res=("xt%d" % xsl,), inc="xg%d" % xsl, amt=16)
            nxg = p.cnt["xg%d" % xsl]
            if ss == 0:
                bfd = [(g["xnb"][:], 0, D, ("xnb",))]
            elif ss == 2:
                bfd = [(g["yev"][:], 0, 512, ("yev",)),
                       (g["yc_sb"][:], 512, D, ("yc0",))]
            elif ss == 3:
                bfd = [(g["xnT_tile"][:], 0, D, ("xnTb",))]
            else:
                bfd = [(g["dtb"][:], 0, 512, ("dtb",)), (g["vb"][:], 512, D, ("vb",))]
            nv, na = emit_ln(p, g, xtv, ve, ae, waits0=[("xg%d" % xsl, nxg)], bf_dests=bfd)
            p.set_tok("xt%d" % xsl, ve, nv)
            ln_na[(ci, ss)] = na
        else:
            na = ln_na[(ci, ss)]
            ks = range(4) if ph == 1 else range(4, 8)
            for k in ks:
                first = (ph == 1 and ss == 0 and k == 0)
                r = ("PTR",) + ((("xeTb", "xTc1") if pp == 1 else ("xTc0",)) if first else ())
                if ss == 0:
                    src_t = g["xnb"][:, k * 128:(k + 1) * 128]
                elif ss == 2:
                    src_t = (g["yev"][:, k * 128:(k + 1) * 128] if k < 4
                             else g["yc_sb"][:, (k - 4) * 128:(k - 3) * 128])
                elif ss == 3:
                    src_t = g["xnT_tile"][:, k * 128:(k + 1) * 128]
                else:
                    src_t = (g["dtb"][:, k * 128:(k + 1) * 128] if k < 4
                             else g["vb"][:, (k - 4) * 128:(k - 3) * 128])
                nt = op("tensor", lambda e, src_t=src_t: e.transpose(
                    g["P_tr"][:, :], src_t, g["identb"][:, :]),
                    waits=[(ae, na)], res=r, inc=te)
                op("vector", lambda e, k=k, ss=ss, base=base: e.tensor_copy(
                    g["xnTc"][:, base + k * SCH + ss * 128:base + k * SCH + (ss + 1) * 128],
                    g["P_tr"][:, :]), waits=[(te, nt)], inc="p3trc", rel=("PTR",))
            if ph == 2 and ss == 3:
                trc_after[ci] = p.cnt["p3trc"]
            if ph == 2:
                if ss == 0:
                    p.set_tok("xnb", "p3trc", p.cnt["p3trc"])
                elif ss == 2:
                    p.set_tok("yev", "p3trc", p.cnt["p3trc"])
                    p.set_tok("yc0", "p3trc", p.cnt["p3trc"])
                elif ss == 3:
                    p.set_tok("xnTb", "p3trc", p.cnt["p3trc"])
                else:
                    p.set_tok("dtb", "p3trc", p.cnt["p3trc"])
                    p.set_tok("vb", "p3trc", p.cnt["p3trc"])

    for part in range(12):
        prep_unit(0, part)
    prep_unit(1, 0)
    prep_unit(1, 3)
    prep_unit(1, 6)
    prep_unit(1, 9)

    isched = {}
    for X in range(1, len(chunks)):
        if X == 1:
            locs = {1: (0, 0), 2: (0, 1), 4: (0, 2), 5: (0, 3),
                    7: (0, 4), 8: (0, 5), 10: (0, 6), 11: (0, 7)}
        else:
            locs = {0: (X - 2, 10), 3: (X - 2, 11), 6: (X - 2, 12),
                    9: (X - 2, 13),
                    1: (X - 1, 0), 2: (X - 1, 1), 4: (X - 1, 2), 5: (X - 1, 3),
                    7: (X - 1, 4), 8: (X - 1, 5),
                    10: (X - 1, 6), 11: (X - 1, 7)}
        for pt, loc in locs.items():
            isched[loc] = (X, pt)

    wN = {}
    for ci, (e01, sc) in enumerate(chunks):
        pp = ci % 2
        base = pp * KD * TCH
        if sc == 0:
            wsl_e = WSLOTS2 if e01 == 0 else WSLOTS5
            wN[e01] = (
                cast_w(p, g, [(g["wg_sl"][e01, k * 128:(k + 1) * 128, :], FF) for k in range(KD)],
                       lambda j, w: g["wbig"][:, j * FF:j * FF + w], "wg%d" % e01,
                       extra_res=("wbigb",), slots=wsl_e),
                cast_w(p, g, [(g["wu_sl"][e01, k * 128:(k + 1) * 128, :], FF) for k in range(KD)],
                       lambda j, w: g["wbig2"][:, j * FF:j * FF + w], "wu%d" % e01,
                       extra_res=("wbig2b",), slots=wsl_e),
                cast_w(p, g, [(g["wd_sl"][e01, f * 128:(f + 1) * 128, :], D) for f in range(FCH)],
                       lambda j, w: g["wdown"][:, j * D:j * D + w], "wd%d" % e01,
                       extra_res=("wdownb",), slots=wsl_e))
        NWG, NWU, NWD = wN[e01]
        waits1 = [("p3trc", trc_after[ci]), ("wg%dc" % e01, NWG)]
        waits2 = [("p3trc", trc_after[ci]), ("wu%dc" % e01, NWU)]
        waits3 = [("wd%dc" % e01, NWD)]

        def inject(slot, ci=ci):
            tgt = isched.get((ci, slot))
            if tgt is not None:
                prep_unit(tgt[0], tgt[1])

        def out_cb(ss, dh, nt, b3, b3r, e01=e01, sc=sc):
            na = p.op("scalar", lambda e, dh=dh, b3=b3: e.activation(
                out=g["eout"][:, dh * 512:(dh + 1) * 512], in_=g[b3][:, :],
                func=AF.Copy), waits=[("p3t", nt)], res=("eoutb",), inc="p3a")
            p.set_tok(b3r, "p3a", na)
            if dh == 1:
                tbl = g["ag5a_in"] if e01 == 0 else g["ag5b_in"]
                tw5 = "ag5wa" if e01 == 0 else "ag5wb"
                p.dma("sync",
                      tbl[sc * 512 + ss * 128:sc * 512 + (ss + 1) * 128, :],
                      g["eout"][:, :], inc=tw5, waits=[("p3a", na)])
                p.set_tok("eoutb", tw5, p.cnt[tw5])

        en1, en2, en3 = mlp_block(
            p, g, lambda k, base=base: g["xnTc"][:, base + k * SCH:base + (k + 1) * SCH],
            waits1, waits2, waits3, out_cb, inject)
        if pp == 1:
            p.set_tok("xeTb", te, en2)
            p.set_tok("xTc1", te, en2)
        else:
            p.set_tok("xTc0", te, en2)
        if sc == NSC - 1:
            p.set_tok("wbigb", te, en1)
            p.set_tok("wbig2b", te, en2)
            p.set_tok("wdownb", te, en3)
            if e01 == 0:
                op("gpsimd", lambda e: e.collective_compute(
                    "AllGather", ALU.bypass, replica_groups=[list(range(NCORE))],
                    ins=[g["ag5a_in"].tensor.ap().opt()],
                    outs=[g["ag5a_out"].tensor.ap().opt()]),
                    waits=[("ag5wa", p.cnt["ag5wa"])], inc="cc")
            else:
                nag5b = p.cnt["ag5wb"]


    NSWG = cast_w(p, g, [(g["sw_gate"][k * 128:(k + 1) * 128, :], FF) for k in range(KD)],
                  lambda j, w: g["wbig"][:, j * FF:j * FF + w], "swg",
                  extra_res=("wbigb",), slots=WSLOTS5)
    NSWD = cast_w(p, g, [(g["sw_down"][f * 128:(f + 1) * 128, :], D) for f in range(FCH)],
                  lambda j, w: g["wdown"][:, j * D:j * D + w], "swd",
                  extra_res=("wdownb",), slots=WSLOTS5)
    NSWU = cast_w(p, g, [(g["sw_up"][k * 128:(k + 1) * 128, :], FF) for k in range(KD)],
                  lambda j, w: g["wbig2"][:, j * FF:j * FF + w], "swu",
                  extra_res=("wbig2b",), slots=WSLOTS5)

    for tc in range(4):
        sl = tc % 2
        for k in range(KD):
            src = g["ag1a_in"] if k < 4 else g["ag1b_in"]
            kk = k % 4
            dma("sync", g["xnTc"][:, (sl * KD + k) * TCH:(sl * KD + k + 1) * TCH],
                src[kk * 128:(kk + 1) * 128, tc * TCH:(tc + 1) * TCH],
                res=("xTc%d" % sl,) if k == 0 else (), inc="sxl%d" % sl)
        nxl = p.cnt["sxl%d" % sl]
        waits1 = [("sxl%d" % sl, nxl), ("swgc", NSWG)]
        waits2 = [("sxl%d" % sl, nxl), ("swuc", NSWU)]
        waits3 = [("swdc", NSWD)]

        def out_cb(ss, dh, nt, b3, b3r, tc=tc):
            ur = 3072 + dh * 512
            na = p.op("scalar", lambda e, ur=ur, b3=b3: e.activation(
                out=g["UP"][:, ur:ur + 512], in_=g[b3][:, :], func=AF.Copy),
                waits=[("p3t", nt)], res=("UPm%d" % dh,), inc="p3a")
            p.set_tok(b3r, "p3a", na)
            p.dma("sync", g["shared_dram"][tc * 512 + ss * 128:tc * 512 + (ss + 1) * 128,
                                           dh * 512:(dh + 1) * 512],
                  g["UP"][:, ur:ur + 512], inc="shst%d" % dh, waits=[("p3a", na)])
            p.set_tok("UPm%d" % dh, "shst%d" % dh, p.cnt["shst%d" % dh])

        sn1, sn2, sn3 = mlp_block(
            p, g,
            lambda k, sl=sl: g["xnTc"][:, (sl * KD + k) * TCH:(sl * KD + k + 1) * TCH],
            waits1, waits2, waits3, out_cb)
        p.set_tok("xTc%d" % sl, te, sn2)
    p.set_tok("wbigb", te, sn1)
    p.set_tok("wbig2b", te, sn2)
    p.set_tok("wdownb", te, sn3)
    for j in range(8):
        p.set_tok("m3y%d" % j, te, sn3)
    op("gpsimd", lambda e: e.collective_compute(
        "AllGather", ALU.bypass, replica_groups=[list(range(NCORE))],
        ins=[g["ag5b_in"].tensor.ap().opt()], outs=[g["ag5b_out"].tensor.ap().opt()]),
        waits=[("ag5wb", nag5b), ("shst0", 16 * 16), ("shst1", 16 * 16)], inc="cc")



def emit_p4(nc, p, g):
    op, dma = p.op, p.dma
    ve, ae, te, pl = "p4v", "p4a", "p4t", "p4p"
    for j in range(8):
        p.set_tok("m3y%d" % j, "p3t", p.cnt["p3t"])

    NWO = cast_w(p, g, [(g["Wout"][k * 128:(k + 1) * 128, :], D) for k in range(KD)],
                 lambda j, w: g["wbig2"][:, j * D:j * D + w], "wo",
                 extra_res=("wbig2b",))
    dma("sync", g["UP"][0:1, 0:D], g["bout"][None, :], inc="bol", res=("UPp4",))
    for dh in range(2):
        nt = op("tensor", lambda e, dh=dh: e.matmul(
            g["P_bc1"][:, :], g["onesr"][:, :], g["UP"][0:1, dh * 512:(dh + 1) * 512],
            start=True, stop=True), waits=[("bol", 16)], res=("PB1",), inc=te)
        nv = op("vector", lambda e, dh=dh: e.tensor_copy(
            g["bout_bc"][:, dh * 512:(dh + 1) * 512], g["P_bc1"][:, :]),
            waits=[(te, nt)], inc=ve)
        p.set_tok("PB1", ve, nv)
    p.set_tok("UPp4", ve, p.cnt[ve])
    p.set_tok("m0P0", "p4t", p.cnt["p4t"])
    p.set_tok("m1P0", "p4t", p.cnt["p4t"])
    p.set_tok("s0P0", "shst0", 16 * 16)
    p.set_tok("s1P0", "shst1", 16 * 16)
    p.also_rel("s0P0", "shst1", 16 * 16)
    p.also_rel("s1P0", "shst0", 16 * 16)
    for _t in ("m0P0", "m1P0", "gP0", "s0P0", "s1P0", "UPp4"):
        p.also_rel(_t, "p3a", p.cnt["p3a"])
        p.also_rel(_t, "p3v", p.cnt["p3v"])
        p.also_rel(_t, "p2a", p.cnt["p2a"])
        p.also_rel(_t, "p2v", p.cnt["p2v"])

    with p.chained():
        op("vector", lambda e: e.tensor_copy(g["ygu"][:], g["ygf"][:]),
           waits=[("cld", p.NCLD)], inc=ve)
    nv_yg = p.cnt[ve]

    for j in range(8):
        op("gpsimd", lambda e, j=j: e.indirect_dma_start(
            out=g["m3"][:, j * TPC:(j + 1) * TPC], out_offset=None,
            in_=g["ag4_out"][:, :],
            in_offset=IndirectOffsetOnAxis(ap=g["ygu"][:, j:j + 1], axis=0),
            bounds_check=p.breg(e, NCORE * DSL * 8 - 1), oob_is_err=False),
            waits=[("cc", 5), ("p4v", nv_yg)], res=("m3y%d" % j,), inc="ygl", amt=16)
    zsbv = g["xnTc"][:, 0:TPC]
    for j in range(8):
        dma("sync", zsbv, g["z_dram"][j * 128:(j + 1) * 128, :],
            waits=[("zst0", 16 * 16), ("zst1", 16 * 16)], res=("zsbb", "xTc0"), inc="zld")
        nz = p.cnt["zld"]
        nv = op("vector", lambda e, j=j, zsbv=zsbv: e.tensor_mul(
            g["m3"][:, j * TPC:(j + 1) * TPC], g["m3"][:, j * TPC:(j + 1) * TPC],
            zsbv),
            waits=[("ygl", 16 * 8), ("zld", nz)], inc=ve)
        p.set_tok("zsbb", ve, nv)
        p.set_tok("xTc0", ve, nv)
    NV_YZ = p.cnt[ve]

    dma("sync", g["xt"][:, 0:D], g["xn_dram"][0:128, :],
        waits=[("xnst", 16 * 16)], res=("xt0",), inc="xnl0")
    for i in range(NTIL):
        P = i % 2
        if P == 0:
            moe0 = g["UP"][:, 0:D]
            moe1 = g["UP"][:, 1024:1024 + D]
            ggv = g["UP"][:, 2048:2048 + D]
            sv = [g["UP"][:, 3072:3584], g["UP"][:, 3584:4096]]
            uv = g["UP"][:, 1024:1024 + D]
        else:
            moe0 = g["a_sb"][:, 0:D]
            moe1 = g["xdb_sb"][:, 0:D]
            ggv = g["h_sb"][:, 0:D]
            sv = [g["dtf"][:, 0:512], g["vf"][:, 0:512]]
            uv = g["xdb_sb"][:, 0:D]
        ex = {"m0": ("wstA",), "m1": ("wstX",), "gg": ("wstH",),
              "s0": ("dtf",), "s1": ("vf",)} if i == 1 else {}

        for dh in range(2):
            pa = "P_acc0" if dh == 0 else "P_acc1"
            for k in range(KD):
                r = ("PA%d" % dh,) if k == 0 else ()
                op("tensor", lambda e, k=k, dh=dh, i=i, pa=pa: e.matmul(
                    g[pa][:, :], g["m3"][:, k * TPC + i * 128:k * TPC + (i + 1) * 128],
                    g["wbig2"][:, k * D + dh * 512:k * D + (dh + 1) * 512],
                    start=(k == 0), stop=(k == KD - 1)),
                    waits=[("p4v", NV_YZ), ("woc", NWO)] if k == 0 else (), res=r, inc=te)
            nt = p.cnt[te]
            nv = op("vector", lambda e, dh=dh, pa=pa, sva=sv[dh]: e.tensor_add(
                sva, g[pa][:, :], g["bout_bc"][:, dh * 512:(dh + 1) * 512]),
                waits=[(te, nt)], res=("s%dP%d" % (dh, P),) + ex.get("s%d" % dh, ()),
                inc=ve)
            p.set_tok("PA%d" % dh, ve, nv)
        nv_s = p.cnt[ve]

        nv_m = []
        for kk in range(2):
            sx = 2 * i + kk
            msl = P * 2 + kk
            if P == 0:
                mgv = g["xnTc"][:, TPC + kk * D:TPC + (kk + 1) * D]
                mgv2 = g["xnTc"][:, kk * D:(kk + 1) * D]
                slot_tok = ("xTc0",)
            else:
                mgv = g["xnTc"][:, 4096 + kk * D:4096 + (kk + 1) * D]
                mgv2 = g["xnTc"][:, 4096 + 2048 + kk * D:4096 + 2048 + (kk + 1) * D]
                slot_tok = ("xTc1", "xeTb") if i == 1 else ()
            mo = moe0 if kk == 0 else moe1
            op("gpsimd", lambda e, sx=sx, mgv=mgv: e.indirect_dma_start(
                out=mgv, out_offset=None,
                in_=g["ag5a_out"][:, :],
                in_offset=IndirectOffsetOnAxis(ap=g["ucols"][:, sx:sx + 1], axis=0),
                bounds_check=p.breg(e, NCORE * CAP - 1), oob_is_err=False),
                waits=[("cc", 6)], res=("mg%d" % msl,) + slot_tok, inc="mglA%d" % msl, amt=16)
            op("gpsimd", lambda e, sx=sx, mgv2=mgv2: e.indirect_dma_start(
                out=mgv2, out_offset=None,
                in_=g["ag5b_out"][:, :],
                in_offset=IndirectOffsetOnAxis(ap=g["ucols"][:, sx:sx + 1], axis=0),
                bounds_check=p.breg(e, NCORE * CAP - 1), oob_is_err=False),
                waits=[("cc", 7)], res=("mgB%d" % msl, "zsbb" if P == 0 else "mgB%d" % msl) + slot_tok,
                inc="mglB%d" % msl, amt=16)
            nmgA = p.cnt["mglA%d" % msl]
            nmgB = p.cnt["mglB%d" % msl]
            na = op("scalar", lambda e, sx=sx, mgv=mgv, mo=mo: e.activation(
                out=mo, in_=mgv, func=AF.Copy,
                scale=g["rcols"][:, sx:sx + 1]),
                waits=[("mglA%d" % msl, nmgA)],
                res=("m%dP%d" % (kk, P),) + ex.get("m%d" % kk, ()), inc=ae)
            p.set_tok("mg%d" % msl, ae, na)
            if P == 0:
                p.also_rel("xTc0", "p4a", na)
            nv = op("vector", lambda e, sx=sx, mgv2=mgv2, mo=mo: e.scalar_tensor_tensor(
                mo, mgv2, g["rcols2"][:, sx:sx + 1], mo, ALU.mult, ALU.add),
                waits=[("mglB%d" % msl, nmgB), ("p4a", na)], inc=ve)
            p.set_tok("mgB%d" % msl, ve, nv)
            if P == 0:
                p.also_rel("zsbb", "p4v", nv)
                p.also_rel("xTc0", "p4v", nv)
            else:
                p.also_rel("xTc1", "p4v", nv)
                p.also_rel("xeTb", "p4v", nv)
            nv_m.append(nv)
        nv_moe = p.cnt[ve]

        xsl = i % 2
        xtv = g["xt"][:, xsl * D:(xsl + 1) * D]
        nxn0 = 16 * (i // 2 + 1)
        nv = op("vector", lambda e, moe0=moe0, moe1=moe1: e.tensor_add(
            moe0, moe0, moe1),
            waits=[("p4v", nv_moe)], inc=ve)
        p.set_tok("m1P%d" % P, ve, nv)
        nv = op("vector", lambda e, xtv=xtv, moe0=moe0: e.tensor_add(moe0, moe0, xtv),
                waits=[("xnl%d" % xsl, nxn0), (ve, nv)], inc=ve)
        dma("sync", xtv, g["shared_dram"][i * 128:(i + 1) * 128, :],
            waits=[("shst0", 16 * 16), ("shst1", 16 * 16), (ve, nv)], inc="xshl")
        nxn = p.cnt["xshl"]
        nv = op("vector", lambda e, xtv=xtv, moe0=moe0: e.tensor_add(moe0, moe0, xtv),
                waits=[("xshl", nxn), (ve, nv)], inc=ve)
        na = op("scalar", lambda e, ggv=ggv, moe0=moe0: e.activation(
            out=ggv, in_=moe0, func=AF.Silu),
            waits=[(ve, nv)], res=("gP%d" % P,) + ex.get("gg", ()), inc=ae)
        p.set_tok("m0P%d" % P, ae, na)
        nvm = op("vector", lambda e, uv=uv, ggv=ggv, sva=sv[0]: e.tensor_mul(
            uv[:, 0:512], sva, ggv[:, 0:512]),
            waits=[("p4a", na), ("p4v", nv_s)], res=("m1P%d" % P,), inc=ve)
        nv = op("vector", lambda e, uv=uv, ggv=ggv, svb=sv[1]: e.tensor_mul(
            uv[:, 512:1024], svb, ggv[:, 512:1024]),
            waits=[("p4a", na), ("p4v", nv_s), (ve, nvm)], inc=ve)
        p.set_tok("s0P%d" % P, ve, nv)
        p.set_tok("s1P%d" % P, ve, nv)
        p.set_tok("gP%d" % P, ve, nv)
        dma("sync", xtv, g["x_own"][i * 128:(i + 1) * 128, :], waits=[(ve, nv)],
            inc="xxl")
        nxn = p.cnt["xxl"]
        if i + 1 < NTIL:
            xsl2 = (i + 1) % 2
            dma("sync", g["xt"][:, xsl2 * D:(xsl2 + 1) * D],
                g["xn_dram"][(i + 1) * 128:(i + 2) * 128, :],
                res=("xt%d" % xsl2,), inc="xnl%d" % xsl2)
        nv = op("vector", lambda e, xtv=xtv, uv=uv: e.tensor_add(g["xnf"][:], uv, xtv),
                waits=[("xxl", nxn), (ve, nv)], res=("xnf",), inc=ve)
        p.set_tok("m1P%d" % P, ve, nv)
        p.set_tok("xt%d" % xsl, ve, nv)
        dma("sync", g["o_out"][i * 128:(i + 1) * 128, :], g["xnf"][:],
            inc="ost", waits=[(ve, nv)])
        p.set_tok("xnf", "ost", p.cnt["ost"])

    allw = [("ost", p.cnt["ost"]), ("cc", p.cnt["cc"])]
    for nm in DEBUG_TAPS:
        dma("sync", g["tap_" + nm][tuple(slice(None) for _ in g[nm].shape)],
            g[nm][tuple(slice(None) for _ in g[nm].shape)], inc="tapw", waits=allw)
    if DEBUG_TAPS:
        op("sync", lambda e: e.nop(), waits=[("tapw", p.cnt["tapw"])])
    op("sync", lambda e: e.nop(), waits=[("ost", p.cnt["ost"])])


# version 35
# speedup vs baseline: 1.1405x; 1.0134x over previous
import contextlib
import numpy as np
import ml_dtypes
import concourse.bass as bass
import concourse.mybir as mybir
from concourse.bass_core_rust import IndirectOffsetOnAxis
from concourse.bass_utils import run_bass_kernel_spmd

F32 = mybir.dt.float32
BF16 = mybir.dt.bfloat16
U32 = mybir.dt.uint32
AF = mybir.ActivationFunctionType
ALU = mybir.AluOpType
AX = mybir.AxisListType.X

NCORE = 8
B, L, D, N = 2, 8192, 1024, 16
T = B * L
TPC = T // NCORE
E, CAP, FF = 16, 2560, 1408
DSL = D // NCORE
TCH = 512
NIT = T // TCH
NTIL = TPC // 128
FCH = FF // 128
KD = D // 128
TBL = E * CAP + 128
TRASH = E * CAP
RPE = CAP // 128
SCH = 512
NSC = CAP // SCH
EPS = 1e-5
BIGF = 1.0e9
DEBUG_TAPS = ()



def _consts(core):
    c = {}
    c["ident_bf"] = np.eye(128, dtype=ml_dtypes.bfloat16)
    c["ident_f"] = np.eye(128, dtype=np.float32)
    c["iota16_bc"] = np.broadcast_to(np.arange(16, dtype=np.float32), (128, 16)).copy()
    c["parb"] = np.broadcast_to((np.arange(16) % 2).astype(np.float32), (128, 16)).copy()
    m = np.zeros((16, 128), np.float32)
    for p_ in range(128):
        m[p_ % 16, p_] = 1.0
    c["bcmask16"] = m.astype(ml_dtypes.bfloat16)
    rm = np.zeros((128, 16 * 128), np.float32)
    for gg in range(16):
        for p_ in range(128):
            rm[gg * 8 + p_ // 16, gg * 128 + p_] = 1.0
    c["repmask"] = rm.astype(ml_dtypes.bfloat16)
    ns = np.zeros((128, 16 * 128), np.float32)
    for gg in range(16):
        for p_ in range(128):
            ns[p_, gg * 128 + 8 * gg + p_ // 16] = 1.0
    c["nsmask"] = ns.astype(ml_dtypes.bfloat16)
    tok1 = np.empty((128, NTIL), np.float32)
    for i in range(NTIL):
        tok1[:, i] = core * TPC + i * 128 + np.arange(128) + 1
    c["tok1"] = tok1
    c["trash_col"] = (TRASH + np.arange(128, dtype=np.float32))[:, None].copy()
    c["mask_lt8"] = (np.arange(8) < core).astype(np.float32)[:, None].copy()
    yg = np.empty((128, 8), np.float32)
    for j in range(8):
        yg[:, j] = j * 1024 + np.arange(128) * 8 + core
    c["ygidx"] = yg
    er = np.zeros((128, 1), np.float32)
    er[0:2 * RPE, 0] = core * 2 * RPE + np.arange(2 * RPE)
    c["erow"] = er
    c["ones_row"] = np.ones((1, 128), np.float32)
    c["ones_col16"] = np.ones((16, 1), np.float32)
    return c


def kernel(**inputs):
    ii = {k: np.ascontiguousarray(np.asarray(v, dtype=np.float32))
          for k, v in inputs.items()}
    x2d = ii["L0_attn_x"].reshape(T, D)
    in_maps = []
    for c in range(NCORE):
        m = {
            "x_own": np.ascontiguousarray(x2d[c * TPC:(c + 1) * TPC]),
            "x_full": x2d,
            "Wz": ii["L0_attn_Wz"], "bz": ii["L0_attn_bz"],
            "Wx_sl": np.ascontiguousarray(ii["L0_attn_Wx"][:, c * DSL:(c + 1) * DSL]),
            "bx_sl": np.ascontiguousarray(ii["L0_attn_bx"][c * DSL:(c + 1) * DSL]),
            "Wdt_sl": np.ascontiguousarray(ii["L0_attn_Wdt"][:, c * DSL:(c + 1) * DSL]),
            "bdt_sl": np.ascontiguousarray(ii["L0_attn_bdt"][c * DSL:(c + 1) * DSL]),
            "WB": ii["L0_attn_WB"], "bB": ii["L0_attn_bB"],
            "WC": ii["L0_attn_WC"], "bC": ii["L0_attn_bC"],
            "A_log_sl": np.ascontiguousarray(ii["L0_attn_A_log"][c * DSL:(c + 1) * DSL]),
            "Wout": ii["L0_attn_Wout"], "bout": ii["L0_attn_bout"],
            "router_w": ii["L0_ffn_router_w"], "router_b": ii["L0_ffn_router_b"],
            "wg_sl": np.ascontiguousarray(ii["L0_ffn_w_gate"][2 * c:2 * c + 2]),
            "wu_sl": np.ascontiguousarray(ii["L0_ffn_w_up"][2 * c:2 * c + 2]),
            "wd_sl": np.ascontiguousarray(ii["L0_ffn_w_down"][2 * c:2 * c + 2]),
            "sw_gate": ii["L0_ffn_sw_gate"], "sw_up": ii["L0_ffn_sw_up"],
            "sw_down": ii["L0_ffn_sw_down"],
        }
        m.update(_consts(c))
        in_maps.append(m)
    nc = build_nc()
    res = run_bass_kernel_spmd(nc, in_maps, core_ids=list(range(NCORE)))
    out = np.concatenate([np.asarray(r["o_out"]) for r in res.results], axis=0)
    return np.ascontiguousarray(out.reshape(B, L, D).astype(np.float32))



class Prog:
    ENGS = ("sync", "tensor", "vector", "scalar", "gpsimd")

    def __init__(self, nc):
        self.nc = nc
        self.q = {e: [] for e in self.ENGS}
        self.semh = {}
        self.cnt = {}
        self.tok = {}
        self.last = {}
        self.auto = False

    def sem(self, name):
        if name not in self.semh:
            self.semh[name] = self.nc.alloc_semaphore(name)
            self.cnt[name] = 0
        return self.semh[name]

    def op(self, eng, builder, waits=(), res=(), inc=None, amt=1, rel=()):
        wl = list(waits)
        for r in res:
            wl.extend(self.tok.get(r, ()))
        if self.auto and eng in ("vector", "scalar") and eng in self.last:
            wl.append(self.last[eng])
        post = None
        incs = []
        if inc is not None:
            self.sem(inc)
            self.cnt[inc] += amt
            post = self.cnt[inc]
            incs = [(inc, amt)]
        for s, _v in wl:
            self.sem(s)
        self.q[eng].append((builder, tuple(wl), tuple(incs)))
        if inc is not None and eng in ("vector", "scalar", "gpsimd", "tensor"):
            self.last[eng] = (inc, post)
        if rel:
            assert inc is not None
            for r in rel:
                self.tok[r] = [(inc, post)]
        return post

    def chained(self):
        import contextlib as _ctx

        @_ctx.contextmanager
        def cm():
            prev = self.auto
            self.auto = True
            try:
                yield
            finally:
                self.auto = prev
        return cm()

    def breg(self, e, val):
        if not hasattr(self, "_bregs"):
            self._bregs = {}
        if val not in self._bregs:
            self._bregs[val] = e.to_reg(val)
        return self._bregs[val]

    def also_rel(self, r, sem, val):
        self.tok.setdefault(r, []).append((sem, val))

    def set_tok(self, r, sem, val):
        self.tok[r] = [(sem, val)]

    def dma(self, eng, out, in_, waits=(), res=(), inc=None, rel=(), **kw):
        return self.op(eng, lambda e: e.dma_start(out=out, in_=in_, **kw),
                       waits=waits, res=res, inc=inc, amt=16, rel=rel)

    def flush(self, block):
        for name in self.ENGS:
            ops = self.q[name]
            if not ops:
                continue

            def mk(ops):
                def f(eng):
                    for builder, waits, incs in ops:
                        seen = {}
                        for s, v in waits:
                            seen[s] = max(seen.get(s, 0), v)
                        for s, v in seen.items():
                            eng.wait_ge(self.semh[s], v)
                        ins = builder(eng)
                        for s, a in incs:
                            ins.then_inc(self.semh[s], a)
                return f

            getattr(block, name)(mk(ops))


def build_nc():
    nc = bass.Bass("TRN2", target_bir_lowering=False, debug=False)
    g = {}

    def dram(name, shape, dt, kind="Internal"):
        g[name] = nc.dram_tensor(name, shape, dt, kind=kind).ap()

    for nm, sh, dt in [
        ("x_own", [TPC, D], F32), ("x_full", [T, D], F32),
        ("Wz", [D, D], F32), ("bz", [D], F32),
        ("Wx_sl", [D, DSL], F32), ("bx_sl", [DSL], F32),
        ("Wdt_sl", [D, DSL], F32), ("bdt_sl", [DSL], F32),
        ("WB", [D, N], F32), ("bB", [N], F32),
        ("WC", [D, N], F32), ("bC", [N], F32),
        ("A_log_sl", [DSL, N], F32),
        ("Wout", [D, D], F32), ("bout", [D], F32),
        ("router_w", [D, N], F32), ("router_b", [N], F32),
        ("wg_sl", [2, D, FF], F32), ("wu_sl", [2, D, FF], F32),
        ("wd_sl", [2, FF, D], F32),
        ("sw_gate", [D, FF], F32), ("sw_up", [D, FF], F32), ("sw_down", [FF, D], F32),
        ("ident_bf", [128, 128], BF16), ("ident_f", [128, 128], F32),
        ("iota16_bc", [128, 16], F32),
        ("bcmask16", [16, 128], BF16), ("repmask", [128, 16 * 128], BF16),
        ("nsmask", [128, 16 * 128], BF16), ("tok1", [128, NTIL], F32),
        ("trash_col", [128, 1], F32), ("mask_lt8", [8, 1], F32),
        ("ygidx", [128, 8], F32), ("parb", [128, 16], F32),
        ("ones_row", [1, 128], F32), ("ones_col16", [16, 1], F32),
        ("erow", [128, 1], F32),
    ]:
        dram(nm, sh, dt, kind="ExternalInput")

    dram("o_out", [TPC, D], F32, kind="ExternalOutput")

    dram("ag1a_in", [D // 2, TPC], BF16)
    dram("ag1b_in", [D // 2, TPC], BF16)
    dram("ag1a_out", [NCORE * D // 2, TPC], BF16)
    dram("ag1b_out", [NCORE * D // 2, TPC], BF16)
    dram("ag2_in", [34, TPC], F32)
    dram("ag2_out", [NCORE * 34, TPC], F32)
    dram("ar3_in", [TBL, 1], F32)
    dram("ar3_out", [TBL, 1], F32)
    dram("ag4_in", [DSL * 8, TPC], BF16)
    dram("ag4_out", [NCORE * DSL * 8, TPC], BF16)
    dram("ag5a_in", [CAP, D], BF16)
    dram("ag5b_in", [CAP, D], BF16)
    dram("ag5a_out", [NCORE * CAP, D], BF16)
    dram("ag5b_out", [NCORE * CAP, D], BF16)
    dram("z_dram", [D, TPC], BF16)
    dram("xn_dram", [TPC, D], F32)
    dram("shared_dram", [TPC, D], F32)

    for nm in DEBUG_TAPS:
        src = g[nm]
        dram("tap_" + nm, list(src.shape), src.dtype, kind="ExternalOutput")

    build_program(nc, g)
    return nc


def build_program(nc, g):
    p = Prog(nc)
    stck = contextlib.ExitStack()

    def sb(name, shape, dt):
        g[name] = stck.enter_context(nc.sbuf_tensor(name, shape, dt))

    def ps(name, shape, dt):
        g[name] = stck.enter_context(nc.psum_tensor(name, shape, dt))

    with stck:
        sb("xt", [128, 2 * D], F32)
        sb("UP", [128, 4417], F32)
        sb("xnf", [128, D], F32)
        sb("xnb", [128, D], BF16)
        sb("xnT_tile", [128, KD * 128], BF16)
        sb("wbig", [128, KD * FF], BF16)
        sb("wbig2", [128, KD * FF], BF16)
        sb("wdown", [128, FCH * D], BF16)
        sb("wstage", [128, FF], F32)
        sb("xnTc", [128, 2 * KD * TCH], BF16)
        sb("wsl", [128, KD * 2 * DSL], BF16)
        sb("wsmall", [128, KD * 3 * N], BF16)
        sb("identb", [128, 128], BF16)
        sb("identf", [128, 128], F32)
        sb("bc16", [16, 128], BF16)
        sb("repm", [128, 16 * 128], BF16)
        sb("nsm", [128, 16 * 128], BF16)
        sb("iotab", [128, 16], F32)
        sb("tok1_sb", [128, NTIL], F32)
        sb("trash_sb", [128, 1], F32)
        sb("mlt8", [8, 1], F32)
        sb("onesr", [1, 128], F32)
        sb("ones16c", [16, 1], F32)
        sb("acols", [128, 16], F32)
        sb("bias_sb", [128, 16], F32)
        sb("bb_sb", [16, 2], F32)
        sb("rb_bc", [128, 16], F32)
        sb("bout_bc", [128, D], F32)
        sb("off_bc", [128, 16], F32)
        sb("col", [128, 48], F32)
        sb("packA", [128, 32], F32)
        sb("packB", [128, 32], F32)
        sb("ohtok", [128, 64], F32)
        sb("oh_c", [16, TCH], F32)
        sb("ones16t", [16, TCH], F32)
        sb("cum_c", [16, TCH], F32)
        sb("carry16", [16, 1], F32)
        sb("lcols", [128, 2 * NTIL], F32)
        sb("pcols", [128, 2 * NTIL], F32)
        sb("rcols", [128, 2 * NTIL], F32)
        sb("rcols2", [128, 2 * NTIL], F32)
        sb("ucols", [128, 4 * NTIL], U32)
        sb("off16", [16, 1], F32)
        sb("tot8", [8, 16], F32)
        sb("dtf", [128, TCH], F32)
        sb("dtb", [128, TCH], BF16)
        sb("vf", [128, TCH], F32)
        sb("vb", [128, TCH], BF16)
        sb("bmc_st", [16, TCH], F32)
        sb("bmcb", [16, TCH], BF16)
        sb("bm_bc", [128, TCH], BF16)
        sb("c_bc", [128, TCH], BF16)
        sb("a_sb", [128, 2 * TCH], F32)
        sb("xdb_sb", [128, 2 * TCH], F32)
        sb("h_sb", [128, 2 * TCH], F32)
        sb("yc_sb", [128, TCH], BF16)
        sb("carryg", [128, 16], F32)
        sb("yev", [128, TCH], BF16)
        sb("m3", [128, 3 * FCH * SCH], BF16)
        sb("eout", [128, D], BF16)
        sb("tidu", [128, 128], U32)
        sb("erowu", [128, 1], U32)
        sb("ygu", [128, 8], U32)

        g["xeT"] = g["xnTc"][:, KD * TCH:2 * KD * TCH]
        g["krow"] = g["UP"][0:33, 1024:1280]
        g["tid"] = g["UP"][:, 2048:2176]
        g["ygf"] = g["UP"][:, 2176:2184]
        g["fconst"] = g["UP"][:, 2184:2186]

        ps("P_acc0", [128, 512], F32)
        ps("P_acc1", [128, 512], F32)
        ps("P_acc2", [128, 512], F32)
        ps("P_acc3", [128, 512], F32)
        ps("P_bc0", [128, 512], F32)
        ps("P_bc1", [128, 512], F32)
        ps("P_y", [128, 512], F32)
        ps("P_tr", [128, 128], BF16)

        emit_p1(nc, p, g)
        emit_p2(nc, p, g)
        emit_p3(nc, p, g)
        emit_p4(nc, p, g)

        with nc.Block() as block:
            p.flush(block)



def emit_ln(p, g, src_ap, ve, ae, waits0=(), bf_dests=None):
    op = p.op
    sq = g["UP"][:, 0:D]
    p.auto = True
    op("vector", lambda e: e.reduce_sum(g["col"][:, 0:1], src_ap, axis=AX),
       waits=waits0, inc=ve)
    na = op("scalar", lambda e: e.activation(out=sq, in_=src_ap, func=AF.Square),
            waits=waits0, res=("UPln",), inc=ae)
    nv = op("vector", lambda e: e.reduce_sum(g["col"][:, 1:2], sq, axis=AX),
            waits=[(ae, na)], inc=ve)
    p.set_tok("UPln", ve, nv)
    op("vector", lambda e: e.tensor_scalar_mul(g["col"][:, 2:3], g["col"][:, 0:1], 1.0 / D), inc=ve)
    op("vector", lambda e: e.tensor_scalar_mul(g["col"][:, 3:4], g["col"][:, 1:2], 1.0 / D), inc=ve)
    op("vector", lambda e: e.tensor_mul(g["col"][:, 4:5], g["col"][:, 2:3], g["col"][:, 2:3]), inc=ve)
    nv = op("vector", lambda e: e.tensor_sub(g["col"][:, 5:6], g["col"][:, 3:4], g["col"][:, 4:5]), inc=ve)
    na = op("scalar", lambda e: e.activation(out=g["col"][:, 6:7], in_=g["col"][:, 5:6],
                                             func=AF.Sqrt, bias=g["fconst"][:, 0:1]),
            waits=[(ve, nv)], res=("fc",), inc=ae)
    op("vector", lambda e: e.reciprocal(g["col"][:, 7:8], g["col"][:, 6:7]),
       waits=[(ae, na)], inc=ve)
    nv = op("vector", lambda e: e.tensor_scalar_mul(g["col"][:, 9:10], g["col"][:, 2:3], -1.0), inc=ve)
    nv = op("vector", lambda e: e.tensor_scalar(
        g["xnf"][:], src_ap, g["col"][:, 9:10], g["col"][:, 7:8], ALU.add, ALU.mult),
        res=("xnf",), inc=ve)
    if bf_dests is None:
        na = op("scalar", lambda e: e.activation(out=g["xnb"][:], in_=g["xnf"][:], func=AF.Copy),
                waits=[(ve, nv)], res=("xnb",), inc=ae)
    else:
        for dst, lo, hi, r in bf_dests[:-1]:
            op("scalar", lambda e, dst=dst, lo=lo, hi=hi: e.activation(
                out=dst, in_=g["xnf"][:, lo:hi], func=AF.Copy),
                waits=[(ve, nv)], res=r, inc=ae)
        dst, lo, hi, r = bf_dests[-1]
        na = op("scalar", lambda e, dst=dst, lo=lo, hi=hi: e.activation(
            out=dst, in_=g["xnf"][:, lo:hi], func=AF.Copy),
            waits=[(ve, nv)], res=r, inc=ae)
    p.auto = False
    return nv, na


WSLOTS2 = (("wstage", 0, "sync", "wst0", ()), ("wstage", 704, "gpsimd", "wst1", ()))
WSLOTS5 = WSLOTS2 + (("a_sb", 0, "sync", "wstA", ("a0", "a1")),
                     ("xdb_sb", 0, "gpsimd", "wstX", ("x0", "x1")),
                     ("h_sb", 0, "sync", "wstH", ("h0", "h1")))


def cast_w(p, g, src_chunks, dst_fn, tag, extra_res=(), slots=WSLOTS2):
    nh = 0
    for j, (src_ap, w) in enumerate(src_chunks):
        for h in range(2):
            lo, hi = h * (w // 2), (w if h else w // 2)
            si = nh % len(slots)
            tn, off, q, tok, own = slots[si]
            p.dma(q, g[tn][:, off:off + hi - lo], src_ap[:, lo:hi],
                  res=(tok,) + own, inc="wls%d" % si)
            nl = p.cnt["wls%d" % si]
            p.op("vector", lambda e, j=j, w=w, lo=lo, hi=hi, tn=tn, off=off: e.tensor_copy(
                dst_fn(j, w)[:, lo:hi], g[tn][:, off:off + hi - lo]),
                waits=[("wls%d" % si, nl)],
                res=extra_res if nh == 0 else (),
                inc=tag + "c", rel=(tok,))
            nh += 1
    return p.cnt[tag + "c"]



def emit_p1(nc, p, g):
    op, dma = p.op, p.dma
    ve, ae, te = "p1v", "p1a", "p1t"

    for nm_s, nm_d in [("ident_bf", "identb"), ("ident_f", "identf"),
                       ("bcmask16", "bc16"), ("repmask", "repm"), ("nsmask", "nsm"),
                       ("iota16_bc", "iotab"), ("tok1", "tok1_sb"),
                       ("trash_col", "trash_sb"), ("mask_lt8", "mlt8"),
                       ("ones_row", "onesr"), ("ones_col16", "ones16c"),
                       ("ygidx", "ygf")]:
        dma("sync", g[nm_d][:], g[nm_s][:, :], inc="cld")
    dma("sync", g["bias_sb"][:, 0:8], g["bz"].rearrange("(k pp) -> pp k", pp=128),
        inc="cld", allow_slow_non_contiguous=True)
    dma("sync", g["bias_sb"][:, 8:9], g["bx_sl"][:, None], inc="cld")
    dma("sync", g["bias_sb"][:, 9:10], g["bdt_sl"][:, None], inc="cld")
    dma("sync", g["bb_sb"][:, 0:1], g["bB"][:, None], inc="cld")
    dma("sync", g["bb_sb"][:, 1:2], g["bC"][:, None], inc="cld")
    dma("sync", g["ohtok"][:, 48:64], g["parb"][:, :], inc="cld")
    dma("sync", g["acols"][:],
        g["A_log_sl"].rearrange("(gg ds) n -> (ds n) gg", gg=16),
        inc="cld", allow_slow_non_contiguous=True)
    p.NCLD = p.cnt["cld"]
    op("vector", lambda e: e.memset(g["fconst"][:, 0:1], float(EPS)), inc="fcs")
    op("vector", lambda e: e.memset(g["fconst"][:, 1:2], 1.0), inc="fcs")
    p.set_tok("fc", "fcs", 2)
    op("scalar", lambda e: e.activation(out=g["acols"][:], in_=g["acols"][:], func=AF.Exp),
       waits=[("cld", p.NCLD)], inc="acol")
    op("vector", lambda e: e.tensor_scalar_mul(g["acols"][:], g["acols"][:], -1.0),
       waits=[("acol", 1)], inc="acol")
    p.set_tok("acolv", "acol", 2)

    p.NWZ = cast_w(p, g, [(g["Wz"][k * 128:(k + 1) * 128, :], D) for k in range(KD)],
                   lambda j, w: g["wbig2"][:, j * D:j * D + w], "wz")
    for k in range(KD):
        for jj, nm in enumerate(["WB", "WC", "router_w"]):
            dma("sync", g["wstage"][:, jj * N:(jj + 1) * N],
                g[nm][k * 128:(k + 1) * 128, :],
                res=("wst0",) if jj == 0 else (), inc="wsml")
        nl = p.cnt["wsml"]
        op("vector", lambda e, k=k: e.tensor_copy(
            g["wsmall"][:, k * 3 * N:(k + 1) * 3 * N],
            g["wstage"][:, 0:3 * N]),
            waits=[("wsml", nl)], inc="wsmc", rel=("wst0",))
    p.NWSM = p.cnt["wsmc"]
    for k in range(KD):
        dma("sync", g["wstage"][:, 0:DSL],
            g["Wdt_sl"][k * 128:(k + 1) * 128, :], res=("wst0",), inc="wsll")
        dma("sync", g["wstage"][:, DSL:2 * DSL],
            g["Wx_sl"][k * 128:(k + 1) * 128, :], inc="wsll")
        nl = p.cnt["wsll"]
        op("vector", lambda e, k=k: e.tensor_copy(
            g["wsl"][:, k * 2 * DSL:(k + 1) * 2 * DSL],
            g["wstage"][:, 0:2 * DSL]),
            waits=[("wsll", nl)], inc="wslc", rel=("wst0",))
    p.NWSL = p.cnt["wslc"]

    dma("sync", g["col"][0:1, 16:32], g["router_b"][None, :], inc="rbl")
    nt = op("tensor", lambda e: e.matmul(g["P_y"][:, 0:16], g["onesr"][:, :],
                                         g["col"][0:1, 16:32], start=True, stop=True),
            waits=[("rbl", 16), ("cld", p.NCLD)], res=("PY",), inc=te)
    op("vector", lambda e: e.tensor_copy(g["rb_bc"][:], g["P_y"][:, 0:16]),
       waits=[(te, nt)], inc="rbb", rel=("PY",))

    op("vector", lambda e: e.memset(g["ones16t"][:], 1.0), inc="ones16")
    op("vector", lambda e: e.memset(g["UP"][:, 4096:4417], 0.0), inc="zset")
    dma("sync", g["ar3_in"].rearrange("(pp k) one -> pp (k one)", pp=128),
        g["UP"][:, 4096:4417], inc="ar3z", waits=[("zset", 1)])
    op("vector", lambda e: e.memset(g["bmc_st"][0:1, 0:512], 0.0), inc="zset")
    for lo, hi in ((16, 528), (528, 1040), (1040, 1552), (1552, 2048)):
        dma("sync", g["ag2_in"][32:33, lo:hi], g["bmc_st"][0:1, 0:hi - lo],
            inc="ag2w", waits=[("zset", 2)])
    for lo in (0, 512, 1024, 1536):
        dma("sync", g["ag2_in"][33:34, lo:lo + 512], g["bmc_st"][0:1, 0:512],
            inc="ag2w", waits=[("zset", 2)])
    p.set_tok("bmst0", "ag2w", p.cnt["ag2w"])
    p.set_tok("bmst1", "ag2w", p.cnt["ag2w"])

    def p1_stageA(i):
        sl = i % 2
        tsl = i % 2
        xtv = g["xt"][:, sl * D:(sl + 1) * D]
        nx = 16 * (i // 2 + 1)
        if i + 1 < NTIL:
            sl2 = (i + 1) % 2
            dma("sync", g["xt"][:, sl2 * D:(sl2 + 1) * D],
                g["x_own"][(i + 1) * 128:(i + 2) * 128, :],
                res=("xt%d" % sl2,), inc="xld%d" % sl2)
        if i % 2 == 0:
            bfd = [(g["xnb"][:], 0, D, ("xnb",))]
            tokD = tokV = "xnb"
        else:
            bfd = [(g["dtb"][:], 0, 512, ("dtb",)), (g["vb"][:], 512, D, ("vb",))]
            tokD, tokV = "dtb", "vb"
        nv, na = emit_ln(p, g, xtv, ve, ae, waits0=[("xld%d" % sl, nx)], bf_dests=bfd)
        p.set_tok("xt%d" % sl, ve, nv)
        dma("sync", g["xn_dram"][i * 128:(i + 1) * 128, :], g["xnf"][:],
            inc="xnst", waits=[(ve, nv)])
        p.set_tok("xnf", "xnst", p.cnt["xnst"])

        for k in range(KD):
            w = [(ae, na), ("cld", p.NCLD)]
            if i % 2 == 0:
                src_t = g["xnb"][:, k * 128:(k + 1) * 128]
            else:
                src_t = (g["dtb"][:, k * 128:(k + 1) * 128] if k < 4
                         else g["vb"][:, (k - 4) * 128:(k - 3) * 128])
            if k == 0:
                tokw = list(p.tok.get("xnTb", ()))
            nt = op("tensor", lambda e, src_t=src_t: e.transpose(
                g["P_tr"][:, :], src_t, g["identb"][:, :]),
                waits=w, res=("PTR",) + (("xnTb",) if k == 0 else ()), inc=te)
            op("vector", lambda e, k=k: e.tensor_copy(
                g["xnT_tile"][:, k * 128:(k + 1) * 128],
                g["P_tr"][:, :]), waits=[(te, nt)] + (tokw if k == 0 else []),
                inc="trc", rel=("PTR",))
        ntr = p.cnt["trc"]
        p.set_tok(tokD, "trc", ntr)
        p.set_tok(tokV, "trc", ntr)

        for half, nm in ((0, "ag1a_in"), (1, "ag1b_in")):
            src = g["xnT_tile"].rearrange("pp (k c) -> pp k c", k=KD)[
                :, half * 4:(half + 1) * 4, :]
            dst = g[nm].rearrange("(k pp) t -> pp k t", pp=128)[:, :, i * 128:(i + 1) * 128]
            dma("sync", dst, src, inc="ag1w", waits=[("trc", ntr)])
        nag1 = p.cnt["ag1w"]

        lps = "P_y" if i % 2 == 0 else "P_bc1"
        lpr = "PY" if i % 2 == 0 else "PB1"
        for k in range(KD):
            r = (lpr,) if k == 0 else ()
            op("tensor", lambda e, k=k, tsl=tsl, lps=lps: e.matmul(
                g[lps][:, 0:16],
                g["xnT_tile"][:, k * 128:(k + 1) * 128],
                g["wsmall"][:, k * 3 * N + 2 * N:k * 3 * N + 3 * N],
                start=(k == 0), stop=(k == KD - 1)),
                waits=[("trc", ntr), ("wsmc", p.NWSM)], res=r, inc=te)
        nt_lg = p.cnt[te]

        for jj, woff in ((0, 0), (1, N)):
            for k in range(KD):
                r = ("PA0",) if k == 0 else ()
                op("tensor", lambda e, k=k, woff=woff, tsl=tsl: e.matmul(
                    g["P_acc0"][0:16, 0:128],
                    g["wsmall"][:, k * 3 * N + woff:k * 3 * N + woff + N],
                    g["xnT_tile"][:, k * 128:(k + 1) * 128],
                    start=(k == 0), stop=(k == KD - 1)),
                    waits=[("trc", ntr), ("wsmc", p.NWSM)], res=r, inc=te)
            nt = p.cnt[te]
            na = op("vector", lambda e, jj=jj: e.tensor_single_scalar(
                g["bmc_st"][0:16, jj * 128:(jj + 1) * 128], g["P_acc0"][0:16, 0:128],
                g["bb_sb"][:, jj:jj + 1], ALU.add),
                waits=[(te, nt)], res=("bmst%d" % jj,), inc=ve)
            p.set_tok("PA0", ve, na)
            dma("sync", g["ag2_in"][jj * 16:(jj + 1) * 16, i * 128:(i + 1) * 128],
                g["bmc_st"][0:16, jj * 128:(jj + 1) * 128], inc="ag2w", waits=[(ve, na)])
        p.set_tok("bmst0", "ag2w", p.cnt["ag2w"])
        p.set_tok("bmst1", "ag2w", p.cnt["ag2w"])
        p.tok["xnTb"] = [(te, p.cnt[te]), ("ag1w", nag1)]
        return {"i": i, "tsl": tsl, "ntr": ntr, "nag1": nag1, "nt": nt_lg,
                "lps": lps, "lpr": lpr}

    def p1_stageB(st):
        i, tsl, ntr, nag1, nt = st["i"], st["tsl"], st["ntr"], st["nag1"], st["nt"]
        lps, lpr = st["lps"], st["lpr"]
        p.auto = True
        nv = op("vector", lambda e, lps=lps: e.tensor_add(
            g["col"][:, 10:26], g[lps][:, 0:16], g["rb_bc"][:]),
            waits=[(te, nt), ("rbb", 1)], inc=ve)
        p.set_tok(lpr, ve, nv)
        lg = g["col"][:, 10:26]
        sc1 = g["UP"][:, 2186:2202]
        sc2 = g["UP"][:, 2202:2218]
        op("vector", lambda e: e.reduce_max(g["col"][:, 26:27], lg, axis=AX), res=("UPln",), inc=ve)
        op("vector", lambda e: e.tensor_single_scalar(g["ohtok"][:, 0:16], lg, g["col"][:, 26:27], ALU.is_equal), inc=ve)
        op("vector", lambda e: e.scalar_tensor_tensor(sc1, g["ohtok"][:, 0:16], -BIGF, lg, ALU.mult, ALU.add), inc=ve)
        op("vector", lambda e: e.reduce_max(g["col"][:, 27:28], sc1, axis=AX), inc=ve)
        op("vector", lambda e: e.tensor_mul(sc2, g["ohtok"][:, 0:16], g["iotab"][:]), inc=ve)
        op("vector", lambda e, i=i: e.reduce_max(g["packA"][:, 2 * i:2 * i + 1], sc2, axis=AX), inc=ve)
        op("vector", lambda e: e.tensor_single_scalar(g["ohtok"][:, 16:32], sc1, g["col"][:, 27:28], ALU.is_equal), inc=ve)
        op("vector", lambda e: e.tensor_mul(sc2, g["ohtok"][:, 16:32], g["iotab"][:]), inc=ve)
        op("vector", lambda e, i=i: e.reduce_max(g["packA"][:, 2 * i + 1:2 * i + 2], sc2, axis=AX), inc=ve)
        nv = op("vector", lambda e: e.tensor_scalar_mul(g["col"][:, 28:29], g["col"][:, 26:27], -1.0), inc=ve)
        na = op("scalar", lambda e: e.activation(out=sc1, in_=lg, func=AF.Exp, bias=g["col"][:, 28:29]),
                waits=[(ve, nv)], inc=ae)
        op("vector", lambda e: e.reduce_sum(g["col"][:, 29:30], sc1, axis=AX),
           waits=[(ae, na)], inc=ve)
        na = op("scalar", lambda e: e.activation(out=g["col"][:, 30:31], in_=g["col"][:, 27:28],
                                                 func=AF.Exp, bias=g["col"][:, 28:29]), inc=ae)
        op("vector", lambda e: e.reciprocal(g["col"][:, 31:32], g["col"][:, 29:30]),
           waits=[(ae, na)], inc=ve)
        op("vector", lambda e: e.tensor_mul(g["col"][:, 32:33], g["col"][:, 30:31], g["col"][:, 31:32]), inc=ve)
        op("vector", lambda e: e.tensor_add(g["col"][:, 33:34], g["col"][:, 31:32], g["col"][:, 32:33]), inc=ve)
        op("vector", lambda e: e.tensor_scalar_add(g["col"][:, 33:34], g["col"][:, 33:34], 1e-9), inc=ve)
        op("vector", lambda e: e.reciprocal(g["col"][:, 34:35], g["col"][:, 33:34]), inc=ve)
        op("vector", lambda e, i=i: e.tensor_mul(g["packB"][:, 2 * i:2 * i + 1], g["col"][:, 31:32], g["col"][:, 34:35]), inc=ve)
        op("vector", lambda e, i=i: e.tensor_mul(g["packB"][:, 2 * i + 1:2 * i + 2], g["col"][:, 32:33], g["col"][:, 34:35]), inc=ve)

        op("vector", lambda e, i=i: e.tensor_single_scalar(
            g["ohtok"][:, 0:16], g["iotab"][:], g["packA"][:, 2 * i:2 * i + 1], ALU.is_equal), inc=ve)
        nv = op("vector", lambda e, i=i: e.tensor_single_scalar(
            g["ohtok"][:, 32:48], g["iotab"][:], g["packA"][:, 2 * i + 1:2 * i + 2], ALU.is_equal), inc=ve)
        nt = op("tensor", lambda e: e.transpose(
            g["P_bc0"][0:64, 0:128], g["ohtok"][:, 0:64], g["identf"][:, :]),
            waits=[(ve, nv), ("cld", p.NCLD)], res=("PB0",), inc=te)
        ti = i % 2
        ohv = g["oh_c"].rearrange("ee (tt pp kk) -> ee tt pp kk", tt=2, kk=2)
        op("vector", lambda e, ti=ti: e.tensor_copy(ohv[:, ti, :, 0], g["P_bc0"][0:16, 0:128]),
           waits=[(te, nt)], res=("ohc",) if ti == 0 else (), inc=ve)
        nv = op("vector", lambda e, ti=ti: e.tensor_copy(ohv[:, ti, :, 1], g["P_bc0"][32:48, 0:128]),
                waits=[(te, nt)], inc=ve)
        p.set_tok("PB0", ve, nv)
        p.auto = False

        if i % 2 == 1:
            emit_p1_chunk(nc, p, g, i // 2)

    dma("sync", g["xt"][:, 0:D], g["x_own"][0:128, :], res=("xt0",), inc="xld0")
    st = p1_stageA(0)
    for i in range(NTIL):
        st_next = p1_stageA(i + 1) if i + 1 < NTIL else None
        p1_stageB(st)
        st = st_next

    emit_p1_post(nc, p, g)


def emit_p1_chunk(nc, p, g, ch):
    op = p.op
    ve, ae, te = "p1v", "p1a", "p1t"
    p.auto = True
    nv_oh = p.cnt[ve]
    init = 0.0 if ch == 0 else g["carry16"][:, 0:1]
    nv = op("vector", lambda e, init=init: e.tensor_tensor_scan(
        g["cum_c"][:], g["ones16t"][:], g["oh_c"][:], init, ALU.mult, ALU.add),
        waits=[("ones16", 1), (ve, nv_oh)], inc=ve)
    op("vector", lambda e: e.tensor_copy(g["carry16"][:, 0:1], g["cum_c"][:, TCH - 1:TCH]), inc=ve)
    nv = op("vector", lambda e: e.tensor_mul(g["oh_c"][:], g["cum_c"][:], g["oh_c"][:]), inc=ve)
    nt = op("tensor", lambda e: e.matmul(g["P_bc1"][0:1, 0:512], g["ones16c"][:, :],
                                         g["oh_c"][:], start=True, stop=True),
            waits=[(ve, nv), ("cld", p.NCLD)], res=("PB1",), inc=te)
    p.set_tok("ohc", te, nt)
    lr = g["P_bc1"][0:1, 0:512].rearrange("one (tt pp kk) -> one tt pp kk", tt=2, kk=2)
    kr = g["krow"].rearrange("kk (tt pp) -> kk tt pp", tt=2)
    op("vector", lambda e, lr=lr, kr=kr: e.tensor_copy(kr[0:1, :, :], lr[:, :, :, 0]),
       waits=[(te, nt)], inc=ve)
    nv = op("vector", lambda e, lr=lr, kr=kr: e.tensor_copy(kr[32:33, :, :], lr[:, :, :, 1]), inc=ve)
    p.set_tok("PB1", ve, nv)
    for ti in range(2):
        tile = ch * 2 + ti
        nt = op("tensor", lambda e, ti=ti: e.transpose(
            g["P_bc1"][0:128, 100:133], g["krow"][0:33, ti * 128:(ti + 1) * 128],
            g["identf"][0:33, 0:33]), waits=[(ve, nv)], res=("PB1",), inc=te)
        op("vector", lambda e, tile=tile: e.tensor_copy(
            g["lcols"][:, tile * 2:tile * 2 + 1], g["P_bc1"][0:128, 100:101]),
            waits=[(te, nt)], inc=ve)
        nv2 = op("vector", lambda e, tile=tile: e.tensor_copy(
            g["lcols"][:, tile * 2 + 1:tile * 2 + 2], g["P_bc1"][0:128, 132:133]),
            waits=[(te, nt)], inc=ve)
        p.set_tok("PB1", ve, nv2)
    p.auto = False
    if True:
        pass


def emit_p1_post(nc, p, g):
    op, dma = p.op, p.dma
    ve, ae, te = "p1v", "p1a", "p1t"

    nv = p.cnt[ve]
    dma("sync", g["ag2_in"][32:33, 0:16], g["carry16"][:, 0:1], inc="ag2c",
        waits=[(ve, nv)])

    op("gpsimd", lambda e: e.collective_compute(
        "AllGather", ALU.bypass, replica_groups=[list(range(NCORE))],
        ins=[g["ag1a_in"].tensor.ap().opt()], outs=[g["ag1a_out"].tensor.ap().opt()]),
        waits=[("ag1w", p.cnt["ag1w"])], inc="cc")
    op("gpsimd", lambda e: e.collective_compute(
        "AllGather", ALU.bypass, replica_groups=[list(range(NCORE))],
        ins=[g["ag1b_in"].tensor.ap().opt()], outs=[g["ag1b_out"].tensor.ap().opt()]),
        inc="cc")
    op("gpsimd", lambda e: e.collective_compute(
        "AllGather", ALU.bypass, replica_groups=[list(range(NCORE))],
        ins=[g["ag2_in"].tensor.ap().opt()], outs=[g["ag2_out"].tensor.ap().opt()]),
        waits=[("ag2w", p.cnt["ag2w"]), ("ag2c", 16)], inc="cc")

    p.auto = True
    dma("sync", g["tot8"][:],
        g["ag2_out"].rearrange("(blk r) t -> blk (r t)", r=34)[:, 32 * TPC:32 * TPC + 16],
        inc="t8l", waits=[("cc", 3)])
    nt = op("tensor", lambda e: e.matmul(g["P_bc1"][0:16, 0:1], g["tot8"][:, :],
                                         g["mlt8"][:, :], start=True, stop=True),
            waits=[("t8l", 16)], res=("PB1",), inc=te)
    na = op("scalar", lambda e: e.activation(out=g["off16"][:], in_=g["P_bc1"][0:16, 0:1],
                                             func=AF.Copy), waits=[(te, nt)], inc=ae)
    nt = op("tensor", lambda e: e.transpose(g["P_bc1"][0:1, 4:20], g["off16"][:, :],
                                            g["identf"][0:16, 0:16]),
            waits=[(ae, na)], inc=te)
    na = op("scalar", lambda e: e.activation(out=g["col"][0:1, 0:16], in_=g["P_bc1"][0:1, 4:20],
                                             func=AF.Copy), waits=[(te, nt)], inc=ae)
    nt = op("tensor", lambda e: e.matmul(g["P_bc1"][:, 4:20], g["onesr"][:, :],
                                         g["col"][0:1, 0:16], start=True, stop=True),
            waits=[(ae, na)], inc=te)
    nv = op("vector", lambda e: e.tensor_copy(g["off_bc"][:], g["P_bc1"][:, 4:20]),
            waits=[(te, nt)], inc=ve)
    p.set_tok("PB1", ve, nv)

    for sx in range(2 * NTIL):
        op("vector", lambda e, sx=sx: e.tensor_single_scalar(
            g["ohtok"][:, 0:16], g["iotab"][:], g["packA"][:, sx:sx + 1], ALU.is_equal), inc=ve)
        op("vector", lambda e: e.tensor_mul(g["UP"][:, 128:144], g["ohtok"][:, 0:16], g["off_bc"][:]), inc=ve)
        op("vector", lambda e, sx=sx: e.reduce_sum(g["UP"][:, sx:sx + 1], g["UP"][:, 128:144], axis=AX), inc=ve)
        op("vector", lambda e: e.tensor_mul(g["UP"][:, 144:160], g["ohtok"][:, 0:16],
                                            g["ohtok"][:, 48:64]), inc=ve)
        op("vector", lambda e, sx=sx: e.reduce_sum(g["UP"][:, 32 + sx:32 + sx + 1],
                                                   g["UP"][:, 144:160], axis=AX), inc=ve)
    NS2 = 2 * NTIL
    offc = g["UP"][:, 0:NS2]
    parc = g["UP"][:, 32:32 + NS2]
    t1 = g["UP"][:, 64:64 + NS2]
    t2 = g["UP"][:, 160:160 + NS2]
    t3 = g["UP"][:, 192:192 + NS2]
    vld = g["UP"][:, 224:224 + NS2]
    op("vector", lambda e, offc=offc: e.tensor_add(g["pcols"][:], g["lcols"][:], offc), inc=ve)
    op("vector", lambda e, vld=vld: e.tensor_single_scalar(
        vld, g["pcols"][:], float(CAP), ALU.is_le), inc=ve)
    op("vector", lambda e, vld=vld: e.tensor_mul(g["rcols"][:], g["packB"][:], vld), inc=ve)
    op("vector", lambda e, parc=parc: e.tensor_mul(g["rcols2"][:], g["rcols"][:], parc), inc=ve)
    op("vector", lambda e, t1=t1, parc=parc: e.tensor_scalar(
        t1, parc, -1.0, 1.0, ALU.mult, ALU.add), inc=ve)
    op("vector", lambda e, t1=t1: e.tensor_mul(g["rcols"][:], g["rcols"][:], t1), inc=ve)
    op("vector", lambda e, t1=t1, parc=parc: e.tensor_sub(t1, g["packA"][:], parc), inc=ve)
    op("vector", lambda e, t1=t1: e.tensor_scalar_mul(t1, t1, float(CAP // 2)), inc=ve)
    op("vector", lambda e, t1=t1: e.tensor_add(t1, t1, g["pcols"][:]), inc=ve)
    op("vector", lambda e, t1=t1: e.tensor_scalar_add(t1, t1, -1.0), inc=ve)
    op("vector", lambda e, t1=t1: e.tensor_scalar_min(t1, t1, float(NCORE * CAP - 1)), inc=ve)
    op("vector", lambda e, t1=t1: e.tensor_copy(g["ucols"][:, 0:NS2], t1), inc=ve)
    op("vector", lambda e, t2=t2: e.tensor_scalar_mul(t2, g["packA"][:], float(CAP)), inc=ve)
    op("vector", lambda e, t2=t2: e.tensor_add(t2, t2, g["pcols"][:]), inc=ve)
    op("vector", lambda e, t2=t2: e.tensor_scalar_add(t2, t2, -1.0), inc=ve)
    op("vector", lambda e, t2=t2, vld=vld: e.tensor_mul(t2, t2, vld), inc=ve)
    op("vector", lambda e, t3=t3, vld=vld: e.tensor_scalar(
        t3, vld, -1.0, 1.0, ALU.mult, ALU.add), inc=ve)
    op("vector", lambda e, t3=t3: e.tensor_single_scalar(
        t3, t3, g["trash_sb"][:, 0:1], ALU.mult), inc=ve)
    op("vector", lambda e, t2=t2, t3=t3: e.tensor_add(t2, t2, t3), inc=ve)
    op("vector", lambda e, t2=t2: e.tensor_copy(g["ucols"][:, NS2:2 * NS2], t2), inc=ve)

    p.auto = False
    NV = p.cnt[ve]
    for i in range(NTIL):
        for kk in range(2):
            sx = 2 * i + kk
            op("gpsimd", lambda e, i=i, sx=sx: e.indirect_dma_start(
                out=g["ar3_in"][:, :],
                out_offset=IndirectOffsetOnAxis(ap=g["ucols"][:, 32 + sx:32 + sx + 1], axis=0),
                in_=g["tok1_sb"][:, i:i + 1], in_offset=None,
                bounds_check=p.breg(e, TBL - 1), oob_is_err=False),
                waits=[("p1v", NV), ("ar3z", 16), ("sct", 16 * sx)], inc="sct", amt=16)
    op("gpsimd", lambda e: e.collective_compute(
        "AllReduce", ALU.add, replica_groups=[list(range(NCORE))],
        ins=[g["ar3_in"].tensor.ap().opt()], outs=[g["ar3_out"].tensor.ap().opt()]),
        waits=[("sct", p.cnt["sct"])], inc="cc")



def emit_p2(nc, p, g):
    op, dma = p.op, p.dma
    ve, ae, te, pl = "p2v", "p2a", "p2t", "p2p"

    def p2_loads(it):
        blk, tw = it // 4, it % 4
        sl = it % 2
        for k in range(KD):
            src = g["ag1a_out"] if k < 4 else g["ag1b_out"]
            kk = k % 4
            w = [("cc", 2)] if it < 2 else []
            dma("sync", g["xnTc"][:, (sl * KD + k) * TCH:(sl * KD + k + 1) * TCH],
                src[blk * 512 + kk * 128:blk * 512 + (kk + 1) * 128, tw * TCH:(tw + 1) * TCH],
                waits=w, res=("xTc%d" % sl,) if k == 0 else (), inc="xTl%d" % sl)
        dma("sync", g["bmc_st"][0:16, 0:TCH],
            g["ag2_out"][blk * 34:blk * 34 + 16, tw * TCH:(tw + 1) * TCH],
            waits=[("cc", 3)], res=("bmst0", "bmst1"), inc="bmLl")

    def emit_head_pe(it):
        sl = it % 2
        nxl = 128 * (it // 2 + 1)
        for k in range(KD):
            r = ("PA0",) if k == 0 else ()
            op("tensor", lambda e, k=k, sl=sl: e.matmul(
                g["P_acc0"][:, :], g["wsl"][:, k * 2 * DSL:k * 2 * DSL + DSL],
                g["xnTc"][:, (sl * KD + k) * TCH:(sl * KD + k + 1) * TCH],
                start=(k == 0), stop=(k == KD - 1)),
                waits=[("xTl%d" % sl, nxl), ("wslc", p.NWSL)] if k == 0 else (), res=r, inc=te)
        n_dt = p.cnt[te]
        for k in range(KD):
            r = ("PA1",) if k == 0 else ()
            op("tensor", lambda e, k=k, sl=sl: e.matmul(
                g["P_acc1"][:, :], g["wsl"][:, k * 2 * DSL + DSL:(k + 1) * 2 * DSL],
                g["xnTc"][:, (sl * KD + k) * TCH:(sl * KD + k + 1) * TCH],
                start=(k == 0), stop=(k == KD - 1)),
                waits=[("xTl%d" % sl, nxl)] if k == 0 else (), res=r, inc=te)
        n_xs = p.cnt[te]
        p.set_tok("xTc%d" % sl, te, n_xs)
        return n_dt, n_xs

    def emit_head_act(it, n_dt, n_xs, w0=()):
        if it % 2 == 0:
            dtb_ap, vb_ap, tokD, tokV = g["dtb"][:], g["vb"][:], "dtb", "vb"
        else:
            dtb_ap, vb_ap = g["m3"][:, 0:512], g["m3"][:, 512:1024]
            tokD, tokV = "m3a0", "m3b0"
        u1 = g["UP"][:, 1024:1536]
        u2 = g["UP"][:, 1536:2048]
        nab = op("scalar", lambda e: e.activation(out=u1, in_=g["P_acc0"][:, :], func=AF.Abs,
                                                  bias=g["bias_sb"][:, 9:10]),
                 waits=[(te, n_dt)] + list(w0), res=("UPsp",), inc=ae)
        nab = op("scalar", lambda e: e.activation(out=u2, in_=u1, func=AF.Exp, scale=-1.0),
                 waits=[(ae, nab)], inc=ae)
        nab = op("scalar", lambda e: e.activation(out=u1, in_=u2, func=AF.Ln,
                                                  bias=g["fconst"][:, 1:2]),
                 waits=[(ae, nab)], res=("fc",), inc=ae)
        na = op("scalar", lambda e: e.activation(out=g["dtf"][:], in_=g["P_acc0"][:, :],
                                                 func=AF.Relu, bias=g["bias_sb"][:, 9:10]),
                waits=[(ae, nab)], res=("dtf",), inc=ae)
        p.set_tok("PA0", ae, na)
        nv = op("vector", lambda e: e.tensor_add(g["dtf"][:], g["dtf"][:], u1),
                waits=[(ae, na)], inc=ve)
        p.set_tok("UPsp", ve, nv)
        nv = op("vector", lambda e: e.scalar_tensor_tensor(
            g["vf"][:], g["P_acc1"][:, :], g["bias_sb"][:, 8:9], g["dtf"][:], ALU.add, ALU.mult),
            waits=[(te, n_xs), (ve, nv)], res=("vf",), inc=ve)
        p.set_tok("PA1", ve, nv)
        na_dt = op("scalar", lambda e, dtb_ap=dtb_ap: e.activation(
            out=dtb_ap, in_=g["dtf"][:], func=AF.Copy),
            waits=[(ve, nv - 1)], res=(tokD,), inc=ae)
        p.set_tok("dtf", ae, na_dt)
        na_v = op("scalar", lambda e, vb_ap=vb_ap: e.activation(
            out=vb_ap, in_=g["vf"][:], func=AF.Copy),
            waits=[(ve, nv)], res=(tokV,), inc=ae)
        p.set_tok("vf", ae, na_v)
        return {"na_dt": na_dt, "na_v": na_v, "dtb_ap": dtb_ap, "vb_ap": vb_ap,
                "tokD": tokD, "tokV": tokV}

    p2_loads(0)
    w0 = [("p1v", p.cnt["p1v"]), ("p1a", p.cnt["p1a"]), ("p1t", p.cnt["p1t"]),
          ("p1p", p.cnt.get("p1p", 0))]
    hs = emit_head_act(0, *emit_head_pe(0), w0=w0)
    for it in range(NIT):
        blk, tw = it // 4, it % 4
        first_of_b = (it % 16) == 0
        sl = it % 2
        nbm = 16 * (it + 1)
        na_dt, na_v = hs["na_dt"], hs["na_v"]
        dtb_ap, vb_ap = hs["dtb_ap"], hs["vb_ap"]

        if it == 0:
            na_b = op("scalar", lambda e: e.activation(
                out=g["bmcb"][0:16, 0:TCH], in_=g["bmc_st"][0:16, 0:TCH], func=AF.Copy),
                waits=[("bmLl", nbm)], res=("bmcb",), inc=ae)
            dma("sync", g["bmc_st"][0:16, 0:TCH],
                g["ag2_out"][blk * 34 + 16:blk * 34 + 32, tw * TCH:(tw + 1) * TCH],
                waits=[("p2a", na_b)], inc="bmCl")
            hb = {"na_b": na_b, "nbm2": p.cnt["bmCl"]}
        na_b, nbm2 = hb["na_b"], hb["nbm2"]
        nt = op("tensor", lambda e: e.matmul(g["P_bc0"][:, :], g["bc16"][:, :],
                                             g["bmcb"][0:16, 0:TCH], start=True, stop=True),
                waits=[(ae, na_b), ("cld", p.NCLD)], res=("PB0",), inc=te)
        nv1 = op("vector", lambda e: e.tensor_copy(g["bm_bc"][:], g["P_bc0"][:, :]),
                 waits=[(te, nt)], res=("bmbc",), inc=ve)
        p.set_tok("PB0", ve, nv1)
        na_c = op("scalar", lambda e: e.activation(out=g["bmcb"][0:16, 0:TCH],
                                                   in_=g["bmc_st"][0:16, 0:TCH], func=AF.Copy),
                  waits=[("bmCl", nbm2), (te, nt)], inc=ae)
        p.set_tok("bmst0", ae, na_c)
        p.set_tok("bmst1", ae, na_c)
        p.set_tok("bmcb", ae, na_c)
        nt = op("tensor", lambda e: e.matmul(g["P_bc0"][:, :], g["bc16"][:, :],
                                             g["bmcb"][0:16, 0:TCH], start=True, stop=True),
                waits=[(ae, na_c)], res=("PB0",), inc=te)
        nv2 = op("vector", lambda e: e.tensor_copy(g["c_bc"][:], g["P_bc0"][:, :]),
                 waits=[(te, nt)], res=("cbc",), inc=ve)
        p.set_tok("PB0", ve, nv2)
        p.also_rel("bmcb", "p2t", nt)
        if it + 1 < NIT:
            p2_loads(it + 1)

        order = [s4 * 4 + q for q in range(4) for s4 in range(4)]
        last_ym = None
        pend = []

        def emit_yred(gi, gg, np_):
            r = ("PY",) if gi == 0 else ()
            ycv = g["yc_sb"][:, 0:TCH] if gi % 2 == 0 else g["m3"][:, 16384:16384 + TCH]
            nty = op("tensor", lambda e, gg=gg, gi=gi, ycv=ycv: e.matmul(
                g["P_y"][:, :], g["nsm"][:, gg * 128:(gg + 1) * 128], ycv,
                start=(gi == 0), stop=(gi == 15)),
                waits=[(pl, np_)], res=r, inc=te)
            p.set_tok("yc%d" % (gi % 2), te, nty)

        for gi, gg in enumerate(order):
            gp = gi % 2
            av = g["a_sb"][:, gp * TCH:(gp + 1) * TCH]
            xv = g["xdb_sb"][:, gp * TCH:(gp + 1) * TCH]
            hv = g["h_sb"][:, gp * TCH:(gp + 1) * TCH]

            bA = "P_bc1" if gp == 0 else "P_acc3"
            bAr = "PB1" if gp == 0 else "PA3"
            bV = "P_bc0" if gp == 0 else "P_acc2"
            bVr = "PB0" if gp == 0 else "PA2"
            nt_a = op("tensor", lambda e, gg=gg, bA=bA, dtb_ap=dtb_ap: e.matmul(
                g[bA][:, :], g["repm"][:, gg * 128:(gg + 1) * 128], dtb_ap,
                start=True, stop=True), waits=[(ae, na_dt)], res=(bAr,), inc=te)
            na2 = op("scalar", lambda e, gg=gg, av=av, bA=bA: e.activation(
                out=av, in_=g[bA][:, :], func=AF.Exp, scale=g["acols"][:, gg:gg + 1]),
                waits=[(te, nt_a), ("acol", 2)], res=("a%d" % gp,), inc=ae)
            p.set_tok(bAr, ae, na2)

            nt_v = op("tensor", lambda e, gg=gg, bV=bV, vb_ap=vb_ap: e.matmul(
                g[bV][:, :], g["repm"][:, gg * 128:(gg + 1) * 128], vb_ap,
                start=True, stop=True), waits=[(ae, na_v)], res=(bVr,), inc=te)
            nv3 = op("vector", lambda e, xv=xv, bV=bV: e.tensor_mul(xv, g[bV][:, :], g["bm_bc"][:]),
                     waits=[(te, nt_v), (ve, nv1)], res=("x%d" % gp,), inc=ve)
            p.set_tok(bVr, ve, nv3)
            if gi >= 2:
                emit_yred(gi - 2, order[gi - 2], pend[gi - 2])

            init = 0.0 if first_of_b else g["carryg"][:, gg:gg + 1]
            nv4 = op("vector", lambda e, av=av, xv=xv, hv=hv, init=init: e.tensor_tensor_scan(
                hv, av, xv, init, ALU.mult, ALU.add),
                waits=[(ae, na2), (ve, nv3)], res=("h%d" % gp,), inc=ve)
            p.set_tok("a%d" % gp, ve, nv4)
            p.set_tok("x%d" % gp, ve, nv4)
            nv5 = op("vector", lambda e, gg=gg, hv=hv: e.tensor_copy(
                g["carryg"][:, gg:gg + 1], hv[:, TCH - 1:TCH]),
                waits=[(ve, nv4)], inc=ve)

            ycv = g["yc_sb"][:, 0:TCH] if gp == 0 else g["m3"][:, 16384:16384 + TCH]
            np_ = op("gpsimd", lambda e, hv=hv, ycv=ycv: e.tensor_mul(ycv, hv, g["c_bc"][:]),
                     waits=[(ve, nv5), (ve, nv2)], res=("yc%d" % gp,), inc=pl)
            p.set_tok("h%d" % gp, pl, np_)
            pend.append(np_)
            last_ym = np_
            if gi == 9 and it + 1 < NIT:
                ndx_next = emit_head_pe(it + 1)
            if gi == 13 and it + 1 < NIT:
                na_b2 = op("scalar", lambda e: e.activation(
                    out=g["bmcb"][0:16, 0:TCH], in_=g["bmc_st"][0:16, 0:TCH],
                    func=AF.Copy),
                    waits=[("bmLl", 16 * (it + 2))], res=("bmcb",), inc=ae)
                blk2, tw2 = (it + 1) // 4, (it + 1) % 4
                dma("sync", g["bmc_st"][0:16, 0:TCH],
                    g["ag2_out"][blk2 * 34 + 16:blk2 * 34 + 32,
                                 tw2 * TCH:(tw2 + 1) * TCH],
                    waits=[("p2a", na_b2)], inc="bmCl")
                hb = {"na_b": na_b2, "nbm2": p.cnt["bmCl"]}
        if it + 1 < NIT:
            hs_next = emit_head_act(it + 1, *ndx_next)
        emit_yred(14, order[14], pend[14])
        emit_yred(15, order[15], pend[15])
        p.set_tok("cbc", pl, last_ym)
        p.set_tok("bmbc", ve, p.cnt[ve])
        p.set_tok(hs["tokD"], te, p.cnt[te])
        p.set_tok(hs["tokV"], te, p.cnt[te])
        if it + 1 < NIT:
            hs = hs_next
        nt_y = p.cnt[te]
        na = op("scalar", lambda e: e.activation(out=g["yev"][:], in_=g["P_y"][:, :], func=AF.Copy),
                waits=[(te, nt_y)], res=("yev",), inc=ae)
        p.set_tok("PY", ae, na)
        dma("sync",
            g["ag4_in"].rearrange("(dp blk) t -> dp blk t", blk=8)[:, blk, tw * TCH:(tw + 1) * TCH],
            g["yev"][:], inc="ag4w", waits=[(ae, na)])
        p.set_tok("yev", "ag4w", p.cnt["ag4w"])

    op("gpsimd", lambda e: e.collective_compute(
        "AllGather", ALU.bypass, replica_groups=[list(range(NCORE))],
        ins=[g["ag4_in"].tensor.ap().opt()], outs=[g["ag4_out"].tensor.ap().opt()]),
        waits=[("ag4w", p.cnt["ag4w"])], inc="cc")





def mlp_block(p, g, rhs_fn, waits1, waits2, waits3, out_cb, inject=None):
    op = p.op
    ae, te = "p3a", "p3t"
    m3 = g["m3"]
    npl_list = []
    h1o, h2o, mto = 0, FCH * SCH, 2 * FCH * SCH
    for f in range(FCH):
        fb = f % 2
        b1 = "P_acc0" if fb == 0 else "P_acc2"
        b1r = "PA0" if fb == 0 else "PA2"
        b2 = "P_acc1" if fb == 0 else "P_acc3"
        b2r = "PA1" if fb == 0 else "PA3"
        for k in range(KD):
            r = (b1r,) if k == 0 else ()
            op("tensor", lambda e, f=f, k=k, b1=b1: e.matmul(
                g[b1][:, :], g["wbig"][:, k * FF + f * 128:k * FF + (f + 1) * 128],
                rhs_fn(k), start=(k == 0), stop=(k == KD - 1)),
                waits=waits1 if k == 0 else (), res=r, inc=te)
        nt1 = p.cnt[te]
        na1 = op("scalar", lambda e, f=f, b1=b1: e.activation(
            out=m3[:, h1o + f * SCH:h1o + (f + 1) * SCH], in_=g[b1][:, :],
            func=AF.Silu), waits=[(te, nt1)], res=("m3a%d" % f,), inc=ae)
        p.set_tok(b1r, ae, na1)
        for k in range(KD):
            r = (b2r,) if k == 0 else ()
            op("tensor", lambda e, f=f, k=k, b2=b2: e.matmul(
                g[b2][:, :], g["wbig2"][:, k * FF + f * 128:k * FF + (f + 1) * 128],
                rhs_fn(k), start=(k == 0), stop=(k == KD - 1)),
                waits=waits2 if k == 0 else (), res=r, inc=te)
        nt2 = p.cnt[te]
        if f == FCH - 1:
            n_mm1 = nt1
            n_mm2 = nt2
        na2 = op("scalar", lambda e, f=f, b2=b2: e.activation(
            out=m3[:, h2o + f * SCH:h2o + (f + 1) * SCH], in_=g[b2][:, :],
            func=AF.Copy), waits=[(te, nt2)], res=("m3b%d" % f,), inc=ae)
        p.set_tok(b2r, ae, na2)
        npl = op("gpsimd", lambda e, f=f: e.tensor_mul(
            m3[:, mto + f * SCH:mto + (f + 1) * SCH],
            m3[:, h1o + f * SCH:h1o + (f + 1) * SCH],
            m3[:, h2o + f * SCH:h2o + (f + 1) * SCH]),
            waits=[(ae, na1), (ae, na2)], res=("m3m%d" % f,), inc="p3p")
        p.set_tok("m3a%d" % f, "p3p", npl)
        p.set_tok("m3b%d" % f, "p3p", npl)
        npl_list.append(npl)
        if inject is not None:
            inject(f)
    n_mm2 = p.cnt[te]
    for ss in range(4):
        for dh in range(2):
            ob = (ss * 2 + dh) % 2
            b3 = "P_bc0" if ob == 0 else "P_bc1"
            b3r = "PB0" if ob == 0 else "PB1"
            for f in range(FCH):
                w = (list(waits3) if f == 0 else []) + [("p3p", npl_list[f])]
                r = (b3r,) if f == 0 else ()
                op("tensor", lambda e, ss=ss, dh=dh, f=f, b3=b3: e.matmul(
                    g[b3][:, :],
                    m3[:, mto + f * SCH + ss * 128:mto + f * SCH + (ss + 1) * 128],
                    g["wdown"][:, f * D + dh * 512:f * D + (dh + 1) * 512],
                    start=(f == 0), stop=(f == FCH - 1)), waits=w, res=r, inc=te)
            out_cb(ss, dh, p.cnt[te], b3, b3r)
            if inject is not None:
                inject(FCH + ss * 2 + dh)
    for f in range(FCH):
        p.set_tok("m3m%d" % f, te, p.cnt[te])
    return n_mm1, n_mm2, p.cnt[te]


def emit_p3(nc, p, g):
    op, dma = p.op, p.dma
    ve, ae, te, pl = "p3v", "p3a", "p3t", "p3p"
    NAG1 = p.cnt["ag1w"]

    for tc in range(4):
        sl = tc % 2
        for k in range(KD):
            src = g["ag1a_in"] if k < 4 else g["ag1b_in"]
            kk = k % 4
            dma("sync", g["xnTc"][:, (sl * KD + k) * TCH:(sl * KD + k + 1) * TCH],
                src[kk * 128:(kk + 1) * 128, tc * TCH:(tc + 1) * TCH],
                waits=[("ag1w", NAG1)], res=("xTc%d" % sl,) if k == 0 else (),
                inc="zxl%d" % sl)
        nxl = p.cnt["zxl%d" % sl]
        for dc in range(KD):
            dp = dc % 2
            zps = "P_acc0" if dp == 0 else "P_acc1"
            zpr = "PA0" if dp == 0 else "PA1"
            zev = g["yev"][:] if dp == 0 else g["dtb"][:]
            zet = "yev" if dp == 0 else "dtb"
            zct = "zst%d" % dp
            for k in range(KD):
                r = (zpr,) if k == 0 else ()
                op("tensor", lambda e, dc=dc, k=k, sl=sl, zps=zps: e.matmul(
                    g[zps][:, :], g["wbig2"][:, k * D + dc * 128:k * D + (dc + 1) * 128],
                    g["xnTc"][:, (sl * KD + k) * TCH:(sl * KD + k + 1) * TCH],
                    start=(k == 0), stop=(k == KD - 1)),
                    waits=[("zxl%d" % sl, nxl), ("wzc", p.NWZ)], res=r, inc=te)
            nt = p.cnt[te]
            na = op("scalar", lambda e, dc=dc, zev=zev, zps=zps: e.activation(
                out=zev, in_=g[zps][:, :], func=AF.Sigmoid,
                bias=g["bias_sb"][:, dc:dc + 1]),
                waits=[(te, nt)], res=(zet,), inc=ae)
            p.set_tok(zpr, ae, na)
            dma("sync", g["z_dram"][dc * 128:(dc + 1) * 128, tc * TCH:(tc + 1) * TCH],
                zev, inc=zct, waits=[(ae, na)])
            p.set_tok(zet, zct, p.cnt[zct])
        p.set_tok("xTc%d" % sl, te, p.cnt[te])
    p.set_tok("wbig2b", te, p.cnt[te])

    dma("sync", g["col"][:, 40:41], g["erow"][:, :], inc="erl")
    p.auto = True
    op("vector", lambda e: e.tensor_copy(g["erowu"][:], g["col"][:, 40:41]),
       waits=[("erl", 16)], inc=ve)
    op("gpsimd", lambda e: e.indirect_dma_start(
        out=g["tid"][:, :], out_offset=None,
        in_=g["ar3_out"].rearrange("(r pp) one -> r (pp one)", pp=128),
        in_offset=IndirectOffsetOnAxis(ap=g["erowu"][:, 0:1], axis=0),
        bounds_check=p.breg(e, TBL // 128 - 1), oob_is_err=False),
        waits=[("cc", 4), ("p3v", p.cnt[ve])], inc="tgl", amt=16)
    nt = op("tensor", lambda e: e.transpose(g["P_bc1"][:, 0:128], g["tid"][:, :],
                                            g["identf"][:, :]),
            waits=[("tgl", 16)], res=("PB1",), inc=te)
    op("vector", lambda e: e.tensor_copy(g["tid"][:, :], g["P_bc1"][:, 0:128]),
       waits=[(te, nt)], inc=ve)
    p.set_tok("PB1", ve, p.cnt[ve])
    op("vector", lambda e: e.tensor_scalar_add(g["tid"][:, :], g["tid"][:, :], -1.0), inc=ve)
    op("vector", lambda e: e.tensor_scalar_max(g["tid"][:, :], g["tid"][:, :], 0.0), inc=ve)
    nv_tid = op("vector", lambda e: e.tensor_copy(g["tidu"][:], g["tid"][:, :]), inc=ve)
    p.auto = False

    chunks = [(e01, sc) for e01 in range(2) for sc in range(NSC)]
    ln_na = {}
    trc_after = {}

    def prep_unit(ci, part):
        e01, sc = chunks[ci]
        pp = ci % 2
        base = pp * KD * TCH
        ss = part // 3
        ph = part % 3
        xsl = ss % 2
        xtv = g["xt"][:, xsl * D:(xsl + 1) * D]
        if ph == 0:
            j = e01 * RPE + sc * 4 + ss
            op("gpsimd", lambda e, j=j, xtv=xtv: e.indirect_dma_start(
                out=xtv, out_offset=None,
                in_=g["x_full"][:, :],
                in_offset=IndirectOffsetOnAxis(ap=g["tidu"][:, j:j + 1], axis=0),
                bounds_check=p.breg(e, T - 1), oob_is_err=False),
                waits=[("p3v", nv_tid)], res=("xt%d" % xsl,), inc="xg%d" % xsl, amt=16)
            nxg = p.cnt["xg%d" % xsl]
            if ss == 0:
                bfd = [(g["xnb"][:], 0, D, ("xnb",))]
            elif ss == 2:
                bfd = [(g["yev"][:], 0, 512, ("yev",)),
                       (g["yc_sb"][:], 512, D, ("yc0",))]
            elif ss == 3:
                bfd = [(g["xnT_tile"][:], 0, D, ("xnTb",))]
            else:
                bfd = [(g["dtb"][:], 0, 512, ("dtb",)), (g["vb"][:], 512, D, ("vb",))]
            nv, na = emit_ln(p, g, xtv, ve, ae, waits0=[("xg%d" % xsl, nxg)], bf_dests=bfd)
            p.set_tok("xt%d" % xsl, ve, nv)
            ln_na[(ci, ss)] = na
        else:
            na = ln_na[(ci, ss)]
            ks = range(4) if ph == 1 else range(4, 8)
            for k in ks:
                first = (ph == 1 and ss == 0 and k == 0)
                r = ("PTR",) + ((("xeTb", "xTc1") if pp == 1 else ("xTc0",)) if first else ())
                if ss == 0:
                    src_t = g["xnb"][:, k * 128:(k + 1) * 128]
                elif ss == 2:
                    src_t = (g["yev"][:, k * 128:(k + 1) * 128] if k < 4
                             else g["yc_sb"][:, (k - 4) * 128:(k - 3) * 128])
                elif ss == 3:
                    src_t = g["xnT_tile"][:, k * 128:(k + 1) * 128]
                else:
                    src_t = (g["dtb"][:, k * 128:(k + 1) * 128] if k < 4
                             else g["vb"][:, (k - 4) * 128:(k - 3) * 128])
                nt = op("tensor", lambda e, src_t=src_t: e.transpose(
                    g["P_tr"][:, :], src_t, g["identb"][:, :]),
                    waits=[(ae, na)], res=r, inc=te)
                op("vector", lambda e, k=k, ss=ss, base=base: e.tensor_copy(
                    g["xnTc"][:, base + k * SCH + ss * 128:base + k * SCH + (ss + 1) * 128],
                    g["P_tr"][:, :]), waits=[(te, nt)], inc="p3trc", rel=("PTR",))
            if ph == 2 and ss == 3:
                trc_after[ci] = p.cnt["p3trc"]
            if ph == 2:
                if ss == 0:
                    p.set_tok("xnb", "p3trc", p.cnt["p3trc"])
                elif ss == 2:
                    p.set_tok("yev", "p3trc", p.cnt["p3trc"])
                    p.set_tok("yc0", "p3trc", p.cnt["p3trc"])
                elif ss == 3:
                    p.set_tok("xnTb", "p3trc", p.cnt["p3trc"])
                else:
                    p.set_tok("dtb", "p3trc", p.cnt["p3trc"])
                    p.set_tok("vb", "p3trc", p.cnt["p3trc"])

    for part in range(12):
        prep_unit(0, part)
    prep_unit(1, 0)
    prep_unit(1, 3)
    prep_unit(1, 6)
    prep_unit(1, 9)

    isched = {}
    for X in range(1, len(chunks)):
        if X == 1:
            locs = {1: (0, 0), 2: (0, 1), 4: (0, 2), 5: (0, 3),
                    7: (0, 4), 8: (0, 5), 10: (0, 6), 11: (0, 7)}
        else:
            locs = {0: (X - 2, 10), 3: (X - 2, 11), 6: (X - 2, 12),
                    9: (X - 2, 13),
                    1: (X - 1, 0), 2: (X - 1, 1), 4: (X - 1, 2), 5: (X - 1, 3),
                    7: (X - 1, 4), 8: (X - 1, 5),
                    10: (X - 1, 6), 11: (X - 1, 7)}
        for pt, loc in locs.items():
            isched[loc] = (X, pt)

    wN = {}
    for ci, (e01, sc) in enumerate(chunks):
        pp = ci % 2
        base = pp * KD * TCH
        if sc == 0:
            wsl_e = WSLOTS2 if e01 == 0 else WSLOTS5
            wN[e01] = (
                cast_w(p, g, [(g["wg_sl"][e01, k * 128:(k + 1) * 128, :], FF) for k in range(KD)],
                       lambda j, w: g["wbig"][:, j * FF:j * FF + w], "wg%d" % e01,
                       extra_res=("wbigb",), slots=wsl_e),
                cast_w(p, g, [(g["wu_sl"][e01, k * 128:(k + 1) * 128, :], FF) for k in range(KD)],
                       lambda j, w: g["wbig2"][:, j * FF:j * FF + w], "wu%d" % e01,
                       extra_res=("wbig2b",), slots=wsl_e),
                cast_w(p, g, [(g["wd_sl"][e01, f * 128:(f + 1) * 128, :], D) for f in range(FCH)],
                       lambda j, w: g["wdown"][:, j * D:j * D + w], "wd%d" % e01,
                       extra_res=("wdownb",), slots=wsl_e))
        NWG, NWU, NWD = wN[e01]
        waits1 = [("p3trc", trc_after[ci]), ("wg%dc" % e01, NWG)]
        waits2 = [("p3trc", trc_after[ci]), ("wu%dc" % e01, NWU)]
        waits3 = [("wd%dc" % e01, NWD)]

        def inject(slot, ci=ci):
            tgt = isched.get((ci, slot))
            if tgt is not None:
                prep_unit(tgt[0], tgt[1])

        def out_cb(ss, dh, nt, b3, b3r, e01=e01, sc=sc):
            na = p.op("scalar", lambda e, dh=dh, b3=b3: e.activation(
                out=g["eout"][:, dh * 512:(dh + 1) * 512], in_=g[b3][:, :],
                func=AF.Copy), waits=[("p3t", nt)], res=("eoutb",), inc="p3a")
            p.set_tok(b3r, "p3a", na)
            if dh == 1:
                tbl = g["ag5a_in"] if e01 == 0 else g["ag5b_in"]
                tw5 = "ag5wa" if e01 == 0 else "ag5wb"
                p.dma("sync",
                      tbl[sc * 512 + ss * 128:sc * 512 + (ss + 1) * 128, :],
                      g["eout"][:, :], inc=tw5, waits=[("p3a", na)])
                p.set_tok("eoutb", tw5, p.cnt[tw5])

        en1, en2, en3 = mlp_block(
            p, g, lambda k, base=base: g["xnTc"][:, base + k * SCH:base + (k + 1) * SCH],
            waits1, waits2, waits3, out_cb, inject)
        if pp == 1:
            p.set_tok("xeTb", te, en2)
            p.set_tok("xTc1", te, en2)
        else:
            p.set_tok("xTc0", te, en2)
        if sc == NSC - 1:
            p.set_tok("wbigb", te, en1)
            p.set_tok("wbig2b", te, en2)
            p.set_tok("wdownb", te, en3)
            if e01 == 0:
                op("gpsimd", lambda e: e.collective_compute(
                    "AllGather", ALU.bypass, replica_groups=[list(range(NCORE))],
                    ins=[g["ag5a_in"].tensor.ap().opt()],
                    outs=[g["ag5a_out"].tensor.ap().opt()]),
                    waits=[("ag5wa", p.cnt["ag5wa"])], inc="cc")
            else:
                nag5b = p.cnt["ag5wb"]


    NSWG = cast_w(p, g, [(g["sw_gate"][k * 128:(k + 1) * 128, :], FF) for k in range(KD)],
                  lambda j, w: g["wbig"][:, j * FF:j * FF + w], "swg",
                  extra_res=("wbigb",), slots=WSLOTS5)
    NSWD = cast_w(p, g, [(g["sw_down"][f * 128:(f + 1) * 128, :], D) for f in range(FCH)],
                  lambda j, w: g["wdown"][:, j * D:j * D + w], "swd",
                  extra_res=("wdownb",), slots=WSLOTS5)
    NSWU = cast_w(p, g, [(g["sw_up"][k * 128:(k + 1) * 128, :], FF) for k in range(KD)],
                  lambda j, w: g["wbig2"][:, j * FF:j * FF + w], "swu",
                  extra_res=("wbig2b",), slots=WSLOTS5)

    for tc in range(4):
        sl = tc % 2
        for k in range(KD):
            src = g["ag1a_in"] if k < 4 else g["ag1b_in"]
            kk = k % 4
            dma("sync", g["xnTc"][:, (sl * KD + k) * TCH:(sl * KD + k + 1) * TCH],
                src[kk * 128:(kk + 1) * 128, tc * TCH:(tc + 1) * TCH],
                res=("xTc%d" % sl,) if k == 0 else (), inc="sxl%d" % sl)
        nxl = p.cnt["sxl%d" % sl]
        waits1 = [("sxl%d" % sl, nxl), ("swgc", NSWG)]
        waits2 = [("sxl%d" % sl, nxl), ("swuc", NSWU)]
        waits3 = [("swdc", NSWD)]

        def out_cb(ss, dh, nt, b3, b3r, tc=tc):
            ur = 3072 + dh * 512
            na = p.op("scalar", lambda e, ur=ur, b3=b3: e.activation(
                out=g["UP"][:, ur:ur + 512], in_=g[b3][:, :], func=AF.Copy),
                waits=[("p3t", nt)], res=("UPm%d" % dh,), inc="p3a")
            p.set_tok(b3r, "p3a", na)
            p.dma("sync", g["shared_dram"][tc * 512 + ss * 128:tc * 512 + (ss + 1) * 128,
                                           dh * 512:(dh + 1) * 512],
                  g["UP"][:, ur:ur + 512], inc="shst%d" % dh, waits=[("p3a", na)])
            p.set_tok("UPm%d" % dh, "shst%d" % dh, p.cnt["shst%d" % dh])

        sn1, sn2, sn3 = mlp_block(
            p, g,
            lambda k, sl=sl: g["xnTc"][:, (sl * KD + k) * TCH:(sl * KD + k + 1) * TCH],
            waits1, waits2, waits3, out_cb)
        p.set_tok("xTc%d" % sl, te, sn2)
    p.set_tok("wbigb", te, sn1)
    p.set_tok("wbig2b", te, sn2)
    p.set_tok("wdownb", te, sn3)
    for j in range(8):
        p.set_tok("m3y%d" % j, te, sn3)
    op("gpsimd", lambda e: e.collective_compute(
        "AllGather", ALU.bypass, replica_groups=[list(range(NCORE))],
        ins=[g["ag5b_in"].tensor.ap().opt()], outs=[g["ag5b_out"].tensor.ap().opt()]),
        waits=[("ag5wb", nag5b), ("shst0", 16 * 16), ("shst1", 16 * 16)], inc="cc")



def emit_p4(nc, p, g):
    op, dma = p.op, p.dma
    ve, ae, te, pl = "p4v", "p4a", "p4t", "p4p"
    for j in range(8):
        p.set_tok("m3y%d" % j, "p3t", p.cnt["p3t"])

    NWO = cast_w(p, g, [(g["Wout"][k * 128:(k + 1) * 128, :], D) for k in range(KD)],
                 lambda j, w: g["wbig2"][:, j * D:j * D + w], "wo",
                 extra_res=("wbig2b",))
    dma("sync", g["UP"][0:1, 0:D], g["bout"][None, :], inc="bol", res=("UPp4",))
    for dh in range(2):
        nt = op("tensor", lambda e, dh=dh: e.matmul(
            g["P_bc1"][:, :], g["onesr"][:, :], g["UP"][0:1, dh * 512:(dh + 1) * 512],
            start=True, stop=True), waits=[("bol", 16)], res=("PB1",), inc=te)
        nv = op("vector", lambda e, dh=dh: e.tensor_copy(
            g["bout_bc"][:, dh * 512:(dh + 1) * 512], g["P_bc1"][:, :]),
            waits=[(te, nt)], inc=ve)
        p.set_tok("PB1", ve, nv)
    p.set_tok("UPp4", ve, p.cnt[ve])
    p.set_tok("m0P0", "p4t", p.cnt["p4t"])
    p.set_tok("m1P0", "p4t", p.cnt["p4t"])
    p.set_tok("s0P0", "shst0", 16 * 16)
    p.set_tok("s1P0", "shst1", 16 * 16)
    p.also_rel("s0P0", "shst1", 16 * 16)
    p.also_rel("s1P0", "shst0", 16 * 16)
    for _t in ("m0P0", "m1P0", "gP0", "s0P0", "s1P0", "UPp4"):
        p.also_rel(_t, "p3a", p.cnt["p3a"])
        p.also_rel(_t, "p3v", p.cnt["p3v"])
        p.also_rel(_t, "p2a", p.cnt["p2a"])
        p.also_rel(_t, "p2v", p.cnt["p2v"])

    with p.chained():
        op("vector", lambda e: e.tensor_copy(g["ygu"][:], g["ygf"][:]),
           waits=[("cld", p.NCLD)], inc=ve)
    nv_yg = p.cnt[ve]

    for j in range(8):
        op("gpsimd", lambda e, j=j: e.indirect_dma_start(
            out=g["m3"][:, j * TPC:(j + 1) * TPC], out_offset=None,
            in_=g["ag4_out"][:, :],
            in_offset=IndirectOffsetOnAxis(ap=g["ygu"][:, j:j + 1], axis=0),
            bounds_check=p.breg(e, NCORE * DSL * 8 - 1), oob_is_err=False),
            waits=[("cc", 5), ("p4v", nv_yg)], res=("m3y%d" % j,), inc="ygl", amt=16)
    zsbv = g["xnTc"][:, 0:TPC]
    for j in range(8):
        dma("sync", zsbv, g["z_dram"][j * 128:(j + 1) * 128, :],
            waits=[("zst0", 16 * 16), ("zst1", 16 * 16)], res=("zsbb", "xTc0"), inc="zld")
        nz = p.cnt["zld"]
        nv = op("vector", lambda e, j=j, zsbv=zsbv: e.tensor_mul(
            g["m3"][:, j * TPC:(j + 1) * TPC], g["m3"][:, j * TPC:(j + 1) * TPC],
            zsbv),
            waits=[("ygl", 16 * 8), ("zld", nz)], inc=ve)
        p.set_tok("zsbb", ve, nv)
        p.set_tok("xTc0", ve, nv)
    NV_YZ = p.cnt[ve]

    dma("sync", g["xt"][:, 0:D], g["xn_dram"][0:128, :],
        waits=[("xnst", 16 * 16)], res=("xt0",), inc="xnl0")
    for i in range(NTIL):
        P = i % 2
        if P == 0:
            moe0 = g["UP"][:, 0:D]
            moe1 = g["UP"][:, 1024:1024 + D]
            ggv = g["UP"][:, 2048:2048 + D]
            sv = [g["UP"][:, 3072:3584], g["UP"][:, 3584:4096]]
            uv = g["UP"][:, 1024:1024 + D]
        else:
            moe0 = g["a_sb"][:, 0:D]
            moe1 = g["xdb_sb"][:, 0:D]
            ggv = g["h_sb"][:, 0:D]
            sv = [g["dtf"][:, 0:512], g["vf"][:, 0:512]]
            uv = g["xdb_sb"][:, 0:D]
        ex = {"m0": ("wstA",), "m1": ("wstX",), "gg": ("wstH",),
              "s0": ("dtf",), "s1": ("vf",)} if i == 1 else {}

        for dh in range(2):
            pa = "P_acc0" if dh == 0 else "P_acc1"
            for k in range(KD):
                r = ("PA%d" % dh,) if k == 0 else ()
                op("tensor", lambda e, k=k, dh=dh, i=i, pa=pa: e.matmul(
                    g[pa][:, :], g["m3"][:, k * TPC + i * 128:k * TPC + (i + 1) * 128],
                    g["wbig2"][:, k * D + dh * 512:k * D + (dh + 1) * 512],
                    start=(k == 0), stop=(k == KD - 1)),
                    waits=[("p4v", NV_YZ), ("woc", NWO)] if k == 0 else (), res=r, inc=te)
            nt = p.cnt[te]
            nv = op("vector", lambda e, dh=dh, pa=pa, sva=sv[dh]: e.tensor_add(
                sva, g[pa][:, :], g["bout_bc"][:, dh * 512:(dh + 1) * 512]),
                waits=[(te, nt)], res=("s%dP%d" % (dh, P),) + ex.get("s%d" % dh, ()),
                inc=ve)
            p.set_tok("PA%d" % dh, ve, nv)
        nv_s = p.cnt[ve]

        nv_m = []
        for kk in range(2):
            sx = 2 * i + kk
            msl = P * 2 + kk
            if P == 0:
                mgv = g["xnTc"][:, TPC + kk * D:TPC + (kk + 1) * D]
                mgv2 = g["xnTc"][:, kk * D:(kk + 1) * D]
                slot_tok = ("xTc0",)
            else:
                mgv = g["xnTc"][:, 4096 + kk * D:4096 + (kk + 1) * D]
                mgv2 = g["xnTc"][:, 4096 + 2048 + kk * D:4096 + 2048 + (kk + 1) * D]
                slot_tok = ("xTc1", "xeTb") if i == 1 else ()
            mo = moe0 if kk == 0 else moe1
            op("gpsimd", lambda e, sx=sx, mgv=mgv: e.indirect_dma_start(
                out=mgv, out_offset=None,
                in_=g["ag5a_out"][:, :],
                in_offset=IndirectOffsetOnAxis(ap=g["ucols"][:, sx:sx + 1], axis=0),
                bounds_check=p.breg(e, NCORE * CAP - 1), oob_is_err=False),
                waits=[("cc", 6)], res=("mg%d" % msl,) + slot_tok, inc="mglA%d" % msl, amt=16)
            op("gpsimd", lambda e, sx=sx, mgv2=mgv2: e.indirect_dma_start(
                out=mgv2, out_offset=None,
                in_=g["ag5b_out"][:, :],
                in_offset=IndirectOffsetOnAxis(ap=g["ucols"][:, sx:sx + 1], axis=0),
                bounds_check=p.breg(e, NCORE * CAP - 1), oob_is_err=False),
                waits=[("cc", 7)], res=("mgB%d" % msl, "zsbb" if P == 0 else "mgB%d" % msl) + slot_tok,
                inc="mglB%d" % msl, amt=16)
            nmgA = p.cnt["mglA%d" % msl]
            nmgB = p.cnt["mglB%d" % msl]
            na = op("scalar", lambda e, sx=sx, mgv=mgv, mo=mo: e.activation(
                out=mo, in_=mgv, func=AF.Copy,
                scale=g["rcols"][:, sx:sx + 1]),
                waits=[("mglA%d" % msl, nmgA)],
                res=("m%dP%d" % (kk, P),) + ex.get("m%d" % kk, ()), inc=ae)
            p.set_tok("mg%d" % msl, ae, na)
            if P == 0:
                p.also_rel("xTc0", "p4a", na)
            nv = op("vector", lambda e, sx=sx, mgv2=mgv2, mo=mo: e.scalar_tensor_tensor(
                mo, mgv2, g["rcols2"][:, sx:sx + 1], mo, ALU.mult, ALU.add),
                waits=[("mglB%d" % msl, nmgB), ("p4a", na)], inc=ve)
            p.set_tok("mgB%d" % msl, ve, nv)
            if P == 0:
                p.also_rel("zsbb", "p4v", nv)
                p.also_rel("xTc0", "p4v", nv)
            else:
                p.also_rel("xTc1", "p4v", nv)
                p.also_rel("xeTb", "p4v", nv)
            nv_m.append(nv)
        nv_moe = p.cnt[ve]

        xsl = i % 2
        xtv = g["xt"][:, xsl * D:(xsl + 1) * D]
        nxn0 = 16 * (i // 2 + 1)
        nv = op("vector", lambda e, moe0=moe0, moe1=moe1: e.tensor_add(
            moe0, moe0, moe1),
            waits=[("p4v", nv_moe)], inc=ve)
        p.set_tok("m1P%d" % P, ve, nv)
        nv = op("vector", lambda e, xtv=xtv, moe0=moe0: e.tensor_add(moe0, moe0, xtv),
                waits=[("xnl%d" % xsl, nxn0), (ve, nv)], inc=ve)
        dma("sync", xtv, g["shared_dram"][i * 128:(i + 1) * 128, :],
            waits=[("shst0", 16 * 16), ("shst1", 16 * 16), (ve, nv)], inc="xshl")
        nxn = p.cnt["xshl"]
        nv = op("vector", lambda e, xtv=xtv, moe0=moe0: e.tensor_add(moe0, moe0, xtv),
                waits=[("xshl", nxn), (ve, nv)], inc=ve)
        na = op("scalar", lambda e, ggv=ggv, moe0=moe0: e.activation(
            out=ggv, in_=moe0, func=AF.Silu),
            waits=[(ve, nv)], res=("gP%d" % P,) + ex.get("gg", ()), inc=ae)
        p.set_tok("m0P%d" % P, ae, na)
        nvm = op("vector", lambda e, uv=uv, ggv=ggv, sva=sv[0]: e.tensor_mul(
            uv[:, 0:512], sva, ggv[:, 0:512]),
            waits=[("p4a", na), ("p4v", nv_s)], res=("m1P%d" % P,), inc=ve)
        nv = op("vector", lambda e, uv=uv, ggv=ggv, svb=sv[1]: e.tensor_mul(
            uv[:, 512:1024], svb, ggv[:, 512:1024]),
            waits=[("p4a", na), ("p4v", nv_s), (ve, nvm)], inc=ve)
        p.set_tok("s0P%d" % P, ve, nv)
        p.set_tok("s1P%d" % P, ve, nv)
        p.set_tok("gP%d" % P, ve, nv)
        dma("sync", xtv, g["x_own"][i * 128:(i + 1) * 128, :], waits=[(ve, nv)],
            inc="xxl")
        nxn = p.cnt["xxl"]
        if i + 1 < NTIL:
            xsl2 = (i + 1) % 2
            dma("sync", g["xt"][:, xsl2 * D:(xsl2 + 1) * D],
                g["xn_dram"][(i + 1) * 128:(i + 2) * 128, :],
                res=("xt%d" % xsl2,), inc="xnl%d" % xsl2)
        nv = op("vector", lambda e, xtv=xtv, uv=uv: e.tensor_add(g["xnf"][:], uv, xtv),
                waits=[("xxl", nxn), (ve, nv)], res=("xnf",), inc=ve)
        p.set_tok("m1P%d" % P, ve, nv)
        p.set_tok("xt%d" % xsl, ve, nv)
        dma("sync", g["o_out"][i * 128:(i + 1) * 128, :], g["xnf"][:],
            inc="ost", waits=[(ve, nv)])
        p.set_tok("xnf", "ost", p.cnt["ost"])

    allw = [("ost", p.cnt["ost"]), ("cc", p.cnt["cc"])]
    for nm in DEBUG_TAPS:
        dma("sync", g["tap_" + nm][tuple(slice(None) for _ in g[nm].shape)],
            g[nm][tuple(slice(None) for _ in g[nm].shape)], inc="tapw", waits=allw)
    if DEBUG_TAPS:
        op("sync", lambda e: e.nop(), waits=[("tapw", p.cnt["tapw"])])
    op("sync", lambda e: e.nop(), waits=[("ost", p.cnt["ost"])])
